# version 20
# baseline (speedup 1.0000x reference)
import numpy as np
import contextlib
import concourse.bass as bass
import concourse.mybir as mybir
from concourse.bass_utils import run_bass_kernel_spmd

F32 = mybir.dt.float32
BF16 = mybir.dt.bfloat16
AF = mybir.ActivationFunctionType
ALU = mybir.AluOpType
AX = mybir.AxisListType

B, T, D, H, HD = 4, 4096, 1024, 16, 64
TOK, NJ = 2048, 16
TK, TKP, NKT = 1366, 1408, 11
DH = 4096
TC, NCH = 512, 4
EXPB = 40.0
DT, DHT = D // 128, DH // 128
HW = 65


def _kvis(j):
    return min(TKP, ((2 * j + 1) * 128 + 127) // 3 + 1)


def _klo(j):
    return (2 * j) * 128 // 3 + 1


def _ktiles(j):
    return (_kvis(j) + 127) // 128


def _masked_kts(j):
    return list(range(_klo(j) // 128, _ktiles(j)))


MAX_MASK = max(len(_masked_kts(j)) for j in range(NJ))


class Bld:
    def __init__(self, nc, block, sems):
        self.nc = nc
        self.block = block
        self.sem = sems
        self.cnt = {k: 0 for k in sems}
        self._record = None
        self._replay = None

    def inc(self, instr, name, k=None):
        if k is None:
            k = 16 if name.startswith("d") else 1
        instr.then_inc(self.sem[name], k)
        self.cnt[name] += k
        return self.cnt[name]

    def wait(self, eng, name, val):
        if self._record is not None:
            self._record.append((name, val))
            return
        if self._replay is not None:
            name, val = self._replay.pop(0)
        if val and val > 0:
            eng.wait_ge(self.sem[name], val)

class _DummyInstr:
    def then_inc(self, sem, k):
        return self

    def annotate(self, *a, **k):
        return self


class _ShadowEngine:
    def __getattr__(self, name):
        return lambda *a, **k: _DummyInstr()


class PhaseBatcher:

    ENGINES = ["sync", "gpsimd", "tensor", "vector", "scalar"]

    def __init__(self, blk, bld):
        self.blk = blk
        self.bld = bld
        self.thunks = []
        self.recs = []
        self._cnt0 = None

    def _add(self, eng, fn):
        b = self.bld
        if not self.thunks:
            self._cnt0 = dict(b.cnt)
        stub = _ShadowEngine()
        b._record = []
        fn(stub)
        self.recs.append(b._record)
        b._record = None
        self.thunks.append((eng, fn))

    def sync(self, fn):
        self._add("sync", fn)

    def gpsimd(self, fn):
        self._add("gpsimd", fn)

    def tensor(self, fn):
        self._add("tensor", fn)

    def vector(self, fn):
        self._add("vector", fn)

    def scalar(self, fn):
        self._add("scalar", fn)

    def flush(self):
        if not self.thunks:
            return
        b = self.bld
        cnt_end = dict(b.cnt)
        recs = self.recs
        b.cnt.clear()
        b.cnt.update(self._cnt0)
        for eng in self.ENGINES:
            mine = [(i, fn) for i, (e, fn) in enumerate(self.thunks) if e == eng]
            if not mine:
                continue

            def _run(engine, mine=mine):
                for i, fn in mine:
                    b._replay = list(recs[i])
                    fn(engine)
                b._replay = None
            getattr(self.blk, eng)(_run)
        assert b.cnt == cnt_end, (b.cnt, cnt_end)
        self.thunks = []
        self.recs = []
        self._cnt0 = None



def build_core_program(debug=False):
    nc = bass.Bass("TRN2", target_bir_lowering=False)

    x_full = nc.dram_tensor("x_full", [T, D], F32, kind="ExternalInput").ap()
    x_own = nc.dram_tensor("x_own", [TOK, D], F32, kind="ExternalInput").ap()
    maskT = nc.dram_tensor("maskT", [TKP, TOK], F32, kind="ExternalInput").ap()
    wq = nc.dram_tensor("wq", [D, D], F32, kind="ExternalInput").ap()
    wk = nc.dram_tensor("wk", [D, D], F32, kind="ExternalInput").ap()
    wv = nc.dram_tensor("wv", [D, D], F32, kind="ExternalInput").ap()
    wo = nc.dram_tensor("wo", [D, D], F32, kind="ExternalInput").ap()
    bo = nc.dram_tensor("bo", [D], F32, kind="ExternalInput").ap()
    wconv = nc.dram_tensor("wconv", [D, D, 3], F32, kind="ExternalInput").ap()
    w1 = nc.dram_tensor("w1", [DH, D], F32, kind="ExternalInput").ap()
    w2 = nc.dram_tensor("w2", [DH, D], F32, kind="ExternalInput").ap()
    h1 = nc.dram_tensor("h1", [DH, DH], F32, kind="ExternalInput").ap()
    h2 = nc.dram_tensor("h2", [DH, DH], F32, kind="ExternalInput").ap()
    w3 = nc.dram_tensor("w3", [D, DH], F32, kind="ExternalInput").ap()
    s1 = nc.dram_tensor("s1", [DH], F32, kind="ExternalInput").ap()
    s2 = nc.dram_tensor("s2", [DH], F32, kind="ExternalInput").ap()
    sh1 = nc.dram_tensor("sh1", [DH], F32, kind="ExternalInput").ap()
    sh2 = nc.dram_tensor("sh2", [DH], F32, kind="ExternalInput").ap()
    s3 = nc.dram_tensor("s3", [D], F32, kind="ExternalInput").ap()
    y = nc.dram_tensor("y", [TOK, D], F32, kind="ExternalOutput").ap()

    n_wq = nc.dram_tensor("n_wq", [D, D], BF16, kind="Internal").ap()
    n_wk = nc.dram_tensor("n_wk", [D, D], BF16, kind="Internal").ap()
    n_wv = nc.dram_tensor("n_wv", [D, D], BF16, kind="Internal").ap()
    n_wo = nc.dram_tensor("n_wo", [D, D], BF16, kind="Internal").ap()
    n_cv = [nc.dram_tensor(f"n_cv{w}", [D, D], BF16, kind="Internal").ap() for w in range(3)]
    n_w1 = nc.dram_tensor("n_w1", [DH, D], BF16, kind="Internal").ap()
    n_w2 = nc.dram_tensor("n_w2", [DH, D], BF16, kind="Internal").ap()
    n_h1 = nc.dram_tensor("n_h1", [DH, DH], BF16, kind="Internal").ap()
    n_h2 = nc.dram_tensor("n_h2", [DH, DH], BF16, kind="Internal").ap()
    n_w3 = nc.dram_tensor("n_w3", [D, DH], BF16, kind="Internal").ap()

    t_w1 = nc.dram_tensor("t_w1", [D, DH], BF16, kind="Internal").ap()
    t_w2 = nc.dram_tensor("t_w2", [D, DH], BF16, kind="Internal").ap()
    t_h1 = nc.dram_tensor("t_h1", [DH, DH], BF16, kind="Internal").ap()
    t_h2 = nc.dram_tensor("t_h2", [DH, DH], BF16, kind="Internal").ap()
    t_w3 = nc.dram_tensor("t_w3", [DH, D], BF16, kind="Internal").ap()

    scr_xf = nc.dram_tensor("scr_xf", [T, D], BF16, kind="Internal").ap()
    scr_xo = nc.dram_tensor("scr_xo", [TOK, D], BF16, kind="Internal").ap()
    scr_o = nc.dram_tensor("scr_o", [TOK, D], BF16, kind="Internal").ap()
    scr_at = nc.dram_tensor("scr_at", [D, TOK], BF16, kind="Internal").ap()
    scr_x2 = nc.dram_tensor("scr_x2", [TOK, D], BF16, kind="Internal").ap()
    scr_ff = nc.dram_tensor("scr_ff", [D, TOK], BF16, kind="Internal").ap()
    scr_xp = nc.dram_tensor("scr_xp", [TOK, D], F32, kind="Internal").ap()

    dbg = {}
    if debug:
        for nm, shp in [("xlnf", [T, D]), ("ktmp", [D, TKP]), ("kfm", [D, TKP]),
                        ("vtm", [TKP, 16 * HW]), ("qfm", [D, TOK]), ("otm", [TOK, D]),
                        ("xp", [TOK, D]), ("hsb", [DH, TC]), ("csb", [DH, TC]),
                        ("attn", [TOK, D])]:
            dbg[nm] = nc.dram_tensor("dbg_" + nm, shp, F32, kind="ExternalOutput").ap()

    stack = contextlib.ExitStack()
    with stack:
        sems = {}
        for s in ["do", "dCV", "dL0", "dL1", "dXW", "dWC", "dWK", "dXO",
                  "dMK0", "dMK1", "dSO", "dOF0", "dOF1", "dAT0", "dAT1", "dTM0", "dTM1", "dTM2", "dTM3", "dL2", "dL3", "dLS2", "dLS3",
                  "dX2S0", "dX2S1", "dX2S2", "dX2S3", "dX2", "dFT2", "dFT3", "dLS0", "dLS1", "dY0", "dY1", "dPT0", "dPT1", "dPW0", "dPW1", "dW0", "dW1", "dFS", "dFT0", "dFT1", "dY", "dDG",
                  "dgA", "dgF", "dgH1", "dgH2", "dXP0", "dXP1", "dXP2", "dXP3",
                  "dXL0", "dXL1", "dWO", "cT", "cV", "cA"]:
            sems[s] = stack.enter_context(nc.semaphore(s))
        blk = stack.enter_context(nc.Block())
        b = Bld(nc, blk, sems)
        bb = PhaseBatcher(blk, b)
        inc, wait, cnt = b.inc, b.wait, b.cnt
        ev = {}

        consts = stack.enter_context(nc.sbuf_tensor("consts", [128, 4], F32))
        NSC = 4 * DHT + 64
        scales = stack.enter_context(nc.sbuf_tensor("scales", [128, NSC], F32))
        S1C, S2C, SH1C, SH2C, S3C, BOC = 0, DHT, 2 * DHT, 3 * DHT, 4 * DHT, 4 * DHT + 32

        bb.vector(lambda v: (
            v.memset(consts[:, 0:1], 1e-5),
            v.memset(consts[:, 1:2], -EXPB),
            ev.__setitem__("consts", inc(v.memset(consts[:, 2:3], 0.0), "cV")),
        ))

        ln_st = stack.enter_context(nc.sbuf_tensor("ln_st", [128, 48], F32))
        ln_waste = stack.enter_context(nc.sbuf_tensor("ln_waste", [128, D], BF16))

        def _prep(g):
            def d2d(dst, src, rows, sem, chunk=512):
                for r0 in range(0, rows, chunk):
                    inc(g.dma_start(dst[r0:r0 + chunk, :], src[r0:r0 + chunk, :]), sem)
            d2d(n_wq, wq, D, "dgA")
            d2d(n_wk, wk, D, "dgA")
            d2d(n_wv, wv, D, "dgA")
            d2d(n_wo, wo, D, "dgA")
            d2d(n_w1, w1, DH, "dgF")
            d2d(n_w2, w2, DH, "dgF")
        bb.gpsimd(_prep)
        ev["dgA"], ev["dgF"], ev["dgH1"], ev["dgH2"] = cnt["dgA"], cnt["dgF"], cnt["dgH1"], cnt["dgH2"]

        with nc.sbuf_tensor("cv_nat", [128, 3 * D], F32) as cv_nat, \
             nc.sbuf_tensor("cv_bf", [128, 3 * D], BF16) as cv_bf:
            for r in range(DT):
                def _l(sy, r=r):
                    wait(sy, "dCV", ev.get("cv_out", 0))
                    ev["cv_in"] = inc(sy.dma_start(
                        cv_nat[:, :],
                        wconv[r * 128:(r + 1) * 128, :, :].rearrange("a b c -> a (b c)")), "dCV")
                bb.sync(_l)

                def _c(v, r=r):
                    wait(v, "dCV", ev["cv_in"])
                    for w in range(3):
                        v.tensor_copy(cv_bf[:, w * D:(w + 1) * D], cv_nat[:, w:3 * D:3])
                    ev["cv_bf"] = inc(v.engine_nop(), "cV")
                bb.vector(_c)

                def _s(sy, r=r):
                    wait(sy, "cV", ev["cv_bf"])
                    for w in range(3):
                        inc(sy.dma_start(n_cv[w][r * 128:(r + 1) * 128, :],
                                         cv_bf[:, w * D:(w + 1) * D]), "dCV")
                    ev["cv_out"] = cnt["dCV"]
                bb.sync(_s)
            ev["conv_w"] = cnt["dCV"]

        with nc.sbuf_tensor("svec_nat", [32, 128], F32) as svec_nat:
            def load_svec(src, n_tiles, dst_col):
                for t0 in range(0, n_tiles, 32):
                    tn = min(32, n_tiles - t0)

                    def _z(v, tn=tn):
                        wait(v, "do", ev.get("svec_in", 0))
                        wait(v, "cV", ev.get("svec_used", 0))
                        if tn < 32:
                            ev["svec_z"] = inc(v.memset(svec_nat[:, :], 0.0), "cV")
                        else:
                            ev["svec_z"] = inc(v.engine_nop(), "cV")
                    bb.vector(_z)

                    def _l(sy, t0=t0, tn=tn):
                        wait(sy, "cV", ev["svec_z"])
                        ev["svec_in"] = inc(sy.dma_start(
                            svec_nat[0:tn, :],
                            src[t0 * 128:(t0 + tn) * 128].rearrange("(a b) -> a b", b=128)), "do")
                    bb.sync(_l)

                    def _t(v, t0=t0, tn=tn, dst_col=dst_col):
                        wait(v, "do", ev["svec_in"])
                        for bi in range((tn + 31) // 32):
                            for pj in range(4):
                                v.transpose(
                                    scales[pj * 32:(pj + 1) * 32,
                                           dst_col + t0 + bi * 32:dst_col + t0 + bi * 32 + 32],
                                    svec_nat[bi * 32:bi * 32 + 32, pj * 32:(pj + 1) * 32])
                        ev["svec_used"] = inc(v.engine_nop(), "cV")
                    bb.vector(_t)
            load_svec(s1, DHT, S1C)
            load_svec(s2, DHT, S2C)
            load_svec(sh1, DHT, SH1C)
            load_svec(sh2, DHT, SH2C)
            load_svec(s3, DT, S3C)
            load_svec(bo, DT, BOC)
            ev["scales"] = cnt["cV"]

        def ln_stats(src_ap, slot, dep_list):
            c0 = 4 * slot

            def _v(v):
                for s, val in dep_list:
                    wait(v, s, val)
                v.reduce_sum(ln_st[:, c0:c0 + 1], src_ap, AX.X)
                inc(v.engine_nop(), "cV")
                wait(v, "cV", cnt["cV"])
                inc(v.tensor_scalar_mul(ln_st[:, c0:c0 + 1], ln_st[:, c0:c0 + 1], -1.0 / D), "cV")
                wait(v, "cV", cnt["cV"])
                ev[f"ln_sum{slot}"] = inc(
                    v.scalar_tensor_tensor(ln_waste[:, :], src_ap,
                                           ln_st[:, c0:c0 + 1], src_ap,
                                           op0=ALU.add, op1=ALU.mult,
                                           accum_out=ln_st[:, c0 + 1:c0 + 2]), "cV")
            bb.vector(_v)

            def _a(sc):
                wait(sc, "cV", ev[f"ln_sum{slot}"])
                wait(sc, "cV", ev["consts"])
                inc(sc.activation(ln_st[:, c0 + 2:c0 + 3], ln_st[:, c0 + 1:c0 + 2], AF.Ln,
                                  bias=consts[:, 0:1], scale=1.0 / D), "cA")
                wait(sc, "cA", cnt["cA"])
                ev[f"ln_rstd{slot}"] = inc(
                    sc.activation(ln_st[:, c0 + 2:c0 + 3], ln_st[:, c0 + 2:c0 + 3],
                                  AF.Exp, bias=0.0, scale=-0.5), "cA")
            bb.scalar(_a)
            return ("cA", ev[f"ln_rstd{slot}"])


        with nc.sbuf_tensor("lx01", [128, 2 * D], F32, side="right") as lx01, \
             nc.sbuf_tensor("lx23", [128, 2 * D], F32, side="right") as lx23, \
             nc.sbuf_tensor("lb01", [128, 2 * D], BF16, side="right") as lb01, \
             nc.sbuf_tensor("lb23", [128, 2 * D], BF16, side="right") as lb23:
            lxs = [lx01[:, 0:D], lx01[:, D:2 * D], lx23[:, 0:D], lx23[:, D:2 * D]]
            lbs = [lb01[:, 0:D], lb01[:, D:2 * D], lb23[:, 0:D], lb23[:, D:2 * D]]

            def ln1_front(src_slice, slot):
                lx = lxs[slot]

                def _l(sy):
                    wait(sy, "cA", ev.get(f"ln_used{slot}", 0))
                    wait(sy, "cV", ev.get(f"lnbf{slot}", 0))
                    ev[f"lnx{slot}"] = inc(sy.dma_start(lx, src_slice), f"dL{slot}")
                bb.sync(_l)
                ln_stats(lx, slot, [(f"dL{slot}", ev[f"lnx{slot}"])])

            def ln1_finish(dst_slice, slot):
                lx, lb = lxs[slot], lbs[slot]
                c0 = 4 * slot

                def _f(v):
                    wait(v, "cA", ev[f"ln_rstd{slot}"])
                    wait(v, f"dLS{slot}", ev.get(f"lnout{slot}", 0))
                    ev[f"lnbf{slot}"] = inc(
                        v.tensor_scalar(lb, lx,
                                        ln_st[:, c0:c0 + 1], ln_st[:, c0 + 2:c0 + 3],
                                        op0=ALU.add, op1=ALU.mult), "cV")
                    ev[f"ln_used{slot}"] = ev[f"ln_rstd{slot}"]
                bb.vector(_f)

                def _o(sy):
                    wait(sy, "cV", ev[f"lnbf{slot}"])
                    ev[f"lnout{slot}"] = inc(sy.dma_start(dst_slice, lb), f"dLS{slot}")
                bb.sync(_o)

            specs = [(x_full[t * 128:(t + 1) * 128, :], scr_xf[t * 128:(t + 1) * 128, :])
                     for t in range(T // 128)]
            specs += [(x_own[t * 128:(t + 1) * 128, :], scr_xo[t * 128:(t + 1) * 128, :])
                      for t in range(TOK // 128)]
            rstd_ev = {}
            for t, (srcs, dsts) in enumerate(specs):
                ln1_front(srcs, t % 4)
                rstd_ev[t] = ev[f"ln_rstd{t % 4}"]
                if t >= 2:
                    ev[f"ln_rstd{(t - 2) % 4}"] = rstd_ev[t - 2]
                    ln1_finish(specs[t - 2][1], (t - 2) % 4)
            for t in (len(specs) - 2, len(specs) - 1):
                ev[f"ln_rstd{t % 4}"] = rstd_ev[t]
                ln1_finish(specs[t][1], t % 4)
            for s_ in range(4):
                ev[f"ln1_{s_}"] = cnt[f"dLS{s_}"]
            ev["ln1_v"] = cnt["cV"]
            bb.flush()

        otm = None
        otm_t = lambda j: otm[:, j * D:(j + 1) * D]

        otm_es = contextlib.ExitStack()
        otm = otm_es.enter_context(nc.sbuf_tensor("otm", [128, NJ * D], BF16, side="left"))
        attn_es = contextlib.ExitStack()
        ktmp_es = contextlib.ExitStack()
        if True:
            ktmp = ktmp_es.enter_context(nc.sbuf_tensor("ktmp", [128, DT * TKP], BF16, side="right"))
            ktmp_t = lambda kd: ktmp[:, kd * TKP:(kd + 1) * TKP]

            with nc.sbuf_tensor("xwin", [128, DT * 1536], BF16, side="right") as xwin, \
                 nc.sbuf_tensor("wt_cv", [128, DT * D], BF16, side="right") as wt_cv, \
                 contextlib.ExitStack() as cps:
                ps_cv = [cps.enter_context(nc.psum_tensor(f"ps_cv{i}", [128, 512], F32))
                         for i in range(8)]
                xwin_t = lambda kd: xwin[:, kd * 1536:(kd + 1) * 1536]
                wt_cv_t = lambda kd: wt_cv[:, kd * D:(kd + 1) * D]

                def _ktmp_init(v):
                    wait(v, "cV", ev["ln1_v"])
                    for s_ in range(4):
                        wait(v, f"dLS{s_}", ev[f"ln1_{s_}"])
                    for kd in range(DT):
                        v.memset(ktmp_t(kd)[:, TK:TKP], 0.0)
                    ev["ktmp_init"] = inc(v.engine_nop(), "cV")
                bb.vector(_ktmp_init)

                conv_chunks = [(0, 512), (512, 512), (1024, 341)]
                for ci, (p0, pn) in enumerate(conv_chunks):
                    rows0 = 3 * p0
                    rows1 = min(T, rows0 + 1536)

                    def _xl(sy, ci=ci, rows0=rows0, rows1=rows1):
                        for s_ in range(4):
                            wait(sy, f"dLS{s_}", ev[f"ln1_{s_}"])
                        wait(sy, "cT", ev.get("conv_mm_c", 0))
                        wait(sy, "cV", ev.get("ktmp_c0", 0))
                        for kd in range(DT):
                            inc(sy.dma_start(xwin_t(kd)[:, 0:rows1 - rows0],
                                             scr_xf[rows0:rows1, kd * 128:(kd + 1) * 128],
                                             transpose=True), "dXW")
                        ev["xwin"] = cnt["dXW"]
                    bb.sync(_xl)
                    if ci == 0:
                        def _c0(v):
                            wait(v, "dXW", ev["xwin"])
                            for kd in range(DT):
                                v.tensor_copy(ktmp_t(kd)[:, 0:1], xwin_t(kd)[:, 0:1])
                            ev["ktmp_c0"] = inc(v.engine_nop(), "cV")
                        bb.vector(_c0)

                    for w in range(3):
                        def _wl(sy, w=w, ci=ci):
                            wait(sy, "dCV", ev["conv_w"])
                            wait(sy, "cT", ev.get("conv_mm_w", 0))
                            for kd in range(DT):
                                inc(sy.dma_start(wt_cv_t(kd), n_cv[w][:, kd * 128:(kd + 1) * 128],
                                                 transpose=True), "dWC")
                            ev["cv_wl"] = cnt["dWC"]
                        bb.sync(_wl)

                        def _mm(t, w=w, ci=ci, p0=p0, pn=pn):
                            wait(t, "dWC", ev["cv_wl"])
                            wait(t, "dXW", ev["xwin"])
                            if w == 0:
                                wait(t, "cV", ev.get("conv_ev_c", 0))
                            for ot in range(DT):
                                for kd in range(DT):
                                    ins = t.matmul(ps_cv[ot][:, 0:pn],
                                                   wt_cv_t(kd)[:, ot * 128:(ot + 1) * 128],
                                                   xwin_t(kd)[:, w:3 * (pn - 1) + w + 1:3],
                                                   start=(w == 0 and kd == 0),
                                                   stop=(w == 2 and kd == DT - 1))
                            ev["conv_mm_w"] = inc(ins, "cT")
                        bb.tensor(_mm)
                    ev["conv_mm_c"] = cnt["cT"]

                    def _ev(v, ci=ci, p0=p0, pn=pn):
                        wait(v, "cT", ev["conv_mm_c"])
                        wait(v, "cV", ev["ktmp_init"])
                        for ot in range(DT):
                            ins = v.tensor_copy(ktmp_t(ot)[:, 1 + p0:1 + p0 + pn],
                                                ps_cv[ot][:, 0:pn])
                        ev["conv_ev_c"] = inc(ins, "cV")
                    bb.vector(_ev)
                ev["ktmp"] = cnt["cV"]
                ev["conv_mm_all"] = cnt["cT"]

            if debug:
                bb.flush()
                _dump_fm(b, nc, blk, ev, ktmp_t, DT, TKP, dbg["ktmp"], dep=("cV", ev["ktmp"]))

            kfm = attn_es.enter_context(nc.sbuf_tensor("kfm", [128, DT * TKP], BF16, side="left"))
            vtm = attn_es.enter_context(nc.sbuf_tensor("vtm", [128, NKT * 16 * HW], BF16, side="left"))
            kfm_t = lambda kd: kfm[:, kd * TKP:(kd + 1) * TKP]
            vtm_t = lambda kt: vtm[:, kt * 16 * HW:(kt + 1) * 16 * HW]

            with nc.sbuf_tensor("wt_kv", [128, DT * D], BF16, side="right") as wt_kv, \
                 nc.psum_tensor("ps_q0", [128, 512], F32) as ps_q0, \
                 nc.psum_tensor("ps_q1", [128, 512], F32) as ps_q1:
                ps_proj = [ps_q0, ps_q1]
                wt_kv_t = lambda kd: wt_kv[:, kd * D:(kd + 1) * D]

                def _wk(sy):
                    wait(sy, "dgA", ev["dgA"])
                    wait(sy, "cT", ev["conv_mm_all"])
                    for kd in range(DT):
                        inc(sy.dma_start(wt_kv_t(kd), n_wk[:, kd * 128:(kd + 1) * 128],
                                         transpose=True), "dWK")
                    ev["wkT"] = cnt["dWK"]
                bb.sync(_wk)

                pp = 0
                kproj_chunks = [(0, 512), (512, 512), (1024, 384)]
                for ot in range(DT):
                    for (p0, pn) in kproj_chunks:
                        ps = ps_proj[pp % 2]

                        def _mm(t, ot=ot, p0=p0, pn=pn, ps=ps, pp=pp):
                            wait(t, "dWK", ev["wkT"])
                            wait(t, "cV", ev["ktmp"])
                            wait(t, "cV", ev.get(f"kp_ev{pp % 2}", 0))
                            for kd in range(DT):
                                ins = t.matmul(ps[:, 0:pn],
                                               wt_kv_t(kd)[:, ot * 128:(ot + 1) * 128],
                                               ktmp_t(kd)[:, p0:p0 + pn],
                                               start=(kd == 0), stop=(kd == DT - 1))
                            ev["kp_mm"] = inc(ins, "cT")
                        bb.tensor(_mm)

                        def _ev(v, ot=ot, p0=p0, pn=pn, ps=ps, pp=pp):
                            wait(v, "cT", ev["kp_mm"])
                            ev[f"kp_ev{pp % 2}"] = inc(
                                v.tensor_copy(kfm_t(ot)[:, p0:p0 + pn], ps[:, 0:pn]), "cV")
                        bb.vector(_ev)
                        pp += 1
                ev["kfm"] = cnt["cV"]
                ev["kp_mm_all"] = cnt["cT"]

                def _wv(sy):
                    wait(sy, "cT", ev["kp_mm_all"])
                    for kd in range(DT):
                        inc(sy.dma_start(wt_kv_t(kd), n_wv[:, kd * 128:(kd + 1) * 128],
                                         transpose=True), "dWK")
                    ev["wvT"] = cnt["dWK"]
                bb.sync(_wv)

                def _vones(v):
                    for kt in range(NKT):
                        v.memset(vtm_t(kt)[:, 64:16 * HW:HW], 1.0)
                    ev["vones"] = inc(v.engine_nop(), "cV")
                bb.vector(_vones)

                for kt in range(NKT):
                    for half in range(2):
                        ps = ps_proj[pp % 2]

                        def _mm(t, kt=kt, half=half, ps=ps, pp=pp):
                            wait(t, "dWK", ev["wvT"])
                            wait(t, "cV", ev.get(f"vp_ev{pp % 2}", ev["kfm"]))
                            for kd in range(DT):
                                ins = t.matmul(ps[:, 0:512],
                                               ktmp_t(kd)[:, kt * 128:(kt + 1) * 128],
                                               wt_kv_t(kd)[:, half * 512:(half + 1) * 512],
                                               start=(kd == 0), stop=(kd == DT - 1))
                            ev["vp_mm"] = inc(ins, "cT")
                        bb.tensor(_mm)

                        def _ev(v, kt=kt, half=half, ps=ps, pp=pp):
                            wait(v, "cT", ev["vp_mm"])
                            wait(v, "cV", ev["vones"])
                            dst = vtm_t(kt)[:, half * 8 * HW:(half + 1) * 8 * HW]
                            ev[f"vp_ev{pp % 2}"] = inc(v.tensor_copy(
                                dst.rearrange("p (h c) -> p h c", c=HW)[:, :, 0:64],
                                ps[:, 0:512].rearrange("p (h c) -> p h c", c=64)), "cV")
                        bb.vector(_ev)
                        pp += 1
                ev["vtm"] = cnt["cV"]
                ev["vp_mm_all"] = cnt["cT"]

            ktmp_es.close()
            qfm = attn_es.enter_context(nc.sbuf_tensor("qfm", [128, DT * TOK], BF16, side="left"))
            qfm_t = lambda kd: qfm[:, kd * TOK:(kd + 1) * TOK]

            with nc.sbuf_tensor("xowin", [128, DT * 512], BF16, side="right") as xowin, \
                 nc.sbuf_tensor("wt_q", [128, DT * D], BF16, side="right") as wt_q, \
                 nc.psum_tensor("ps_qa", [128, 512], F32) as ps_qa, \
                 nc.psum_tensor("ps_qb", [128, 512], F32) as ps_qb:
                ps_proj = [ps_qa, ps_qb]
                xowin_t = lambda kd: xowin[:, kd * 512:(kd + 1) * 512]
                wt_q_t = lambda kd: wt_q[:, kd * D:(kd + 1) * D]

                def _wq(sy):
                    wait(sy, "dgA", ev["dgA"])
                    wait(sy, "cT", ev["vp_mm_all"])
                    for kd in range(DT):
                        inc(sy.dma_start(wt_q_t(kd), n_wq[:, kd * 128:(kd + 1) * 128],
                                         transpose=True), "dWK")
                    ev["wqT"] = cnt["dWK"]
                bb.sync(_wq)

                pp = 0
                for nch in range(TOK // 512):
                    def _xl(sy, nch=nch):
                        for s_ in range(4):
                            wait(sy, f"dLS{s_}", ev[f"ln1_{s_}"])
                        wait(sy, "cT", ev.get("qp_mm_n", 0))
                        for kd in range(DT):
                            inc(sy.dma_start(xowin_t(kd),
                                             scr_xo[nch * 512:(nch + 1) * 512,
                                                    kd * 128:(kd + 1) * 128],
                                             transpose=True), "dXO")
                        ev["xowin"] = cnt["dXO"]
                    bb.sync(_xl)

                    for ot in range(DT):
                        ps = ps_proj[pp % 2]

                        def _mm(t, ot=ot, nch=nch, ps=ps, pp=pp):
                            wait(t, "dWK", ev["wqT"])
                            wait(t, "dXO", ev["xowin"])
                            if pp < 2:
                                wait(t, "cV", ev["vtm"])
                            wait(t, "cA", ev.get(f"qp_ev{pp % 2}", 0))
                            for kd in range(DT):
                                ins = t.matmul(ps[:, 0:512],
                                               wt_q_t(kd)[:, ot * 128:(ot + 1) * 128],
                                               xowin_t(kd),
                                               start=(kd == 0), stop=(kd == DT - 1))
                            ev["qp_mm"] = inc(ins, "cT")
                        bb.tensor(_mm)

                        def _ev(sc, ot=ot, nch=nch, ps=ps, pp=pp):
                            wait(sc, "cT", ev["qp_mm"])
                            ev[f"qp_ev{pp % 2}"] = inc(
                                sc.activation(qfm_t(ot)[:, nch * 512:(nch + 1) * 512],
                                              ps[:, 0:512], AF.Copy, bias=0.0, scale=0.125), "cA")
                        bb.scalar(_ev)
                        pp += 1
                    ev["qp_mm_n"] = cnt["cT"]
                ev["qfm"] = cnt["cA"]

            bb.flush()
            if debug:
                _dump_fm(b, nc, blk, ev, kfm_t, DT, TKP, dbg["kfm"], dep=("cV", ev["kfm"]))
                _dump_fm(b, nc, blk, ev, vtm_t, NKT, 16 * HW, dbg["vtm"], dep=("cV", ev["vtm"]))
                _dump_fm(b, nc, blk, ev, qfm_t, DT, TOK, dbg["qfm"], dep=("cA", ev["qfm"]))

        p2es = contextlib.ExitStack()
        p2b0 = p2es.enter_context(nc.sbuf_tensor("p2b0", [128, DH], BF16, side="right"))
        p2b = [p2b0, p2b0]
        cast_q = {"h1": [(n_h1, h1, r0, "dgH1") for r0 in range(0, DH, 256)],
                  "h2": [(n_h2, h2, r0, "dgH2") for r0 in range(0, DH, 256)],
                  "w3": [(n_w3, w3, r0, "dgH2") for r0 in range(0, D, 256)]}
        cast_order = ["h1", "h2", "w3"]

        def emit_casts(n):
            def _g(g, items):
                for dst_, src_, r0_, sem_ in items:
                    inc(g.dma_start(dst_[r0_:r0_ + 256, :], src_[r0_:r0_ + 256, :]), sem_)
            items = []
            while n > 0 and cast_order:
                qn = cast_order[0]
                if not cast_q[qn]:
                    cast_order.pop(0)
                    continue
                items.append(cast_q[qn].pop(0))
                n -= 1
            if items:
                bb.gpsimd(lambda g, items=tuple(items): _g(g, items))
            ev["dgF"], ev["dgH1"], ev["dgH2"] = cnt["dgF"], cnt["dgH1"], cnt["dgH2"]

        def drain_casts_for(w):
            names = {"t_h1": "h1", "t_h2": "h2", "t_w3": "w3"}
            qn = names.get(w)
            while qn and cast_q[qn]:
                emit_casts(8)
        p2_steps = []
        for nat_, tdst_, I_, O_, gate_, mark_ in [
                (n_w1, t_w1, D, DH, "dgF", None),
                (n_w2, t_w2, D, DH, "dgF", "p2_ab"),
                (n_h1, t_h1, DH, DH, "dgH1", "p2_h1"),
                (n_h2, t_h2, DH, DH, "dgH2", "p2_h2"),
                (n_w3, t_w3, DH, D, "dgH2", "p2_w3")]:
            nkd = I_ // 128
            for kd_ in range(nkd):
                p2_steps.append((nat_, tdst_, O_, gate_, kd_,
                                 mark_ if kd_ == nkd - 1 else None))
        p2_i = [0]

        def emit_p2(nsteps):
            while nsteps > 0 and p2_i[0] < len(p2_steps):
                nat_, tdst_, O_, gate_, kd_, mark_ = p2_steps[p2_i[0]]
                if kd_ == 0:
                    drain_casts_for(tdst_.tensor.name)
                par = p2_i[0] % 2

                def _t(sy, nat_=nat_, O_=O_, gate_=gate_, kd_=kd_, par=par, i=p2_i[0]):
                    if i < 2:
                        wait(sy, "cA", ev["qfm"])
                    wait(sy, gate_, ev[gate_])
                    wait(sy, "dPW0", ev.get("p2w0", 0))
                    wait(sy, "dPW1", ev.get("p2w1", 0))
                    ev[f"p2t{par}"] = inc(sy.dma_start(
                        p2b[par][:, 0:O_], nat_[0:O_, kd_ * 128:(kd_ + 1) * 128],
                        transpose=True), f"dPT{par}")
                bb.sync(_t)

                def _w(g, tdst_=tdst_, O_=O_, kd_=kd_, par=par):
                    wait(g, f"dPT{par}", ev[f"p2t{par}"])
                    ev[f"p2w{par}"] = inc(g.dma_start(
                        tdst_[kd_ * 128:(kd_ + 1) * 128, 0:O_], p2b[par][:, 0:O_]), f"dPW{par}")
                bb.gpsimd(_w)
                if mark_:
                    ev[mark_] = (cnt["dPW0"], cnt["dPW1"])
                p2_i[0] += 1
                nsteps -= 1

        def _mk_pair(jp):
            return list(range(_klo(2 * jp) // 128, _ktiles(2 * jp + 1)))

        MKP = max(len(_mk_pair(jp)) for jp in range(NJ // 2))
        NMK = sum(len(_mk_pair(jp)) for jp in range(NJ // 2))
        a2 = contextlib.ExitStack()
        with a2:
            pt_sb = a2.enter_context(nc.sbuf_tensor("pt_sb", [128, 2 * 6 * 256], BF16, side="right"))
            mk_sb = a2.enter_context(nc.sbuf_tensor("mk_sb", [128, NMK * 256], BF16, side="right"))
            ps_scA = a2.enter_context(nc.psum_tensor("ps_scA", [128, 3 * 512], F32))
            ps_scB = a2.enter_context(nc.psum_tensor("ps_scB", [128, 3 * 512], F32))
            ps_oA = a2.enter_context(nc.psum_tensor("ps_oA", [128, 512], F32))
            ps_oB = a2.enter_context(nc.psum_tensor("ps_oB", [128, 512], F32))
            ps_sc = [ps_scA, ps_scB]
            ps_o = [ps_oA, ps_oB]
            mk_base = {}
            _mb = 0
            for jp_ in range(NJ // 2):
                mk_base[jp_] = _mb
                _mb += len(_mk_pair(jp_))
            mk_slot = lambda jp, i: mk_sb[:, (mk_base[jp] + i) * 256:
                                          (mk_base[jp] + i + 1) * 256]

            def _mload_all(g):
                wait(g, "cA", ev["qfm"])
                for jp_ in range(NJ // 2):
                    for i_, kt_ in enumerate(_mk_pair(jp_)):
                        inc(g.dma_start(mk_slot(jp_, i_),
                                        maskT[kt_ * 128:(kt_ + 1) * 128,
                                              2 * jp_ * 128:(2 * jp_ + 2) * 128]), "dMK0")
            bb.gpsimd(_mload_all)
            ev["mload_all"] = cnt["dMK0"]

            gnum = 0
            itc = 0
            for jp in range(NJ // 2):
                mk = _mk_pair(jp)
                j0 = 2 * jp

                emit_casts(7)
                emit_p2(16)

                nkt = _ktiles(2 * jp + 1)
                for h in range(H):
                    hq, hp = h // 2, (h % 2) * 64
                    groups = [list(range(nkt))[i:i + 6] for i in range(0, nkt, 6)]
                    po = ps_o[itc % 2]
                    for gi, g in enumerate(groups):
                        psc = ps_sc[gnum % 2]
                        slot = gnum % 2

                        def _sc_mm(t, jp=jp, h=h, g=tuple(g), psc=psc, hq=hq, hp=hp,
                                   gnum=gnum, j0=j0):
                            if gnum < 2:
                                wait(t, "cA", ev["qfm"])
                                wait(t, "cV", ev["kfm"])
                                wait(t, "cV", ev["vtm"])
                            wait(t, "cA", ev.get(f"exp_g{gnum - 2}", 0))
                            q_sl = qfm_t(hq)[hp:hp + 64, j0 * 128:(j0 + 2) * 128]
                            for si, kt in enumerate(g):
                                last = (si == len(g) - 1)
                                ins = t.matmul(psc[:, si * 256:si * 256 + 256],
                                               kfm_t(hq)[hp:hp + 64, kt * 128:(kt + 1) * 128],
                                               q_sl,
                                               start=(si % 2 == 0),
                                               stop=(si % 2 == 1) or last)
                            ev[f"scmm_g{gnum}"] = inc(ins, "cT")
                        bb.tensor(_sc_mm)

                        has_mask = any(kt in mk for kt in g)
                        if has_mask:
                            def _mask(v, jp=jp, g=tuple(g), psc=psc, mk=tuple(mk), gnum=gnum):
                                wait(v, "cT", ev[f"scmm_g{gnum}"])
                                wait(v, "dMK0", ev["mload_all"])
                                for si, kt in enumerate(g):
                                    if kt in mk:
                                        mi = list(mk).index(kt)
                                        ins = v.tensor_tensor(psc[:, si * 256:si * 256 + 256],
                                                              psc[:, si * 256:si * 256 + 256],
                                                              mk_slot(jp, mi), ALU.add)
                                ev[f"mask_g{gnum}"] = inc(ins, "cV")
                            bb.vector(_mask)

                        def _exp(sc, g=tuple(g), psc=psc, slot=slot, gnum=gnum, jp=jp,
                                 has_mask=has_mask):
                            if has_mask:
                                wait(sc, "cV", ev[f"mask_g{gnum}"])
                            else:
                                wait(sc, "cT", ev[f"scmm_g{gnum}"])
                            wait(sc, "cT", ev.get(f"pv_g{gnum - 2}", 0))
                            ng = len(g)
                            ins = sc.activation(
                                pt_sb[:, slot * 1536:slot * 1536 + ng * 256],
                                psc[:, 0:ng * 256],
                                AF.Exp, bias=consts[:, 1:2], scale=1.0)
                            ev[f"exp_g{gnum}"] = inc(ins, "cA")
                            ev[f"mask_used{jp}"] = ev[f"exp_g{gnum}"]
                        bb.scalar(_exp)

                        def _pv(t, g=tuple(g), po=po, slot=slot, gnum=gnum, h=h,
                                gi=gi, ngroups=len(groups), itc=itc):
                            wait(t, "cA", ev[f"exp_g{gnum}"])
                            if gi == 0 and itc >= 2:
                                wait(t, "cV", ev.get(f"oevac{itc - 2}", 0))
                            for si, kt in enumerate(g):
                                first = (gi == 0 and si == 0)
                                lastg = (gi == ngroups - 1 and si == len(g) - 1)
                                t.matmul(po[:, 0:65],
                                         pt_sb[:, slot * 1536 + si * 256:
                                               slot * 1536 + si * 256 + 128],
                                         vtm_t(kt)[:, h * HW:(h + 1) * HW],
                                         start=first, stop=False)
                                ins = t.matmul(po[:, 65:130],
                                               pt_sb[:, slot * 1536 + si * 256 + 128:
                                                     slot * 1536 + si * 256 + 256],
                                               vtm_t(kt)[:, h * HW:(h + 1) * HW],
                                               start=False, stop=lastg)
                            ev[f"pv_g{gnum}"] = inc(ins, "cT")
                        bb.tensor(_pv)
                        gnum += 1

                    def _oev(v, jp=jp, h=h, po=po, itc=itc, j0=j0):
                        wait(v, "cT", ev[f"pv_g{gnum - 1}"])
                        inc(v.reciprocal(po[:, 64:130:65], po[:, 64:130:65]), "cV")
                        wait(v, "cV", cnt["cV"])
                        v.tensor_scalar(otm_t(j0)[:, h * 64:(h + 1) * 64],
                                        po[:, 0:64], po[:, 64:65], None, op0=ALU.mult)
                        ins = v.tensor_scalar(otm_t(j0 + 1)[:, h * 64:(h + 1) * 64],
                                              po[:, 65:129], po[:, 129:130], None, op0=ALU.mult)
                        ev[f"oevac{itc}"] = inc(ins, "cV")
                    bb.vector(_oev)
                    itc += 1

                def _so(sy, jp=jp, val=cnt["cV"]):
                    wait(sy, "cV", val)
                    for j in (2 * jp, 2 * jp + 1):
                        inc(sy.dma_start(scr_o[j * 128:(j + 1) * 128, :], otm_t(j)), "dSO")
                bb.sync(_so)
            emit_casts(10 ** 6)
            emit_p2(len(p2_steps))
            ev["otm"] = cnt["cV"]
            ev["scr_o"] = cnt["dSO"]
            ev["attn_T"] = cnt["cT"]
            ev["attn_A"] = cnt["cA"]
            bb.flush()
        attn_es.close()

        otm_es.close()

        hslab = stack.enter_context(nc.sbuf_tensor("hslab", [128, DHT * TC], BF16, side="left"))
        cslab = stack.enter_context(nc.sbuf_tensor("cslab", [128, DHT * TC], BF16, side="left"))
        h_t = lambda m: hslab[:, m * TC:(m + 1) * TC]
        c_t = lambda m: cslab[:, m * TC:(m + 1) * TC]

        fb = contextlib.ExitStack()
        with fb:
            ln_tmp = fb.enter_context(nc.sbuf_tensor("ln_tmp", [128, D], F32, side="right"))
            x2fm = fb.enter_context(nc.sbuf_tensor("x2fm", [128, DT * TC], BF16, side="right"))
            wbuf = fb.enter_context(nc.sbuf_tensor("wbuf", [128, 2 * 32 * 256], BF16, side="right"))
            silu_sb = fb.enter_context(nc.sbuf_tensor("silu_sb", [128, 2 * 512], F32, side="right"))
            ff_sb = fb.enter_context(nc.sbuf_tensor("ff_sb", [128, 512], BF16, side="right"))
            ff_tm = fb.enter_context(nc.sbuf_tensor("ff_tm", [128, 4 * D], BF16, side="right"))
            y_sb = fb.enter_context(nc.sbuf_tensor("y_sb", [128, 2 * D], F32, side="right"))
            wt_o = fb.enter_context(nc.sbuf_tensor("wt_o", [128, DT * D], BF16, side="right"))
            ofm = fb.enter_context(nc.sbuf_tensor("ofm", [128, DT * 512], BF16, side="right"))
            at_sb = fb.enter_context(nc.sbuf_tensor("at_sb", [128, 2 * 512], BF16, side="right"))
            at_tm = fb.enter_context(nc.sbuf_tensor("at_tm", [128, 4 * D], BF16, side="right"))
            lb2 = fb.enter_context(nc.sbuf_tensor("lb2", [128, 2 * D], BF16, side="right"))
            xpg = fb.enter_context(nc.sbuf_tensor("xpg", [128, 4 * D], F32, side="right"))
            xpl = fb.enter_context(nc.sbuf_tensor("xpl", [128, 2 * D], F32, side="right"))
            ps_f = [fb.enter_context(nc.psum_tensor(f"ps_f{i}", [128, 512], F32))
                    for i in range(8)]
            ps_w = [ps_f[4], ps_f[5]]
            x2fm_t = lambda kd: x2fm[:, kd * TC:(kd + 1) * TC]
            wt_o_t = lambda kd: wt_o[:, kd * D:(kd + 1) * D]
            ofm_t = lambda kd: ofm[:, kd * 512:(kd + 1) * 512]
            xpg_t = lambda j: xpg[:, (j % 4) * D:(j % 4 + 1) * D]

            def wbuf_t(par, i):
                return wbuf[:, (par * 32 + i) * 256:(par * 32 + i + 1) * 256]

            def woT_load():
                def _w(sy):
                    wait(sy, "dgA", ev["dgA"])
                    wait(sy, "cT", ev["attn_T"])
                    wait(sy, "cV", ev["otm"])
                    for kd in range(DT):
                        inc(sy.dma_start(wt_o_t(kd), n_wo[:, kd * 128:(kd + 1) * 128],
                                         transpose=True), "dWO")
                    ev["woT"] = cnt["dWO"]
                bb.sync(_w)

            def a3_ofl(g):
                def _ofl(sy, g=g):
                    if g == 0:
                        wait(sy, "cV", ev["consts"])
                        wait(sy, "cT", ev["attn_T"])
                    wait(sy, "cT", ev.get(f"wo_mm_g{g - 1}", 0))
                    wait(sy, "dSO", ev["scr_o"])
                    for kd in range(DT):
                        inc(sy.dma_start(ofm_t(kd),
                                         scr_o[g * 512:(g + 1) * 512, kd * 128:(kd + 1) * 128],
                                         transpose=True), "dOF0")
                    ev[f"ofl{g}"] = cnt["dOF0"]
                bb.sync(_ofl)

            def a3_wo_m(g, m):
                def _mm(t, g=g, m=m):
                    if g == 0 and m == 0:
                        wait(t, "cV", ev["otm"])
                        wait(t, "cA", ev["attn_A"])
                    wait(t, "dOF0", ev[f"ofl{g}"])
                    wait(t, "dWO", ev["woT"])
                    wait(t, "cA", ev.get(f"at_ev{m % 2}", 0))
                    for kd in range(DT):
                        ins = t.matmul(ps_w[m % 2][:, 0:512],
                                       wt_o_t(kd)[:, m * 128:(m + 1) * 128],
                                       ofm_t(kd),
                                       start=(kd == 0), stop=(kd == DT - 1))
                    ev["at_mm"] = inc(ins, "cT")
                    if m == DT - 1:
                        ev[f"wo_mm_g{g}"] = ev["at_mm"]
                bb.tensor(_mm)

                def _ev(sc, g=g, m=m):
                    wait(sc, "cT", ev["at_mm"])
                    wait(sc, f"dAT{m % 2}", ev.get(f"at_w{m % 2}", 0))
                    wait(sc, "cV", ev["scales"])
                    ev["at_act"] = inc(
                        sc.activation(at_sb[:, (m % 2) * 512:(m % 2) * 512 + 512],
                                      ps_w[m % 2][:, 0:512], AF.Identity,
                                      bias=scales[:, BOC + m:BOC + m + 1], scale=1.0), "cA")
                    ev[f"at_ev{m % 2}"] = ev["at_act"]
                bb.scalar(_ev)

                def _st(sy, g=g, m=m):
                    wait(sy, "cA", ev["at_act"])
                    ev[f"at_w{m % 2}"] = inc(sy.dma_start(
                        scr_at[m * 128:(m + 1) * 128, g * 512:(g + 1) * 512],
                        at_sb[:, (m % 2) * 512:(m % 2) * 512 + 512]), f"dAT{m % 2}")
                    if m == DT - 1:
                        ev[f"at_grp{g}"] = (cnt["dAT0"], cnt["dAT1"])
                bb.gpsimd(_st)

            def a3_front(j):
                slot = j % 4
                g = j // 4

                def _l(sy, j=j, slot=slot, g=g):
                    wait(sy, "dAT0", ev[f"at_grp{g}"][0])
                    wait(sy, "dAT1", ev[f"at_grp{g}"][1])
                    wait(sy, "cV", ev.get(f"lb2{j - 4}", 0))
                    wait(sy, f"dXP{j % 4}", ev.get(f"xpw{j - 4}", 0))
                    inc(sy.dma_start(at_tm[:, slot * D:(slot + 1) * D],
                                     scr_at[:, j * 128:(j + 1) * 128], transpose=True),
                        f"dTM{slot}")
                    ev[f"resx{j}"] = inc(sy.dma_start(xpg_t(j), x_own[j * 128:(j + 1) * 128, :]),
                                         f"dTM{slot}")
                bb.sync(_l)
                ln_stats(at_tm[:, slot * D:(slot + 1) * D], slot,
                         [(f"dTM{slot}", ev[f"resx{j}"])])
                ev[f"a3rstd{j}"] = ev[f"ln_rstd{slot}"]

            def a3_finish(j):
                slot = j % 4
                c0 = 4 * slot

                def _f(v, j=j, slot=slot, c0=c0, vprev=cnt["cV"]):
                    wait(v, "cA", ev[f"a3rstd{j}"])
                    wait(v, f"dTM{slot}", ev[f"resx{j}"])
                    wait(v, "cV", vprev)
                    inc(v.tensor_scalar(ln_tmp[:, :], at_tm[:, slot * D:(slot + 1) * D],
                                        ln_st[:, c0:c0 + 1], ln_st[:, c0 + 2:c0 + 3],
                                        op0=ALU.add, op1=ALU.mult), "cV")
                    wait(v, "cV", cnt["cV"])
                    ins = v.tensor_tensor(xpg_t(j), xpg_t(j), ln_tmp[:, :], ALU.add)
                    ev[f"xp{j}"] = inc(ins, "cV")
                bb.vector(_f)

                def _sp(sy, j=j):
                    wait(sy, "cV", ev[f"xp{j}"])
                    ev[f"xpw{j}"] = inc(sy.dma_start(scr_xp[j * 128:(j + 1) * 128, :],
                                                     xpg_t(j)), f"dXP{j % 4}")
                bb.gpsimd(_sp)

            def ln2_front(j):
                slot = 4 + j % 4
                ln_stats(xpg_t(j), slot, [("cV", ev[f"xp{j}"])])
                ev[f"l2rstd{j}"] = ev[f"ln_rstd{slot}"]

            def ln2_finish(j):
                slot = 4 + j % 4
                b2 = j % 2
                c0 = 4 * slot

                def _f(v, j=j, slot=slot, c0=c0, b2=b2):
                    wait(v, "cA", ev[f"l2rstd{j}"])
                    wait(v, f"dX2S{b2}", ev.get(f"x2w{j - 2}", 0))
                    ins = v.tensor_scalar(lb2[:, b2 * D:(b2 + 1) * D], xpg_t(j),
                                          ln_st[:, c0:c0 + 1], ln_st[:, c0 + 2:c0 + 3],
                                          op0=ALU.add, op1=ALU.mult)
                    ev[f"lb2{j}"] = inc(ins, "cV")
                bb.vector(_f)

                def _w(sy, j=j, b2=b2):
                    wait(sy, "cV", ev[f"lb2{j}"])
                    ev[f"x2w{j}"] = inc(sy.dma_start(
                        scr_x2[j * 128:(j + 1) * 128, :], lb2[:, b2 * D:(b2 + 1) * D]), f"dX2S{b2}")
                bb.gpsimd(_w)

            a3_pend = []

            def pump(n):
                while n > 0 and a3_pend:
                    f_ = a3_pend.pop(0)
                    if f_ is not None:
                        f_()
                    n -= 1

            def a3_group_bundles(g):
                js = [4 * g + i_ for i_ in range(4)]
                out = []
                if g == 0:
                    out.append(woT_load)
                out.append(lambda g=g: a3_ofl(g))
                out.extend([None] * 6)
                for m in range(DT):
                    out.append(lambda g=g, m=m: a3_wo_m(g, m))
                out.extend([None] * 2)
                out.append(lambda j=js[0]: a3_front(j))
                out.append(lambda j=js[1]: a3_front(j))
                out.append(lambda j=js[0]: (a3_finish(j), ln2_front(j)))
                out.append(lambda j=js[2]: a3_front(j))
                out.append(lambda j=js[1]: (a3_finish(j), ln2_front(j)))
                out.append(lambda j=js[3]: a3_front(j))
                out.append(lambda j=js[2]: (a3_finish(j), ln2_front(j)))
                out.append(lambda j=js[0]: ln2_finish(j))
                out.append(lambda j=js[3]: (a3_finish(j), ln2_front(j)))
                out.append(lambda j=js[1]: ln2_finish(j))
                out.append(lambda j=js[2]: ln2_finish(j))
                out.append(lambda j=js[3]: ln2_finish(j))
                return out

            for f_ in a3_group_bundles(0):
                if f_ is not None:
                    f_()

            for C in range(NCH):
                pump(10 ** 6)
                if C + 1 < NCH:
                    a3_pend.extend(a3_group_bundles(C + 1))

                def _x2l(sy, C=C):
                    wait(sy, "dX2S0", ev[f"x2w{4 * C + 2}"])
                    wait(sy, "dX2S1", ev[f"x2w{4 * C + 3}"])
                    if C > 0:
                        wait(sy, "cT", ev.get(f"ab_mm_all{C - 1}", 0))
                    for kd in range(DT):
                        inc(sy.dma_start(x2fm_t(kd),
                                         scr_x2[C * 512:(C + 1) * 512, kd * 128:(kd + 1) * 128],
                                         transpose=True), "dX2")
                    ev["x2fm"] = cnt["dX2"]
                bb.sync(_x2l)

                prev_T_ab = cnt["cT"]
                for mb in range(DH // 256):
                    par = mb % 2

                    def _wl(sy, mb=mb, par=par, C=C, prev_T_ab=prev_T_ab):
                        if C == 0 and mb == 0:
                            wait(sy, "dPW0", ev["p2_ab"][0])
                            wait(sy, "dPW1", ev["p2_ab"][1])
                        wait(sy, "cT", max(ev.get(f"ab_mmb{mb - 2}", 0) if mb >= 2 else 0, prev_T_ab))
                        d1 = wbuf[:, (par * 32) * 256:(par * 32 + DT) * 256]
                        d2 = wbuf[:, (par * 32 + 8) * 256:(par * 32 + 8 + DT) * 256]
                        inc(sy.dma_start(
                            d1.rearrange("p (a c) -> p a c", c=256),
                            t_w1.rearrange("(a p) c -> p a c", p=128)[:, :, mb * 256:(mb + 1) * 256]),
                            f"dW{par}")
                        inc(sy.dma_start(
                            d2.rearrange("p (a c) -> p a c", c=256),
                            t_w2.rearrange("(a p) c -> p a c", p=128)[:, :, mb * 256:(mb + 1) * 256]),
                            f"dW{par}")
                        ev[f"wl_ab{mb}"] = cnt[f"dW{par}"]
                    bb.scalar(_wl)

                    for mt in range(2):
                        m = 2 * mb + mt

                        def _mm(t, mb=mb, par=par, mt=mt, m=m, C=C):
                            wait(t, f"dW{par}", ev[f"wl_ab{mb}"])
                            wait(t, "dX2", ev["x2fm"])
                            if C > 0 and m < 2:
                                wait(t, "cA", ev["o3_done"])
                            wait(t, "cV", ev.get(f"h_ev{m - 2}", 0) if m >= 2 else 0)
                            pa, pb_ = ps_f[2 * mt], ps_f[2 * mt + 1]
                            for kd in range(DT):
                                t.matmul(pa[:, 0:TC],
                                         wbuf_t(par, kd)[:, mt * 128:(mt + 1) * 128],
                                         x2fm_t(kd),
                                         start=(kd == 0), stop=(kd == DT - 1))
                            for kd in range(DT):
                                ins = t.matmul(pb_[:, 0:TC],
                                               wbuf_t(par, 8 + kd)[:, mt * 128:(mt + 1) * 128],
                                               x2fm_t(kd),
                                               start=(kd == 0), stop=(kd == DT - 1))
                            ev[f"ab_mm{m}"] = inc(ins, "cT")
                            if mt == 1:
                                ev[f"ab_mmb{mb}"] = ev[f"ab_mm{m}"]
                        bb.tensor(_mm)

                        def _silu(sc, mt=mt, m=m):
                            wait(sc, "cT", ev[f"ab_mm{m}"])
                            wait(sc, "cV", ev.get(f"h_ev{m - 2}", 0) if m >= 2 else 0)
                            ev[f"silu{mt}"] = inc(sc.activation(
                                silu_sb[:, mt * 512:mt * 512 + TC], ps_f[2 * mt][:, 0:TC],
                                AF.Silu, bias=0.0,
                                scale=scales[:, S1C + m:S1C + m + 1]), "cA")
                        bb.scalar(_silu)

                        def _h(v, mt=mt, m=m):
                            wait(v, "cA", ev[f"silu{mt}"])
                            ins = v.scalar_tensor_tensor(
                                h_t(m), ps_f[2 * mt + 1][:, 0:TC],
                                scales[:, S2C + m:S2C + m + 1],
                                silu_sb[:, mt * 512:mt * 512 + TC],
                                op0=ALU.mult, op1=ALU.mult)
                            ev[f"h_ev{m}"] = inc(ins, "cV")
                        bb.vector(_h)
                    pump(1)
                ev[f"ab_mm_all{C}"] = cnt["cT"]
                ev["h_all"] = cnt["cV"]

                def _hid_pass(nsrc, src_t, dst_t, scol, dst_name, dep, p2gate, C=C):
                    prev_T = cnt["cT"]
                    for mb in range(DH // 256):
                        par = mb % 2
                        pb = ps_f[2 * par:2 * par + 2]

                        def _wl(sy, mb=mb, par=par):
                            if C == 0 and mb == 0:
                                wait(sy, "dPW0", ev[p2gate][0])
                                wait(sy, "dPW1", ev[p2gate][1])
                            wait(sy, "cT", max(ev.get(f"{dst_name}_mm{mb - 2}", 0) if mb >= 2 else 0, prev_T))
                            dall = wbuf[:, (par * 32) * 256:(par * 32 + 16) * 256]
                            dall2 = wbuf[:, (par * 32 + 16) * 256:(par * 32 + 32) * 256]
                            tv = nsrc.rearrange("(a p) c -> p a c", p=128)
                            inc(sy.dma_start(
                                dall.rearrange("p (a c) -> p a c", c=256),
                                tv[:, 0:16, mb * 256:(mb + 1) * 256]), f"dW{par}")
                            inc(sy.dma_start(
                                dall2.rearrange("p (a c) -> p a c", c=256),
                                tv[:, 16:32, mb * 256:(mb + 1) * 256]), f"dW{par}")
                            ev[f"{dst_name}_wl{mb}"] = cnt[f"dW{par}"]
                        bb.scalar(_wl)

                        def _mm(t, mb=mb, par=par, pb=pb):
                            wait(t, f"dW{par}", ev[f"{dst_name}_wl{mb}"])
                            if mb == 0:
                                wait(t, dep[0], dep[1])
                            wait(t, "cA", ev.get(f"{dst_name}_ev{mb - 2}", 0) if mb >= 2 else 0)
                            for mt in range(2):
                                for k in range(DHT):
                                    ins = t.matmul(pb[mt][:, 0:TC],
                                                   wbuf_t(par, k)[:, mt * 128:(mt + 1) * 128],
                                                   src_t(k),
                                                   start=(k == 0), stop=(k == DHT - 1))
                            ev[f"{dst_name}_mm{mb}"] = inc(ins, "cT")
                        bb.tensor(_mm)

                        def _ge(sc, mb=mb, pb=pb):
                            wait(sc, "cT", ev[f"{dst_name}_mm{mb}"])
                            for mt in range(2):
                                m = mb * 2 + mt
                                ins = sc.activation(dst_t(m), pb[mt][:, 0:TC], AF.Gelu,
                                                    bias=0.0,
                                                    scale=scales[:, scol + m:scol + m + 1])
                            ev[f"{dst_name}_ev{mb}"] = inc(ins, "cA")
                        bb.scalar(_ge)
                        pump(1)
                    ev[f"{dst_name}_all"] = cnt["cA"]
                    ev[f"{dst_name}_mm_done"] = cnt["cT"]

                _hid_pass(t_h1, h_t, c_t, SH1C, "c", ("cV", ev["h_all"]), "p2_h1")
                _hid_pass(t_h2, c_t, h_t, SH2C, "d", ("cA", ev["c_all"]), "p2_h2")

                if debug and C == 0:
                    bb.flush()
                    _dump_fm(b, nc, blk, ev, c_t, DHT, TC, dbg["csb"],
                             dep=("cA", ev["c_all"]))
                    _dump_fm(b, nc, blk, ev, h_t, DHT, TC, dbg["hsb"],
                             dep=("cA", ev["d_all"]))

                prev_T_o3 = cnt["cT"]
                for mb in range(D // 256):
                    par = mb % 2
                    pb = ps_f[2 * par:2 * par + 2]

                    def _wl(sy, mb=mb, par=par, C=C, prev_T_o3=prev_T_o3):
                        if C == 0 and mb == 0:
                            wait(sy, "dPW0", ev["p2_w3"][0])
                            wait(sy, "dPW1", ev["p2_w3"][1])
                        wait(sy, "cT", max(ev.get(f"o3_mm{mb - 2}", 0) if mb >= 2 else 0, prev_T_o3))
                        dall = wbuf[:, (par * 32) * 256:(par * 32 + 16) * 256]
                        dall2 = wbuf[:, (par * 32 + 16) * 256:(par * 32 + 32) * 256]
                        tv = t_w3.rearrange("(a p) c -> p a c", p=128)
                        inc(sy.dma_start(
                            dall.rearrange("p (a c) -> p a c", c=256),
                            tv[:, 0:16, mb * 256:(mb + 1) * 256]), f"dW{par}")
                        inc(sy.dma_start(
                            dall2.rearrange("p (a c) -> p a c", c=256),
                            tv[:, 16:32, mb * 256:(mb + 1) * 256]), f"dW{par}")
                        ev[f"o3_wl{mb}"] = cnt[f"dW{par}"]
                    bb.scalar(_wl)

                    def _mm(t, mb=mb, par=par, pb=pb):
                        wait(t, f"dW{par}", ev[f"o3_wl{mb}"])
                        if mb == 0:
                            wait(t, "cA", ev["d_all"])
                        wait(t, "cA", ev.get(f"o3_ev{mb - 2}", 0) if mb >= 2 else 0)
                        for mt in range(2):
                            for k in range(DHT):
                                ins = t.matmul(pb[mt][:, 0:TC],
                                               wbuf_t(par, k)[:, mt * 128:(mt + 1) * 128],
                                               h_t(k),
                                               start=(k == 0), stop=(k == DHT - 1))
                        ev[f"o3_mm{mb}"] = inc(ins, "cT")
                    bb.tensor(_mm)

                    for mt in range(2):
                        m = mb * 2 + mt

                        def _ev(sc, mb=mb, pb=pb, mt=mt, m=m):
                            wait(sc, "cT", ev[f"o3_mm{mb}"])
                            wait(sc, "dFS", ev.get("ff_w", 0))
                            ev["ff_act"] = inc(sc.activation(
                                ff_sb[:, :], pb[mt][:, 0:TC], AF.Copy, bias=0.0,
                                scale=scales[:, S3C + m:S3C + m + 1]), "cA")
                            if mt == 1:
                                ev[f"o3_ev{mb}"] = ev["ff_act"]
                        bb.scalar(_ev)

                        def _st(sy, m=m, C=C):
                            wait(sy, "cA", ev["ff_act"])
                            ev["ff_w"] = inc(sy.dma_start(
                                scr_ff[m * 128:(m + 1) * 128, C * 512:(C + 1) * 512],
                                ff_sb[:, :]), "dFS")
                        bb.gpsimd(_st)
                    pump(1)
                ev[f"scr_ff{C}"] = cnt["dFS"]
                ev["o3_done"] = cnt["cA"]

                def ln4_front(jj, C=C):
                    j = C * 4 + jj
                    slot = jj

                    def _l(sy, j=j, slot=slot, C=C):
                        wait(sy, "dFS", ev[f"scr_ff{C}"])
                        wait(sy, "cV", ev.get(f"ysb{j - 4}", 0))
                        ev[f"ff_tm{j}"] = inc(sy.dma_start(
                            ff_tm[:, slot * D:(slot + 1) * D],
                            scr_ff[:, j * 128:(j + 1) * 128], transpose=True), f"dFT{slot}")
                    bb.sync(_l)
                    ln_stats(ff_tm[:, slot * D:(slot + 1) * D], 8 + jj,
                             [(f"dFT{slot}", ev[f"ff_tm{j}"])])
                    ev[f"l4rstd{j}"] = ev[f"ln_rstd{8 + jj}"]

                def xpl_load(jj, C=C):
                    j = C * 4 + jj

                    def _l(sy, j=j, jj=jj):
                        wait(sy, "cV", ev.get(f"ysb{j - 2}", 0))
                        wait(sy, f"dXP{j % 4}", ev[f"xpw{j}"])
                        ev[f"xpl{j}"] = inc(sy.dma_start(
                            xpl[:, (jj % 2) * D:(jj % 2 + 1) * D],
                            scr_xp[j * 128:(j + 1) * 128, :]), f"dXL{jj % 2}")
                    bb.scalar(_l)

                def ln4_finish(jj, C=C):
                    j = C * 4 + jj
                    slot = jj
                    c0 = 4 * (8 + jj)

                    def _f(v, j=j, slot=slot, c0=c0, jj=jj, vprev=cnt["cV"]):
                        wait(v, "cA", ev[f"l4rstd{j}"])
                        wait(v, f"dXL{jj % 2}", ev[f"xpl{j}"])
                        wait(v, f"dY{slot % 2}", ev.get(f"yout{j - 2}", 0))
                        wait(v, "cV", vprev)
                        inc(v.tensor_scalar(ln_tmp[:, :], ff_tm[:, slot * D:(slot + 1) * D],
                                            ln_st[:, c0:c0 + 1], ln_st[:, c0 + 2:c0 + 3],
                                            op0=ALU.add, op1=ALU.mult), "cV")
                        wait(v, "cV", cnt["cV"])
                        ins = v.tensor_add(y_sb[:, (slot % 2) * D:(slot % 2 + 1) * D],
                                           ln_tmp[:, :], xpl[:, (jj % 2) * D:(jj % 2 + 1) * D])
                        ev[f"ysb{j}"] = inc(ins, "cV")
                    bb.vector(_f)

                    def _o(sy, j=j, slot=slot):
                        wait(sy, "cV", ev[f"ysb{j}"])
                        ev[f"yout{j}"] = inc(sy.dma_start(
                            y[j * 128:(j + 1) * 128, :], y_sb[:, (slot % 2) * D:(slot % 2 + 1) * D]), f"dY{slot % 2}")
                    bb.gpsimd(_o)

                ln4_front(0)
                xpl_load(0)
                ln4_front(1)
                xpl_load(1)
                ln4_finish(0)
                ln4_front(2)
                xpl_load(2)
                ln4_finish(1)
                ln4_front(3)
                xpl_load(3)
                ln4_finish(2)
                ln4_finish(3)
                bb.flush()

            def _fin(sy):
                for s in cnt:
                    if s.startswith("d"):
                        wait(sy, s, cnt[s])
            bb.sync(_fin)
            bb.flush()
        p2es.close()

    return nc


_DUMP_ID = [0]


def _dump_fm(b, nc, blk, ev, slab_t, n_tiles, width, dram, dep=None):
    _DUMP_ID[0] += 1
    nm = f"dump{_DUMP_ID[0]}"
    with nc.sbuf_tensor(nm, [128, width], F32) as df:
        for i in range(n_tiles):
            def _c(v, i=i):
                if dep is not None:
                    b.wait(v, dep[0], dep[1])
                b.wait(v, "dDG", ev.get("dump_o", 0))
                ev["dump_c"] = b.inc(v.tensor_copy(df[:, 0:width], slab_t(i)), "cV")
            blk.vector(_c)

            def _o(sy, i=i):
                b.wait(sy, "cV", ev["dump_c"])
                ev["dump_o"] = b.inc(sy.dma_start(dram[i * 128:(i + 1) * 128, 0:width],
                                                  df[:, 0:width]), "dDG")
            blk.sync(_o)

        def _g(v):
            b.wait(v, "dDG", ev["dump_o"])
            v.engine_nop()
        blk.vector(_g)


_dump_tm = _dump_fm



def _build_mask(parity):
    j = np.arange(NJ)
    g = 2 * j + parity
    gidx = (g[:, None] * 128 + np.arange(128)[None, :]).reshape(-1)
    k = np.arange(TKP)
    vis = (3.0 * k[:, None] < gidx[None, :] + 0.001) & (k[:, None] < TK)
    m = np.where(vis, np.float32(0.0), np.float32(-1e9)).astype(np.float32)
    return m


_NC_CACHE = {}


def _get_nc(debug=False):
    key = bool(debug)
    if key not in _NC_CACHE:
        _NC_CACHE[key] = build_core_program(debug=key)
    return _NC_CACHE[key]


def kernel(L0_attn_x, L0_attn_Wq, L0_attn_Wk, L0_attn_Wv, L0_attn_Wo, L0_attn_bo,
           L0_attn_Wconv, L0_ffn_w1_w, L0_ffn_w1_s, L0_ffn_w2_w, L0_ffn_w2_s,
           L0_ffn_h1_w, L0_ffn_h1_s, L0_ffn_h2_w, L0_ffn_h2_s, L0_ffn_w3_w,
           L0_ffn_w3_s):
    x = np.ascontiguousarray(np.asarray(L0_attn_x, dtype=np.float32))
    masks = [_build_mask(0), _build_mask(1)]
    own_rows = []
    for par in range(2):
        g = 2 * np.arange(NJ) + par
        own_rows.append((g[:, None] * 128 + np.arange(128)[None, :]).reshape(-1))

    common = {
        "wq": np.ascontiguousarray(L0_attn_Wq, dtype=np.float32),
        "wk": np.ascontiguousarray(L0_attn_Wk, dtype=np.float32),
        "wv": np.ascontiguousarray(L0_attn_Wv, dtype=np.float32),
        "wo": np.ascontiguousarray(L0_attn_Wo, dtype=np.float32),
        "bo": np.ascontiguousarray(L0_attn_bo, dtype=np.float32),
        "wconv": np.ascontiguousarray(L0_attn_Wconv, dtype=np.float32),
        "w1": np.ascontiguousarray(L0_ffn_w1_w, dtype=np.float32),
        "w2": np.ascontiguousarray(L0_ffn_w2_w, dtype=np.float32),
        "h1": np.ascontiguousarray(L0_ffn_h1_w, dtype=np.float32),
        "h2": np.ascontiguousarray(L0_ffn_h2_w, dtype=np.float32),
        "w3": np.ascontiguousarray(L0_ffn_w3_w, dtype=np.float32),
        "s1": np.ascontiguousarray(L0_ffn_w1_s, dtype=np.float32),
        "s2": np.ascontiguousarray(L0_ffn_w2_s, dtype=np.float32),
        "sh1": np.ascontiguousarray(L0_ffn_h1_s, dtype=np.float32),
        "sh2": np.ascontiguousarray(L0_ffn_h2_s, dtype=np.float32),
        "s3": np.ascontiguousarray(L0_ffn_w3_s, dtype=np.float32),
    }
    in_maps = []
    for c in range(8):
        bb, par = c // 2, c % 2
        m = dict(common)
        m["x_full"] = np.ascontiguousarray(x[bb])
        m["x_own"] = np.ascontiguousarray(x[bb][own_rows[par]])
        m["maskT"] = masks[par]
        in_maps.append(m)

    nc = _get_nc(debug=False)
    res = run_bass_kernel_spmd(nc, in_maps, core_ids=list(range(8)))

    out = np.empty((B, T, D), dtype=np.float32)
    for c in range(8):
        bb, par = c // 2, c % 2
        out[bb][own_rows[par]] = res.results[c]["y"]
    return out


# version 21
# speedup vs baseline: 1.4187x; 1.4187x over previous
import numpy as np
import contextlib
import concourse.bass as bass
import concourse.mybir as mybir
from concourse.bass_utils import run_bass_kernel_spmd

F32 = mybir.dt.float32
BF16 = mybir.dt.bfloat16
AF = mybir.ActivationFunctionType
ALU = mybir.AluOpType
AX = mybir.AxisListType

B, T, D, H, HD = 4, 4096, 1024, 16, 64
TOK, NJ = 2048, 16
TK, TKP, NKT = 1366, 1408, 11
DH = 4096
TC, NCH = 512, 4
EXPB = 40.0
DT, DHT = D // 128, DH // 128
HW = 65


def _kvis(j):
    return min(TKP, ((2 * j + 1) * 128 + 127) // 3 + 1)


def _klo(j):
    return (2 * j) * 128 // 3 + 1


def _ktiles(j):
    return (_kvis(j) + 127) // 128


def _masked_kts(j):
    return list(range(_klo(j) // 128, _ktiles(j)))


MAX_MASK = max(len(_masked_kts(j)) for j in range(NJ))


class Bld:
    def __init__(self, nc, block, sems):
        self.nc = nc
        self.block = block
        self.sem = sems
        self.cnt = {k: 0 for k in sems}
        self._record = None
        self._replay = None

    def inc(self, instr, name, k=None):
        if k is None:
            k = 16 if name.startswith("d") else 1
        instr.then_inc(self.sem[name], k)
        self.cnt[name] += k
        return self.cnt[name]

    def wait(self, eng, name, val):
        if self._record is not None:
            self._record.append((name, val))
            return
        if self._replay is not None:
            name, val = self._replay.pop(0)
        if val and val > 0:
            eng.wait_ge(self.sem[name], val)

class _DummyInstr:
    def then_inc(self, sem, k):
        return self

    def annotate(self, *a, **k):
        return self


class _ShadowEngine:
    def __getattr__(self, name):
        return lambda *a, **k: _DummyInstr()


class PhaseBatcher:

    ENGINES = ["sync", "gpsimd", "tensor", "vector", "scalar"]

    def __init__(self, blk, bld):
        self.blk = blk
        self.bld = bld
        self.thunks = []
        self.recs = []
        self._cnt0 = None

    def _add(self, eng, fn):
        b = self.bld
        if not self.thunks:
            self._cnt0 = dict(b.cnt)
        stub = _ShadowEngine()
        b._record = []
        fn(stub)
        self.recs.append(b._record)
        b._record = None
        self.thunks.append((eng, fn))

    def sync(self, fn):
        self._add("sync", fn)

    def gpsimd(self, fn):
        self._add("gpsimd", fn)

    def tensor(self, fn):
        self._add("tensor", fn)

    def vector(self, fn):
        self._add("vector", fn)

    def scalar(self, fn):
        self._add("scalar", fn)

    def flush(self):
        if not self.thunks:
            return
        b = self.bld
        cnt_end = dict(b.cnt)
        recs = self.recs
        b.cnt.clear()
        b.cnt.update(self._cnt0)
        for eng in self.ENGINES:
            mine = [(i, fn) for i, (e, fn) in enumerate(self.thunks) if e == eng]
            if not mine:
                continue

            def _run(engine, mine=mine):
                for i, fn in mine:
                    b._replay = list(recs[i])
                    fn(engine)
                b._replay = None
            getattr(self.blk, eng)(_run)
        assert b.cnt == cnt_end, (b.cnt, cnt_end)
        self.thunks = []
        self.recs = []
        self._cnt0 = None



def build_core_program(debug=False):
    nc = bass.Bass("TRN2", target_bir_lowering=False)

    x_full = nc.dram_tensor("x_full", [T, D], F32, kind="ExternalInput").ap()
    x_own = nc.dram_tensor("x_own", [TOK, D], F32, kind="ExternalInput").ap()
    maskT = nc.dram_tensor("maskT", [TKP, TOK], F32, kind="ExternalInput").ap()
    wq = nc.dram_tensor("wq", [D, D], F32, kind="ExternalInput").ap()
    wk = nc.dram_tensor("wk", [D, D], F32, kind="ExternalInput").ap()
    wv = nc.dram_tensor("wv", [D, D], F32, kind="ExternalInput").ap()
    wo = nc.dram_tensor("wo", [D, D], F32, kind="ExternalInput").ap()
    bo = nc.dram_tensor("bo", [D], F32, kind="ExternalInput").ap()
    wconv = nc.dram_tensor("wconv", [D, D, 3], F32, kind="ExternalInput").ap()
    w1 = nc.dram_tensor("w1", [DH, D], F32, kind="ExternalInput").ap()
    w2 = nc.dram_tensor("w2", [DH, D], F32, kind="ExternalInput").ap()
    h1 = nc.dram_tensor("h1", [DH, DH], F32, kind="ExternalInput").ap()
    h2 = nc.dram_tensor("h2", [DH, DH], F32, kind="ExternalInput").ap()
    w3 = nc.dram_tensor("w3", [D, DH], F32, kind="ExternalInput").ap()
    s1 = nc.dram_tensor("s1", [DH], F32, kind="ExternalInput").ap()
    s2 = nc.dram_tensor("s2", [DH], F32, kind="ExternalInput").ap()
    sh1 = nc.dram_tensor("sh1", [DH], F32, kind="ExternalInput").ap()
    sh2 = nc.dram_tensor("sh2", [DH], F32, kind="ExternalInput").ap()
    s3 = nc.dram_tensor("s3", [D], F32, kind="ExternalInput").ap()
    y = nc.dram_tensor("y", [TOK, D], F32, kind="ExternalOutput").ap()

    n_wq = nc.dram_tensor("n_wq", [D, D], BF16, kind="Internal").ap()
    n_wk = nc.dram_tensor("n_wk", [D, D], BF16, kind="Internal").ap()
    n_wv = nc.dram_tensor("n_wv", [D, D], BF16, kind="Internal").ap()
    n_wo = nc.dram_tensor("n_wo", [D, D], BF16, kind="Internal").ap()
    n_cv = [nc.dram_tensor(f"n_cv{w}", [D, D], BF16, kind="Internal").ap() for w in range(3)]
    n_w1 = nc.dram_tensor("n_w1", [DH, D], BF16, kind="Internal").ap()
    n_w2 = nc.dram_tensor("n_w2", [DH, D], BF16, kind="Internal").ap()
    n_h1 = nc.dram_tensor("n_h1", [DH, DH], BF16, kind="Internal").ap()
    n_h2 = nc.dram_tensor("n_h2", [DH, DH], BF16, kind="Internal").ap()
    n_w3 = nc.dram_tensor("n_w3", [D, DH], BF16, kind="Internal").ap()

    t_w1 = nc.dram_tensor("t_w1", [D, DH], BF16, kind="Internal").ap()
    t_w2 = nc.dram_tensor("t_w2", [D, DH], BF16, kind="Internal").ap()
    t_h1 = nc.dram_tensor("t_h1", [DH, DH], BF16, kind="Internal").ap()
    t_h2 = nc.dram_tensor("t_h2", [DH, DH], BF16, kind="Internal").ap()
    t_w3 = nc.dram_tensor("t_w3", [DH, D], BF16, kind="Internal").ap()

    scr_xf = nc.dram_tensor("scr_xf", [T, D], BF16, kind="Internal").ap()
    scr_xo = nc.dram_tensor("scr_xo", [TOK, D], BF16, kind="Internal").ap()
    scr_o = nc.dram_tensor("scr_o", [TOK, D], BF16, kind="Internal").ap()
    scr_at = nc.dram_tensor("scr_at", [D, TOK], BF16, kind="Internal").ap()
    scr_x2 = nc.dram_tensor("scr_x2", [TOK, D], BF16, kind="Internal").ap()
    scr_ff = nc.dram_tensor("scr_ff", [D, TOK], BF16, kind="Internal").ap()
    scr_xp = nc.dram_tensor("scr_xp", [TOK, D], F32, kind="Internal").ap()

    dbg = {}
    if debug:
        for nm, shp in [("xlnf", [T, D]), ("ktmp", [D, TKP]), ("kfm", [D, TKP]),
                        ("vtm", [TKP, 16 * HW]), ("qfm", [D, TOK]), ("otm", [TOK, D]),
                        ("xp", [TOK, D]), ("hsb", [DH, TC]), ("csb", [DH, TC]),
                        ("attn", [TOK, D])]:
            dbg[nm] = nc.dram_tensor("dbg_" + nm, shp, F32, kind="ExternalOutput").ap()

    stack = contextlib.ExitStack()
    with stack:
        sems = {}
        for s in ["do", "dCV", "dL0", "dL1", "dXW", "dWC", "dWK", "dXO",
                  "dMK0", "dMK1", "dSO", "dOF0", "dOF1", "dAT0", "dAT1", "dTM0", "dTM1", "dTM2", "dTM3", "dL2", "dL3", "dLS2", "dLS3",
                  "dX2S0", "dX2S1", "dX2S2", "dX2S3", "dX2", "dFT2", "dFT3", "dLS0", "dLS1", "dY0", "dY1", "dPT0", "dPT1", "dPW0", "dPW1", "dW0", "dW1", "dFS", "dFT0", "dFT1", "dY", "dDG",
                  "dgA", "dgF", "dgH1", "dgH2", "dXP0", "dXP1", "dXP2", "dXP3",
                  "dXL0", "dXL1", "dWO", "cT", "cV", "cA"]:
            sems[s] = stack.enter_context(nc.semaphore(s))
        blk = stack.enter_context(nc.Block())
        b = Bld(nc, blk, sems)
        bb = PhaseBatcher(blk, b)
        inc, wait, cnt = b.inc, b.wait, b.cnt
        ev = {}

        consts = stack.enter_context(nc.sbuf_tensor("consts", [128, 4], F32))
        NSC = 4 * DHT + 64
        scales = stack.enter_context(nc.sbuf_tensor("scales", [128, NSC], F32))
        S1C, S2C, SH1C, SH2C, S3C, BOC = 0, DHT, 2 * DHT, 3 * DHT, 4 * DHT, 4 * DHT + 32

        bb.vector(lambda v: (
            v.memset(consts[:, 0:1], 1e-5),
            v.memset(consts[:, 1:2], -EXPB),
            ev.__setitem__("consts", inc(v.memset(consts[:, 2:3], 0.0), "cV")),
        ))

        ln_st = stack.enter_context(nc.sbuf_tensor("ln_st", [128, 48], F32))
        ln_waste = stack.enter_context(nc.sbuf_tensor("ln_waste", [128, D], BF16))

        def _prep(g):
            def d2d(dst, src, rows, sem, chunk=512):
                for r0 in range(0, rows, chunk):
                    inc(g.dma_start(dst[r0:r0 + chunk, :], src[r0:r0 + chunk, :]), sem)
            d2d(n_wq, wq, D, "dgA")
            d2d(n_wk, wk, D, "dgA")
            d2d(n_wv, wv, D, "dgA")
            d2d(n_wo, wo, D, "dgA")
            d2d(n_w1, w1, DH, "dgF")
            d2d(n_w2, w2, DH, "dgF")
        bb.gpsimd(_prep)
        ev["dgA"], ev["dgF"], ev["dgH1"], ev["dgH2"] = cnt["dgA"], cnt["dgF"], cnt["dgH1"], cnt["dgH2"]

        with nc.sbuf_tensor("cv_nat", [128, 3 * D], F32) as cv_nat, \
             nc.sbuf_tensor("cv_bf", [128, 3 * D], BF16) as cv_bf:
            for r in range(DT):
                def _l(sy, r=r):
                    wait(sy, "dCV", ev.get("cv_out", 0))
                    ev["cv_in"] = inc(sy.dma_start(
                        cv_nat[:, :],
                        wconv[r * 128:(r + 1) * 128, :, :].rearrange("a b c -> a (b c)")), "dCV")
                bb.sync(_l)

                def _c(v, r=r):
                    wait(v, "dCV", ev["cv_in"])
                    for w in range(3):
                        v.tensor_copy(cv_bf[:, w * D:(w + 1) * D], cv_nat[:, w:3 * D:3])
                    ev["cv_bf"] = inc(v.engine_nop(), "cV")
                bb.vector(_c)

                def _s(sy, r=r):
                    wait(sy, "cV", ev["cv_bf"])
                    for w in range(3):
                        inc(sy.dma_start(n_cv[w][r * 128:(r + 1) * 128, :],
                                         cv_bf[:, w * D:(w + 1) * D]), "dCV")
                    ev["cv_out"] = cnt["dCV"]
                bb.sync(_s)
            ev["conv_w"] = cnt["dCV"]

        with nc.sbuf_tensor("svec_nat", [32, 128], F32) as svec_nat:
            def load_svec(src, n_tiles, dst_col):
                for t0 in range(0, n_tiles, 32):
                    tn = min(32, n_tiles - t0)

                    def _z(v, tn=tn):
                        wait(v, "do", ev.get("svec_in", 0))
                        wait(v, "cV", ev.get("svec_used", 0))
                        if tn < 32:
                            ev["svec_z"] = inc(v.memset(svec_nat[:, :], 0.0), "cV")
                        else:
                            ev["svec_z"] = inc(v.engine_nop(), "cV")
                    bb.vector(_z)

                    def _l(sy, t0=t0, tn=tn):
                        wait(sy, "cV", ev["svec_z"])
                        ev["svec_in"] = inc(sy.dma_start(
                            svec_nat[0:tn, :],
                            src[t0 * 128:(t0 + tn) * 128].rearrange("(a b) -> a b", b=128)), "do")
                    bb.sync(_l)

                    def _t(v, t0=t0, tn=tn, dst_col=dst_col):
                        wait(v, "do", ev["svec_in"])
                        for bi in range((tn + 31) // 32):
                            for pj in range(4):
                                v.transpose(
                                    scales[pj * 32:(pj + 1) * 32,
                                           dst_col + t0 + bi * 32:dst_col + t0 + bi * 32 + 32],
                                    svec_nat[bi * 32:bi * 32 + 32, pj * 32:(pj + 1) * 32])
                        ev["svec_used"] = inc(v.engine_nop(), "cV")
                    bb.vector(_t)
            load_svec(s1, DHT, S1C)
            load_svec(s2, DHT, S2C)
            load_svec(sh1, DHT, SH1C)
            load_svec(sh2, DHT, SH2C)
            load_svec(s3, DT, S3C)
            load_svec(bo, DT, BOC)
            ev["scales"] = cnt["cV"]

        def ln_stats(src_ap, slot, dep_list):
            c0 = 4 * slot

            def _v(v):
                for s, val in dep_list:
                    wait(v, s, val)
                v.reduce_sum(ln_st[:, c0:c0 + 1], src_ap, AX.X)
                inc(v.engine_nop(), "cV")
                wait(v, "cV", cnt["cV"])
                inc(v.tensor_scalar_mul(ln_st[:, c0:c0 + 1], ln_st[:, c0:c0 + 1], -1.0 / D), "cV")
                wait(v, "cV", cnt["cV"])
                ev[f"ln_sum{slot}"] = inc(
                    v.scalar_tensor_tensor(ln_waste[:, :], src_ap,
                                           ln_st[:, c0:c0 + 1], src_ap,
                                           op0=ALU.add, op1=ALU.mult,
                                           accum_out=ln_st[:, c0 + 1:c0 + 2]), "cV")
            bb.vector(_v)

            def _a(sc):
                wait(sc, "cV", ev[f"ln_sum{slot}"])
                wait(sc, "cV", ev["consts"])
                inc(sc.activation(ln_st[:, c0 + 2:c0 + 3], ln_st[:, c0 + 1:c0 + 2], AF.Ln,
                                  bias=consts[:, 0:1], scale=1.0 / D), "cA")
                wait(sc, "cA", cnt["cA"])
                ev[f"ln_rstd{slot}"] = inc(
                    sc.activation(ln_st[:, c0 + 2:c0 + 3], ln_st[:, c0 + 2:c0 + 3],
                                  AF.Exp, bias=0.0, scale=-0.5), "cA")
            bb.scalar(_a)
            return ("cA", ev[f"ln_rstd{slot}"])


        with nc.sbuf_tensor("lx01", [128, 2 * D], F32, side="right") as lx01, \
             nc.sbuf_tensor("lx23", [128, 2 * D], F32, side="right") as lx23, \
             nc.sbuf_tensor("lb01", [128, 2 * D], BF16, side="right") as lb01, \
             nc.sbuf_tensor("lb23", [128, 2 * D], BF16, side="right") as lb23:
            lxs = [lx01[:, 0:D], lx01[:, D:2 * D], lx23[:, 0:D], lx23[:, D:2 * D]]
            lbs = [lb01[:, 0:D], lb01[:, D:2 * D], lb23[:, 0:D], lb23[:, D:2 * D]]

            def ln1_front(src_slice, slot):
                lx = lxs[slot]

                def _l(sy):
                    wait(sy, "cA", ev.get(f"ln_used{slot}", 0))
                    wait(sy, "cV", ev.get(f"lnbf{slot}", 0))
                    ev[f"lnx{slot}"] = inc(sy.dma_start(lx, src_slice), f"dL{slot}")
                bb.sync(_l)
                ln_stats(lx, slot, [(f"dL{slot}", ev[f"lnx{slot}"])])

            def ln1_finish(dst_slice, slot):
                lx, lb = lxs[slot], lbs[slot]
                c0 = 4 * slot

                def _f(v):
                    wait(v, "cA", ev[f"ln_rstd{slot}"])
                    wait(v, f"dLS{slot}", ev.get(f"lnout{slot}", 0))
                    ev[f"lnbf{slot}"] = inc(
                        v.tensor_scalar(lb, lx,
                                        ln_st[:, c0:c0 + 1], ln_st[:, c0 + 2:c0 + 3],
                                        op0=ALU.add, op1=ALU.mult), "cV")
                    ev[f"ln_used{slot}"] = ev[f"ln_rstd{slot}"]
                bb.vector(_f)

                def _o(sy):
                    wait(sy, "cV", ev[f"lnbf{slot}"])
                    ev[f"lnout{slot}"] = inc(sy.dma_start(dst_slice, lb), f"dLS{slot}")
                bb.sync(_o)

            specs = [(x_full[t * 128:(t + 1) * 128, :], scr_xf[t * 128:(t + 1) * 128, :])
                     for t in range(T // 128)]
            specs += [(x_own[t * 128:(t + 1) * 128, :], scr_xo[t * 128:(t + 1) * 128, :])
                      for t in range(TOK // 128)]
            rstd_ev = {}
            for t, (srcs, dsts) in enumerate(specs):
                ln1_front(srcs, t % 4)
                rstd_ev[t] = ev[f"ln_rstd{t % 4}"]
                if t >= 2:
                    ev[f"ln_rstd{(t - 2) % 4}"] = rstd_ev[t - 2]
                    ln1_finish(specs[t - 2][1], (t - 2) % 4)
            for t in (len(specs) - 2, len(specs) - 1):
                ev[f"ln_rstd{t % 4}"] = rstd_ev[t]
                ln1_finish(specs[t][1], t % 4)
            for s_ in range(4):
                ev[f"ln1_{s_}"] = cnt[f"dLS{s_}"]
            ev["ln1_v"] = cnt["cV"]
            bb.flush()

        otm = None
        otm_t = lambda j: otm[:, j * D:(j + 1) * D]

        otm_es = contextlib.ExitStack()
        otm = otm_es.enter_context(nc.sbuf_tensor("otm", [128, NJ * D], BF16, side="left"))
        attn_es = contextlib.ExitStack()
        ktmp_es = contextlib.ExitStack()
        if True:
            ktmp = ktmp_es.enter_context(nc.sbuf_tensor("ktmp", [128, DT * TKP], BF16, side="right"))
            ktmp_t = lambda kd: ktmp[:, kd * TKP:(kd + 1) * TKP]

            with nc.sbuf_tensor("xwin", [128, DT * 1536], BF16, side="right") as xwin, \
                 nc.sbuf_tensor("wt_cv", [128, DT * D], BF16, side="right") as wt_cv, \
                 contextlib.ExitStack() as cps:
                ps_cv = [cps.enter_context(nc.psum_tensor(f"ps_cv{i}", [128, 512], F32))
                         for i in range(8)]
                xwin_t = lambda kd: xwin[:, kd * 1536:(kd + 1) * 1536]
                wt_cv_t = lambda kd: wt_cv[:, kd * D:(kd + 1) * D]

                def _ktmp_init(v):
                    wait(v, "cV", ev["ln1_v"])
                    for s_ in range(4):
                        wait(v, f"dLS{s_}", ev[f"ln1_{s_}"])
                    for kd in range(DT):
                        v.memset(ktmp_t(kd)[:, TK:TKP], 0.0)
                    ev["ktmp_init"] = inc(v.engine_nop(), "cV")
                bb.vector(_ktmp_init)

                conv_chunks = [(0, 512), (512, 512), (1024, 341)]
                for ci, (p0, pn) in enumerate(conv_chunks):
                    rows0 = 3 * p0
                    rows1 = min(T, rows0 + 1536)

                    def _xl(sy, ci=ci, rows0=rows0, rows1=rows1):
                        for s_ in range(4):
                            wait(sy, f"dLS{s_}", ev[f"ln1_{s_}"])
                        wait(sy, "cT", ev.get("conv_mm_c", 0))
                        wait(sy, "cV", ev.get("ktmp_c0", 0))
                        for kd in range(DT):
                            inc(sy.dma_start(xwin_t(kd)[:, 0:rows1 - rows0],
                                             scr_xf[rows0:rows1, kd * 128:(kd + 1) * 128],
                                             transpose=True), "dXW")
                        ev["xwin"] = cnt["dXW"]
                    bb.sync(_xl)
                    if ci == 0:
                        def _c0(v):
                            wait(v, "dXW", ev["xwin"])
                            for kd in range(DT):
                                v.tensor_copy(ktmp_t(kd)[:, 0:1], xwin_t(kd)[:, 0:1])
                            ev["ktmp_c0"] = inc(v.engine_nop(), "cV")
                        bb.vector(_c0)

                    for w in range(3):
                        def _wl(sy, w=w, ci=ci):
                            wait(sy, "dCV", ev["conv_w"])
                            wait(sy, "cT", ev.get("conv_mm_w", 0))
                            for kd in range(DT):
                                inc(sy.dma_start(wt_cv_t(kd), n_cv[w][:, kd * 128:(kd + 1) * 128],
                                                 transpose=True), "dWC")
                            ev["cv_wl"] = cnt["dWC"]
                        bb.sync(_wl)

                        def _mm(t, w=w, ci=ci, p0=p0, pn=pn):
                            wait(t, "dWC", ev["cv_wl"])
                            wait(t, "dXW", ev["xwin"])
                            if w == 0:
                                wait(t, "cV", ev.get("conv_ev_c", 0))
                            for ot in range(DT):
                                for kd in range(DT):
                                    ins = t.matmul(ps_cv[ot][:, 0:pn],
                                                   wt_cv_t(kd)[:, ot * 128:(ot + 1) * 128],
                                                   xwin_t(kd)[:, w:3 * (pn - 1) + w + 1:3],
                                                   start=(w == 0 and kd == 0),
                                                   stop=(w == 2 and kd == DT - 1))
                            ev["conv_mm_w"] = inc(ins, "cT")
                        bb.tensor(_mm)
                    ev["conv_mm_c"] = cnt["cT"]

                    def _ev(v, ci=ci, p0=p0, pn=pn):
                        wait(v, "cT", ev["conv_mm_c"])
                        wait(v, "cV", ev["ktmp_init"])
                        for ot in range(DT):
                            ins = v.tensor_copy(ktmp_t(ot)[:, 1 + p0:1 + p0 + pn],
                                                ps_cv[ot][:, 0:pn])
                        ev["conv_ev_c"] = inc(ins, "cV")
                    bb.vector(_ev)
                ev["ktmp"] = cnt["cV"]
                ev["conv_mm_all"] = cnt["cT"]

            if debug:
                bb.flush()
                _dump_fm(b, nc, blk, ev, ktmp_t, DT, TKP, dbg["ktmp"], dep=("cV", ev["ktmp"]))

            kfm = attn_es.enter_context(nc.sbuf_tensor("kfm", [128, DT * TKP], BF16, side="left"))
            vtm = attn_es.enter_context(nc.sbuf_tensor("vtm", [128, NKT * 16 * HW], BF16, side="left"))
            kfm_t = lambda kd: kfm[:, kd * TKP:(kd + 1) * TKP]
            vtm_t = lambda kt: vtm[:, kt * 16 * HW:(kt + 1) * 16 * HW]

            with nc.sbuf_tensor("wt_kv", [128, DT * D], BF16, side="right") as wt_kv, \
                 nc.psum_tensor("ps_q0", [128, 512], F32) as ps_q0, \
                 nc.psum_tensor("ps_q1", [128, 512], F32) as ps_q1:
                ps_proj = [ps_q0, ps_q1]
                wt_kv_t = lambda kd: wt_kv[:, kd * D:(kd + 1) * D]

                def _wk(sy):
                    wait(sy, "dgA", ev["dgA"])
                    wait(sy, "cT", ev["conv_mm_all"])
                    for kd in range(DT):
                        inc(sy.dma_start(wt_kv_t(kd), n_wk[:, kd * 128:(kd + 1) * 128],
                                         transpose=True), "dWK")
                    ev["wkT"] = cnt["dWK"]
                bb.sync(_wk)

                pp = 0
                kproj_chunks = [(0, 512), (512, 512), (1024, 384)]
                for ot in range(DT):
                    for (p0, pn) in kproj_chunks:
                        ps = ps_proj[pp % 2]

                        def _mm(t, ot=ot, p0=p0, pn=pn, ps=ps, pp=pp):
                            wait(t, "dWK", ev["wkT"])
                            wait(t, "cV", ev["ktmp"])
                            wait(t, "cV", ev.get(f"kp_ev{pp % 2}", 0))
                            for kd in range(DT):
                                ins = t.matmul(ps[:, 0:pn],
                                               wt_kv_t(kd)[:, ot * 128:(ot + 1) * 128],
                                               ktmp_t(kd)[:, p0:p0 + pn],
                                               start=(kd == 0), stop=(kd == DT - 1))
                            ev["kp_mm"] = inc(ins, "cT")
                        bb.tensor(_mm)

                        def _ev(v, ot=ot, p0=p0, pn=pn, ps=ps, pp=pp):
                            wait(v, "cT", ev["kp_mm"])
                            ev[f"kp_ev{pp % 2}"] = inc(
                                v.tensor_copy(kfm_t(ot)[:, p0:p0 + pn], ps[:, 0:pn]), "cV")
                        bb.vector(_ev)
                        pp += 1
                ev["kfm"] = cnt["cV"]
                ev["kp_mm_all"] = cnt["cT"]

                def _wv(sy):
                    wait(sy, "cT", ev["kp_mm_all"])
                    for kd in range(DT):
                        inc(sy.dma_start(wt_kv_t(kd), n_wv[:, kd * 128:(kd + 1) * 128],
                                         transpose=True), "dWK")
                    ev["wvT"] = cnt["dWK"]
                bb.sync(_wv)

                def _vones(v):
                    for kt in range(NKT):
                        v.memset(vtm_t(kt)[:, 64:16 * HW:HW], 1.0)
                    ev["vones"] = inc(v.engine_nop(), "cV")
                bb.vector(_vones)

                for kt in range(NKT):
                    for half in range(2):
                        ps = ps_proj[pp % 2]

                        def _mm(t, kt=kt, half=half, ps=ps, pp=pp):
                            wait(t, "dWK", ev["wvT"])
                            wait(t, "cV", ev.get(f"vp_ev{pp % 2}", ev["kfm"]))
                            for kd in range(DT):
                                ins = t.matmul(ps[:, 0:512],
                                               ktmp_t(kd)[:, kt * 128:(kt + 1) * 128],
                                               wt_kv_t(kd)[:, half * 512:(half + 1) * 512],
                                               start=(kd == 0), stop=(kd == DT - 1))
                            ev["vp_mm"] = inc(ins, "cT")
                        bb.tensor(_mm)

                        def _ev(v, kt=kt, half=half, ps=ps, pp=pp):
                            wait(v, "cT", ev["vp_mm"])
                            wait(v, "cV", ev["vones"])
                            dst = vtm_t(kt)[:, half * 8 * HW:(half + 1) * 8 * HW]
                            ev[f"vp_ev{pp % 2}"] = inc(v.tensor_copy(
                                dst.rearrange("p (h c) -> p h c", c=HW)[:, :, 0:64],
                                ps[:, 0:512].rearrange("p (h c) -> p h c", c=64)), "cV")
                        bb.vector(_ev)
                        pp += 1
                ev["vtm"] = cnt["cV"]
                ev["vp_mm_all"] = cnt["cT"]

            ktmp_es.close()
            qfm = attn_es.enter_context(nc.sbuf_tensor("qfm", [128, DT * TOK], BF16, side="left"))
            qfm_t = lambda kd: qfm[:, kd * TOK:(kd + 1) * TOK]

            with nc.sbuf_tensor("xowin", [128, DT * 512], BF16, side="right") as xowin, \
                 nc.sbuf_tensor("wt_q", [128, DT * D], BF16, side="right") as wt_q, \
                 nc.psum_tensor("ps_qa", [128, 512], F32) as ps_qa, \
                 nc.psum_tensor("ps_qb", [128, 512], F32) as ps_qb:
                ps_proj = [ps_qa, ps_qb]
                xowin_t = lambda kd: xowin[:, kd * 512:(kd + 1) * 512]
                wt_q_t = lambda kd: wt_q[:, kd * D:(kd + 1) * D]

                def _wq(sy):
                    wait(sy, "dgA", ev["dgA"])
                    wait(sy, "cT", ev["vp_mm_all"])
                    for kd in range(DT):
                        inc(sy.dma_start(wt_q_t(kd), n_wq[:, kd * 128:(kd + 1) * 128],
                                         transpose=True), "dWK")
                    ev["wqT"] = cnt["dWK"]
                bb.sync(_wq)

                pp = 0
                for nch in range(TOK // 512):
                    def _xl(sy, nch=nch):
                        for s_ in range(4):
                            wait(sy, f"dLS{s_}", ev[f"ln1_{s_}"])
                        wait(sy, "cT", ev.get("qp_mm_n", 0))
                        for kd in range(DT):
                            inc(sy.dma_start(xowin_t(kd),
                                             scr_xo[nch * 512:(nch + 1) * 512,
                                                    kd * 128:(kd + 1) * 128],
                                             transpose=True), "dXO")
                        ev["xowin"] = cnt["dXO"]
                    bb.sync(_xl)

                    for ot in range(DT):
                        ps = ps_proj[pp % 2]

                        def _mm(t, ot=ot, nch=nch, ps=ps, pp=pp):
                            wait(t, "dWK", ev["wqT"])
                            wait(t, "dXO", ev["xowin"])
                            if pp < 2:
                                wait(t, "cV", ev["vtm"])
                            wait(t, "cA", ev.get(f"qp_ev{pp % 2}", 0))
                            for kd in range(DT):
                                ins = t.matmul(ps[:, 0:512],
                                               wt_q_t(kd)[:, ot * 128:(ot + 1) * 128],
                                               xowin_t(kd),
                                               start=(kd == 0), stop=(kd == DT - 1))
                            ev["qp_mm"] = inc(ins, "cT")
                        bb.tensor(_mm)

                        def _ev(sc, ot=ot, nch=nch, ps=ps, pp=pp):
                            wait(sc, "cT", ev["qp_mm"])
                            ev[f"qp_ev{pp % 2}"] = inc(
                                sc.activation(qfm_t(ot)[:, nch * 512:(nch + 1) * 512],
                                              ps[:, 0:512], AF.Copy, bias=0.0, scale=0.125), "cA")
                        bb.scalar(_ev)
                        pp += 1
                    ev["qp_mm_n"] = cnt["cT"]
                ev["qfm"] = cnt["cA"]

            bb.flush()
            if debug:
                _dump_fm(b, nc, blk, ev, kfm_t, DT, TKP, dbg["kfm"], dep=("cV", ev["kfm"]))
                _dump_fm(b, nc, blk, ev, vtm_t, NKT, 16 * HW, dbg["vtm"], dep=("cV", ev["vtm"]))
                _dump_fm(b, nc, blk, ev, qfm_t, DT, TOK, dbg["qfm"], dep=("cA", ev["qfm"]))

        p2es = contextlib.ExitStack()
        p2b0 = p2es.enter_context(nc.sbuf_tensor("p2b0", [128, DH], BF16, side="right"))
        p2b = [p2b0, p2b0]
        cast_q = {"h1": [(n_h1, h1, r0, "dgH1") for r0 in range(0, DH, 256)],
                  "h2": [(n_h2, h2, r0, "dgH2") for r0 in range(0, DH, 256)],
                  "w3": [(n_w3, w3, r0, "dgH2") for r0 in range(0, D, 256)]}
        cast_order = ["h1", "h2", "w3"]

        def emit_casts(n):
            def _g(g, items):
                for dst_, src_, r0_, sem_ in items:
                    inc(g.dma_start(dst_[r0_:r0_ + 256, :], src_[r0_:r0_ + 256, :]), sem_)
            items = []
            while n > 0 and cast_order:
                qn = cast_order[0]
                if not cast_q[qn]:
                    cast_order.pop(0)
                    continue
                items.append(cast_q[qn].pop(0))
                n -= 1
            if items:
                bb.gpsimd(lambda g, items=tuple(items): _g(g, items))
            ev["dgF"], ev["dgH1"], ev["dgH2"] = cnt["dgF"], cnt["dgH1"], cnt["dgH2"]

        def drain_casts_for(w):
            names = {"t_h1": "h1", "t_h2": "h2", "t_w3": "w3"}
            qn = names.get(w)
            while qn and cast_q[qn]:
                emit_casts(8)
        p2_steps = []
        for nat_, tdst_, I_, O_, gate_, mark_ in [
                (n_w1, t_w1, D, DH, "dgF", None),
                (n_w2, t_w2, D, DH, "dgF", "p2_ab"),
                (n_h1, t_h1, DH, DH, "dgH1", "p2_h1"),
                (n_h2, t_h2, DH, DH, "dgH2", "p2_h2"),
                (n_w3, t_w3, DH, D, "dgH2", "p2_w3")]:
            nkd = I_ // 128
            for kd_ in range(nkd):
                p2_steps.append((nat_, tdst_, O_, gate_, kd_,
                                 mark_ if kd_ == nkd - 1 else None))
        p2_i = [0]

        def emit_p2(nsteps):
            while nsteps > 0 and p2_i[0] < len(p2_steps):
                nat_, tdst_, O_, gate_, kd_, mark_ = p2_steps[p2_i[0]]
                if kd_ == 0:
                    drain_casts_for(tdst_.tensor.name)
                par = p2_i[0] % 2

                def _t(sy, nat_=nat_, O_=O_, gate_=gate_, kd_=kd_, par=par, i=p2_i[0]):
                    if i < 2:
                        wait(sy, "cA", ev["qfm"])
                    wait(sy, gate_, ev[gate_])
                    wait(sy, "dPW0", ev.get("p2w0", 0))
                    wait(sy, "dPW1", ev.get("p2w1", 0))
                    ev[f"p2t{par}"] = inc(sy.dma_start(
                        p2b[par][:, 0:O_], nat_[0:O_, kd_ * 128:(kd_ + 1) * 128],
                        transpose=True), f"dPT{par}")
                bb.sync(_t)

                def _w(g, tdst_=tdst_, O_=O_, kd_=kd_, par=par):
                    wait(g, f"dPT{par}", ev[f"p2t{par}"])
                    ev[f"p2w{par}"] = inc(g.dma_start(
                        tdst_[kd_ * 128:(kd_ + 1) * 128, 0:O_], p2b[par][:, 0:O_]), f"dPW{par}")
                bb.gpsimd(_w)
                if mark_:
                    ev[mark_] = (cnt["dPW0"], cnt["dPW1"])
                p2_i[0] += 1
                nsteps -= 1

        def _mk_pair(jp):
            return list(range(_klo(2 * jp) // 128, _ktiles(2 * jp + 1)))

        MKP = max(len(_mk_pair(jp)) for jp in range(NJ // 2))
        NMK = sum(len(_mk_pair(jp)) for jp in range(NJ // 2))
        a2 = contextlib.ExitStack()
        with a2:
            pt_sb = a2.enter_context(nc.sbuf_tensor("pt_sb", [128, 2 * 6 * 256], BF16, side="right"))
            mk_sb = a2.enter_context(nc.sbuf_tensor("mk_sb", [128, NMK * 256], BF16, side="right"))
            ps_scA = a2.enter_context(nc.psum_tensor("ps_scA", [128, 3 * 512], F32))
            ps_scB = a2.enter_context(nc.psum_tensor("ps_scB", [128, 3 * 512], F32))
            ps_oA = a2.enter_context(nc.psum_tensor("ps_oA", [128, 512], F32))
            ps_oB = a2.enter_context(nc.psum_tensor("ps_oB", [128, 512], F32))
            ps_sc = [ps_scA, ps_scB]
            ps_o = [ps_oA, ps_oB]
            mk_base = {}
            _mb = 0
            for jp_ in range(NJ // 2):
                mk_base[jp_] = _mb
                _mb += len(_mk_pair(jp_))
            mk_slot = lambda jp, i: mk_sb[:, (mk_base[jp] + i) * 256:
                                          (mk_base[jp] + i + 1) * 256]

            def _mload_all(g):
                wait(g, "cA", ev["qfm"])
                for jp_ in range(NJ // 2):
                    for i_, kt_ in enumerate(_mk_pair(jp_)):
                        inc(g.dma_start(mk_slot(jp_, i_),
                                        maskT[kt_ * 128:(kt_ + 1) * 128,
                                              2 * jp_ * 128:(2 * jp_ + 2) * 128]), "dMK0")
            bb.gpsimd(_mload_all)
            ev["mload_all"] = cnt["dMK0"]

            gnum = 0
            itc = 0
            for jp in range(NJ // 2):
                mk = _mk_pair(jp)
                j0 = 2 * jp

                emit_casts(7)
                emit_p2(16)

                nkt = _ktiles(2 * jp + 1)
                for h in range(H):
                    hq, hp = h // 2, (h % 2) * 64
                    groups = [list(range(nkt))[i:i + 6] for i in range(0, nkt, 6)]
                    po = ps_o[itc % 2]
                    for gi, g in enumerate(groups):
                        psc = ps_sc[gnum % 2]
                        slot = gnum % 2

                        def _sc_mm(t, jp=jp, h=h, g=tuple(g), psc=psc, hq=hq, hp=hp,
                                   gnum=gnum, j0=j0):
                            if gnum < 2:
                                wait(t, "cA", ev["qfm"])
                                wait(t, "cV", ev["kfm"])
                                wait(t, "cV", ev["vtm"])
                            wait(t, "cA", ev.get(f"exp_g{gnum - 2}", 0))
                            q_sl = qfm_t(hq)[hp:hp + 64, j0 * 128:(j0 + 2) * 128]
                            for si, kt in enumerate(g):
                                last = (si == len(g) - 1)
                                ins = t.matmul(psc[:, si * 256:si * 256 + 256],
                                               kfm_t(hq)[hp:hp + 64, kt * 128:(kt + 1) * 128],
                                               q_sl,
                                               start=(si % 2 == 0),
                                               stop=(si % 2 == 1) or last)
                            ev[f"scmm_g{gnum}"] = inc(ins, "cT")
                        bb.tensor(_sc_mm)

                        has_mask = any(kt in mk for kt in g)
                        if has_mask:
                            def _mask(v, jp=jp, g=tuple(g), psc=psc, mk=tuple(mk), gnum=gnum):
                                wait(v, "cT", ev[f"scmm_g{gnum}"])
                                wait(v, "dMK0", ev["mload_all"])
                                for si, kt in enumerate(g):
                                    if kt in mk:
                                        mi = list(mk).index(kt)
                                        ins = v.tensor_tensor(psc[:, si * 256:si * 256 + 256],
                                                              psc[:, si * 256:si * 256 + 256],
                                                              mk_slot(jp, mi), ALU.add)
                                ev[f"mask_g{gnum}"] = inc(ins, "cV")
                            bb.vector(_mask)

                        def _exp(sc, g=tuple(g), psc=psc, slot=slot, gnum=gnum, jp=jp,
                                 has_mask=has_mask):
                            if has_mask:
                                wait(sc, "cV", ev[f"mask_g{gnum}"])
                            else:
                                wait(sc, "cT", ev[f"scmm_g{gnum}"])
                            wait(sc, "cT", ev.get(f"pv_g{gnum - 2}", 0))
                            ng = len(g)
                            ins = sc.activation(
                                pt_sb[:, slot * 1536:slot * 1536 + ng * 256],
                                psc[:, 0:ng * 256],
                                AF.Exp, bias=consts[:, 1:2], scale=1.0)
                            ev[f"exp_g{gnum}"] = inc(ins, "cA")
                            ev[f"mask_used{jp}"] = ev[f"exp_g{gnum}"]
                        bb.scalar(_exp)

                        def _pv(t, g=tuple(g), po=po, slot=slot, gnum=gnum, h=h,
                                gi=gi, ngroups=len(groups), itc=itc):
                            wait(t, "cA", ev[f"exp_g{gnum}"])
                            if gi == 0 and itc >= 2:
                                wait(t, "cV", ev.get(f"oevac{itc - 2}", 0))
                            for si, kt in enumerate(g):
                                first = (gi == 0 and si == 0)
                                lastg = (gi == ngroups - 1 and si == len(g) - 1)
                                t.matmul(po[:, 0:65],
                                         pt_sb[:, slot * 1536 + si * 256:
                                               slot * 1536 + si * 256 + 128],
                                         vtm_t(kt)[:, h * HW:(h + 1) * HW],
                                         start=first, stop=False)
                                ins = t.matmul(po[:, 65:130],
                                               pt_sb[:, slot * 1536 + si * 256 + 128:
                                                     slot * 1536 + si * 256 + 256],
                                               vtm_t(kt)[:, h * HW:(h + 1) * HW],
                                               start=False, stop=lastg)
                            ev[f"pv_g{gnum}"] = inc(ins, "cT")
                        bb.tensor(_pv)
                        gnum += 1

                    def _oev(v, jp=jp, h=h, po=po, itc=itc, j0=j0):
                        wait(v, "cT", ev[f"pv_g{gnum - 1}"])
                        inc(v.reciprocal(po[:, 64:130:65], po[:, 64:130:65]), "cV")
                        wait(v, "cV", cnt["cV"])
                        v.tensor_scalar(otm_t(j0)[:, h * 64:(h + 1) * 64],
                                        po[:, 0:64], po[:, 64:65], None, op0=ALU.mult)
                        ins = v.tensor_scalar(otm_t(j0 + 1)[:, h * 64:(h + 1) * 64],
                                              po[:, 65:129], po[:, 129:130], None, op0=ALU.mult)
                        ev[f"oevac{itc}"] = inc(ins, "cV")
                    bb.vector(_oev)
                    itc += 1

                def _so(sy, jp=jp, val=cnt["cV"]):
                    wait(sy, "cV", val)
                    for j in (2 * jp, 2 * jp + 1):
                        inc(sy.dma_start(scr_o[j * 128:(j + 1) * 128, :], otm_t(j)), "dSO")
                bb.sync(_so)
            emit_casts(10 ** 6)
            emit_p2(len(p2_steps))
            ev["otm"] = cnt["cV"]
            ev["scr_o"] = cnt["dSO"]
            ev["attn_T"] = cnt["cT"]
            ev["attn_A"] = cnt["cA"]
            bb.flush()
        attn_es.close()

        otm_es.close()

        hslab = stack.enter_context(nc.sbuf_tensor("hslab", [128, DHT * TC], BF16, side="left"))
        cslab = stack.enter_context(nc.sbuf_tensor("cslab", [128, DHT * TC], BF16, side="left"))
        h_t = lambda m: hslab[:, m * TC:(m + 1) * TC]
        c_t = lambda m: cslab[:, m * TC:(m + 1) * TC]

        fb = contextlib.ExitStack()
        with fb:
            ln_tmp = fb.enter_context(nc.sbuf_tensor("ln_tmp", [128, D], F32, side="right"))
            x2fm = fb.enter_context(nc.sbuf_tensor("x2fm", [128, DT * TC], BF16, side="right"))
            wbuf = fb.enter_context(nc.sbuf_tensor("wbuf", [128, 2 * 32 * 256], BF16, side="right"))
            silu_sb = fb.enter_context(nc.sbuf_tensor("silu_sb", [128, 2 * 512], F32, side="right"))
            ff_sb = fb.enter_context(nc.sbuf_tensor("ff_sb", [128, 512], BF16, side="right"))
            ff_tm = fb.enter_context(nc.sbuf_tensor("ff_tm", [128, 4 * D], BF16, side="right"))
            y_sb = fb.enter_context(nc.sbuf_tensor("y_sb", [128, 2 * D], F32, side="right"))
            wt_o = fb.enter_context(nc.sbuf_tensor("wt_o", [128, DT * D], BF16, side="right"))
            ofm = fb.enter_context(nc.sbuf_tensor("ofm", [128, DT * 512], BF16, side="right"))
            at_sb = fb.enter_context(nc.sbuf_tensor("at_sb", [128, 2 * 512], BF16, side="right"))
            at_tm = fb.enter_context(nc.sbuf_tensor("at_tm", [128, 4 * D], BF16, side="right"))
            lb2 = fb.enter_context(nc.sbuf_tensor("lb2", [128, 2 * D], BF16, side="right"))
            xpg = fb.enter_context(nc.sbuf_tensor("xpg", [128, 4 * D], F32, side="right"))
            xpl = fb.enter_context(nc.sbuf_tensor("xpl", [128, 2 * D], F32, side="right"))
            ps_f = [fb.enter_context(nc.psum_tensor(f"ps_f{i}", [128, 512], F32))
                    for i in range(8)]
            ps_w = [ps_f[4], ps_f[5]]
            x2fm_t = lambda kd: x2fm[:, kd * TC:(kd + 1) * TC]
            wt_o_t = lambda kd: wt_o[:, kd * D:(kd + 1) * D]
            ofm_t = lambda kd: ofm[:, kd * 512:(kd + 1) * 512]
            xpg_t = lambda j: xpg[:, (j % 4) * D:(j % 4 + 1) * D]

            def wbuf_t(par, i):
                return wbuf[:, (par * 32 + i) * 256:(par * 32 + i + 1) * 256]

            def woT_load():
                def _w(sy):
                    wait(sy, "dgA", ev["dgA"])
                    wait(sy, "cT", ev["attn_T"])
                    wait(sy, "cV", ev["otm"])
                    for kd in range(DT):
                        inc(sy.dma_start(wt_o_t(kd), n_wo[:, kd * 128:(kd + 1) * 128],
                                         transpose=True), "dWO")
                    ev["woT"] = cnt["dWO"]
                bb.sync(_w)

            def a3_ofl(g):
                def _ofl(sy, g=g):
                    if g == 0:
                        wait(sy, "cV", ev["consts"])
                        wait(sy, "cT", ev["attn_T"])
                    wait(sy, "cT", ev.get(f"wo_mm_g{g - 1}", 0))
                    wait(sy, "dSO", ev["scr_o"])
                    for kd in range(DT):
                        inc(sy.dma_start(ofm_t(kd),
                                         scr_o[g * 512:(g + 1) * 512, kd * 128:(kd + 1) * 128],
                                         transpose=True), "dOF0")
                    ev[f"ofl{g}"] = cnt["dOF0"]
                bb.sync(_ofl)

            def a3_wo_m(g, m):
                def _mm(t, g=g, m=m):
                    if g == 0 and m == 0:
                        wait(t, "cV", ev["otm"])
                        wait(t, "cA", ev["attn_A"])
                    wait(t, "dOF0", ev[f"ofl{g}"])
                    wait(t, "dWO", ev["woT"])
                    wait(t, "cA", ev.get(f"at_ev{m % 2}", 0))
                    for kd in range(DT):
                        ins = t.matmul(ps_w[m % 2][:, 0:512],
                                       wt_o_t(kd)[:, m * 128:(m + 1) * 128],
                                       ofm_t(kd),
                                       start=(kd == 0), stop=(kd == DT - 1))
                    ev["at_mm"] = inc(ins, "cT")
                    if m == DT - 1:
                        ev[f"wo_mm_g{g}"] = ev["at_mm"]
                bb.tensor(_mm)

                def _ev(sc, g=g, m=m):
                    wait(sc, "cT", ev["at_mm"])
                    wait(sc, f"dAT{m % 2}", ev.get(f"at_w{m % 2}", 0))
                    wait(sc, "cV", ev["scales"])
                    ev["at_act"] = inc(
                        sc.activation(at_sb[:, (m % 2) * 512:(m % 2) * 512 + 512],
                                      ps_w[m % 2][:, 0:512], AF.Identity,
                                      bias=scales[:, BOC + m:BOC + m + 1], scale=1.0), "cA")
                    ev[f"at_ev{m % 2}"] = ev["at_act"]
                bb.scalar(_ev)

                def _st(sy, g=g, m=m):
                    wait(sy, "cA", ev["at_act"])
                    ev[f"at_w{m % 2}"] = inc(sy.dma_start(
                        scr_at[m * 128:(m + 1) * 128, g * 512:(g + 1) * 512],
                        at_sb[:, (m % 2) * 512:(m % 2) * 512 + 512]), f"dAT{m % 2}")
                    if m == DT - 1:
                        ev[f"at_grp{g}"] = (cnt["dAT0"], cnt["dAT1"])
                bb.gpsimd(_st)

            def a3_front(j):
                slot = j % 4
                g = j // 4

                def _l(sy, j=j, slot=slot, g=g):
                    wait(sy, "dAT0", ev[f"at_grp{g}"][0])
                    wait(sy, "dAT1", ev[f"at_grp{g}"][1])
                    wait(sy, "cV", ev.get(f"lb2{j - 4}", 0))
                    wait(sy, f"dXP{j % 4}", ev.get(f"xpw{j - 4}", 0))
                    inc(sy.dma_start(at_tm[:, slot * D:(slot + 1) * D],
                                     scr_at[:, j * 128:(j + 1) * 128], transpose=True),
                        f"dTM{slot}")
                    ev[f"resx{j}"] = inc(sy.dma_start(xpg_t(j), x_own[j * 128:(j + 1) * 128, :]),
                                         f"dTM{slot}")
                bb.sync(_l)
                ln_stats(at_tm[:, slot * D:(slot + 1) * D], slot,
                         [(f"dTM{slot}", ev[f"resx{j}"])])
                ev[f"a3rstd{j}"] = ev[f"ln_rstd{slot}"]

            def a3_finish(j):
                slot = j % 4
                c0 = 4 * slot

                def _f(v, j=j, slot=slot, c0=c0, vprev=cnt["cV"]):
                    wait(v, "cA", ev[f"a3rstd{j}"])
                    wait(v, f"dTM{slot}", ev[f"resx{j}"])
                    wait(v, "cV", vprev)
                    inc(v.tensor_scalar(ln_tmp[:, :], at_tm[:, slot * D:(slot + 1) * D],
                                        ln_st[:, c0:c0 + 1], ln_st[:, c0 + 2:c0 + 3],
                                        op0=ALU.add, op1=ALU.mult), "cV")
                    wait(v, "cV", cnt["cV"])
                    ins = v.tensor_tensor(xpg_t(j), xpg_t(j), ln_tmp[:, :], ALU.add)
                    ev[f"xp{j}"] = inc(ins, "cV")
                bb.vector(_f)

                def _sp(sy, j=j):
                    wait(sy, "cV", ev[f"xp{j}"])
                    ev[f"xpw{j}"] = inc(sy.dma_start(scr_xp[j * 128:(j + 1) * 128, :],
                                                     xpg_t(j)), f"dXP{j % 4}")
                bb.gpsimd(_sp)

            def ln2_front(j):
                slot = 4 + j % 4
                ln_stats(xpg_t(j), slot, [("cV", ev[f"xp{j}"])])
                ev[f"l2rstd{j}"] = ev[f"ln_rstd{slot}"]

            def ln2_finish(j):
                slot = 4 + j % 4
                b2 = j % 2
                c0 = 4 * slot

                def _f(v, j=j, slot=slot, c0=c0, b2=b2):
                    wait(v, "cA", ev[f"l2rstd{j}"])
                    wait(v, f"dX2S{b2}", ev.get(f"x2w{j - 2}", 0))
                    ins = v.tensor_scalar(lb2[:, b2 * D:(b2 + 1) * D], xpg_t(j),
                                          ln_st[:, c0:c0 + 1], ln_st[:, c0 + 2:c0 + 3],
                                          op0=ALU.add, op1=ALU.mult)
                    ev[f"lb2{j}"] = inc(ins, "cV")
                bb.vector(_f)

                def _w(sy, j=j, b2=b2):
                    wait(sy, "cV", ev[f"lb2{j}"])
                    ev[f"x2w{j}"] = inc(sy.dma_start(
                        scr_x2[j * 128:(j + 1) * 128, :], lb2[:, b2 * D:(b2 + 1) * D]), f"dX2S{b2}")
                bb.gpsimd(_w)

            a3_pend = []

            def pump(n):
                while n > 0 and a3_pend:
                    f_ = a3_pend.pop(0)
                    if f_ is not None:
                        f_()
                    n -= 1

            def a3_group_bundles(g):
                js = [4 * g + i_ for i_ in range(4)]
                out = []
                if g == 0:
                    out.append(woT_load)
                out.append(lambda g=g: a3_ofl(g))
                out.extend([None] * 6)
                for m in range(DT):
                    out.append(lambda g=g, m=m: a3_wo_m(g, m))
                out.extend([None] * 2)
                out.append(lambda j=js[0]: a3_front(j))
                out.append(lambda j=js[1]: a3_front(j))
                out.append(lambda j=js[0]: (a3_finish(j), ln2_front(j)))
                out.append(lambda j=js[2]: a3_front(j))
                out.append(lambda j=js[1]: (a3_finish(j), ln2_front(j)))
                out.append(lambda j=js[3]: a3_front(j))
                out.append(lambda j=js[2]: (a3_finish(j), ln2_front(j)))
                out.append(lambda j=js[0]: ln2_finish(j))
                out.append(lambda j=js[3]: (a3_finish(j), ln2_front(j)))
                out.append(lambda j=js[1]: ln2_finish(j))
                out.append(lambda j=js[2]: ln2_finish(j))
                out.append(lambda j=js[3]: ln2_finish(j))
                return out

            for f_ in a3_group_bundles(0):
                if f_ is not None:
                    f_()

            for C in range(NCH):
                pump(10 ** 6)
                if C + 1 < NCH:
                    a3_pend.extend(a3_group_bundles(C + 1))

                def _x2l(sy, C=C):
                    wait(sy, "dX2S0", ev[f"x2w{4 * C + 2}"])
                    wait(sy, "dX2S1", ev[f"x2w{4 * C + 3}"])
                    if C > 0:
                        wait(sy, "cT", ev.get(f"ab_mm_all{C - 1}", 0))
                    for kd in range(DT):
                        inc(sy.dma_start(x2fm_t(kd),
                                         scr_x2[C * 512:(C + 1) * 512, kd * 128:(kd + 1) * 128],
                                         transpose=True), "dX2")
                    ev["x2fm"] = cnt["dX2"]
                bb.sync(_x2l)

                prev_T_ab = cnt["cT"]
                for mb in range(DH // 256):
                    par = mb % 2

                    def _wl(sy, mb=mb, par=par, C=C, prev_T_ab=prev_T_ab):
                        if C == 0 and mb == 0:
                            wait(sy, "dPW0", ev["p2_ab"][0])
                            wait(sy, "dPW1", ev["p2_ab"][1])
                        wait(sy, "cT", max(ev.get(f"ab_mmb{mb - 2}", 0) if mb >= 2 else 0, prev_T_ab))
                        d1 = wbuf[:, (par * 32) * 256:(par * 32 + DT) * 256]
                        d2 = wbuf[:, (par * 32 + 8) * 256:(par * 32 + 8 + DT) * 256]
                        inc(sy.dma_start(
                            d1.rearrange("p (a c) -> p a c", c=256),
                            t_w1.rearrange("(a p) c -> p a c", p=128)[:, :, mb * 256:(mb + 1) * 256]),
                            f"dW{par}")
                        inc(sy.dma_start(
                            d2.rearrange("p (a c) -> p a c", c=256),
                            t_w2.rearrange("(a p) c -> p a c", p=128)[:, :, mb * 256:(mb + 1) * 256]),
                            f"dW{par}")
                        ev[f"wl_ab{mb}"] = cnt[f"dW{par}"]
                    bb.sync(_wl)

                    for mt in range(2):
                        m = 2 * mb + mt

                        def _mm(t, mb=mb, par=par, mt=mt, m=m, C=C):
                            wait(t, f"dW{par}", ev[f"wl_ab{mb}"])
                            wait(t, "dX2", ev["x2fm"])
                            if C > 0 and m < 2:
                                wait(t, "cA", ev["o3_done"])
                            wait(t, "cV", ev.get(f"h_ev{m - 2}", 0) if m >= 2 else 0)
                            pa, pb_ = ps_f[2 * mt], ps_f[2 * mt + 1]
                            for kd in range(DT):
                                t.matmul(pa[:, 0:TC],
                                         wbuf_t(par, kd)[:, mt * 128:(mt + 1) * 128],
                                         x2fm_t(kd),
                                         start=(kd == 0), stop=(kd == DT - 1))
                            for kd in range(DT):
                                ins = t.matmul(pb_[:, 0:TC],
                                               wbuf_t(par, 8 + kd)[:, mt * 128:(mt + 1) * 128],
                                               x2fm_t(kd),
                                               start=(kd == 0), stop=(kd == DT - 1))
                            ev[f"ab_mm{m}"] = inc(ins, "cT")
                            if mt == 1:
                                ev[f"ab_mmb{mb}"] = ev[f"ab_mm{m}"]
                        bb.tensor(_mm)

                        def _silu(sc, mt=mt, m=m):
                            wait(sc, "cT", ev[f"ab_mm{m}"])
                            wait(sc, "cV", ev.get(f"h_ev{m - 2}", 0) if m >= 2 else 0)
                            ev[f"silu{mt}"] = inc(sc.activation(
                                silu_sb[:, mt * 512:mt * 512 + TC], ps_f[2 * mt][:, 0:TC],
                                AF.Silu, bias=0.0,
                                scale=scales[:, S1C + m:S1C + m + 1]), "cA")
                        bb.scalar(_silu)

                        def _h(v, mt=mt, m=m):
                            wait(v, "cA", ev[f"silu{mt}"])
                            ins = v.scalar_tensor_tensor(
                                h_t(m), ps_f[2 * mt + 1][:, 0:TC],
                                scales[:, S2C + m:S2C + m + 1],
                                silu_sb[:, mt * 512:mt * 512 + TC],
                                op0=ALU.mult, op1=ALU.mult)
                            ev[f"h_ev{m}"] = inc(ins, "cV")
                        bb.vector(_h)
                    pump(1)
                ev[f"ab_mm_all{C}"] = cnt["cT"]
                ev["h_all"] = cnt["cV"]

                def _hid_pass(nsrc, src_t, dst_t, scol, dst_name, dep, p2gate, C=C):
                    prev_T = cnt["cT"]
                    for mb in range(DH // 256):
                        par = mb % 2
                        pb = ps_f[2 * par:2 * par + 2]

                        def _wl(sy, mb=mb, par=par):
                            if C == 0 and mb == 0:
                                wait(sy, "dPW0", ev[p2gate][0])
                                wait(sy, "dPW1", ev[p2gate][1])
                            wait(sy, "cT", max(ev.get(f"{dst_name}_mm{mb - 2}", 0) if mb >= 2 else 0, prev_T))
                            dall = wbuf[:, (par * 32) * 256:(par * 32 + 16) * 256]
                            dall2 = wbuf[:, (par * 32 + 16) * 256:(par * 32 + 32) * 256]
                            tv = nsrc.rearrange("(a p) c -> p a c", p=128)
                            inc(sy.dma_start(
                                dall.rearrange("p (a c) -> p a c", c=256),
                                tv[:, 0:16, mb * 256:(mb + 1) * 256]), f"dW{par}")
                            inc(sy.dma_start(
                                dall2.rearrange("p (a c) -> p a c", c=256),
                                tv[:, 16:32, mb * 256:(mb + 1) * 256]), f"dW{par}")
                            ev[f"{dst_name}_wl{mb}"] = cnt[f"dW{par}"]
                        bb.sync(_wl)

                        def _mm(t, mb=mb, par=par, pb=pb):
                            wait(t, f"dW{par}", ev[f"{dst_name}_wl{mb}"])
                            if mb == 0:
                                wait(t, dep[0], dep[1])
                            wait(t, "cA", ev.get(f"{dst_name}_ev{mb - 2}", 0) if mb >= 2 else 0)
                            for mt in range(2):
                                for k in range(DHT):
                                    ins = t.matmul(pb[mt][:, 0:TC],
                                                   wbuf_t(par, k)[:, mt * 128:(mt + 1) * 128],
                                                   src_t(k),
                                                   start=(k == 0), stop=(k == DHT - 1))
                            ev[f"{dst_name}_mm{mb}"] = inc(ins, "cT")
                        bb.tensor(_mm)

                        def _ge(sc, mb=mb, pb=pb):
                            wait(sc, "cT", ev[f"{dst_name}_mm{mb}"])
                            for mt in range(2):
                                m = mb * 2 + mt
                                ins = sc.activation(dst_t(m), pb[mt][:, 0:TC], AF.Gelu,
                                                    bias=0.0,
                                                    scale=scales[:, scol + m:scol + m + 1])
                            ev[f"{dst_name}_ev{mb}"] = inc(ins, "cA")
                        bb.scalar(_ge)
                        pump(1)
                    ev[f"{dst_name}_all"] = cnt["cA"]
                    ev[f"{dst_name}_mm_done"] = cnt["cT"]

                _hid_pass(t_h1, h_t, c_t, SH1C, "c", ("cV", ev["h_all"]), "p2_h1")
                _hid_pass(t_h2, c_t, h_t, SH2C, "d", ("cA", ev["c_all"]), "p2_h2")

                if debug and C == 0:
                    bb.flush()
                    _dump_fm(b, nc, blk, ev, c_t, DHT, TC, dbg["csb"],
                             dep=("cA", ev["c_all"]))
                    _dump_fm(b, nc, blk, ev, h_t, DHT, TC, dbg["hsb"],
                             dep=("cA", ev["d_all"]))

                prev_T_o3 = cnt["cT"]
                for mb in range(D // 256):
                    par = mb % 2
                    pb = ps_f[2 * par:2 * par + 2]

                    def _wl(sy, mb=mb, par=par, C=C, prev_T_o3=prev_T_o3):
                        if C == 0 and mb == 0:
                            wait(sy, "dPW0", ev["p2_w3"][0])
                            wait(sy, "dPW1", ev["p2_w3"][1])
                        wait(sy, "cT", max(ev.get(f"o3_mm{mb - 2}", 0) if mb >= 2 else 0, prev_T_o3))
                        dall = wbuf[:, (par * 32) * 256:(par * 32 + 16) * 256]
                        dall2 = wbuf[:, (par * 32 + 16) * 256:(par * 32 + 32) * 256]
                        tv = t_w3.rearrange("(a p) c -> p a c", p=128)
                        inc(sy.dma_start(
                            dall.rearrange("p (a c) -> p a c", c=256),
                            tv[:, 0:16, mb * 256:(mb + 1) * 256]), f"dW{par}")
                        inc(sy.dma_start(
                            dall2.rearrange("p (a c) -> p a c", c=256),
                            tv[:, 16:32, mb * 256:(mb + 1) * 256]), f"dW{par}")
                        ev[f"o3_wl{mb}"] = cnt[f"dW{par}"]
                    bb.sync(_wl)

                    def _mm(t, mb=mb, par=par, pb=pb):
                        wait(t, f"dW{par}", ev[f"o3_wl{mb}"])
                        if mb == 0:
                            wait(t, "cA", ev["d_all"])
                        wait(t, "cA", ev.get(f"o3_ev{mb - 2}", 0) if mb >= 2 else 0)
                        for mt in range(2):
                            for k in range(DHT):
                                ins = t.matmul(pb[mt][:, 0:TC],
                                               wbuf_t(par, k)[:, mt * 128:(mt + 1) * 128],
                                               h_t(k),
                                               start=(k == 0), stop=(k == DHT - 1))
                        ev[f"o3_mm{mb}"] = inc(ins, "cT")
                    bb.tensor(_mm)

                    for mt in range(2):
                        m = mb * 2 + mt

                        def _ev(sc, mb=mb, pb=pb, mt=mt, m=m):
                            wait(sc, "cT", ev[f"o3_mm{mb}"])
                            wait(sc, "dFS", ev.get("ff_w", 0))
                            ev["ff_act"] = inc(sc.activation(
                                ff_sb[:, :], pb[mt][:, 0:TC], AF.Copy, bias=0.0,
                                scale=scales[:, S3C + m:S3C + m + 1]), "cA")
                            if mt == 1:
                                ev[f"o3_ev{mb}"] = ev["ff_act"]
                        bb.scalar(_ev)

                        def _st(sy, m=m, C=C):
                            wait(sy, "cA", ev["ff_act"])
                            ev["ff_w"] = inc(sy.dma_start(
                                scr_ff[m * 128:(m + 1) * 128, C * 512:(C + 1) * 512],
                                ff_sb[:, :]), "dFS")
                        bb.gpsimd(_st)
                    pump(1)
                ev[f"scr_ff{C}"] = cnt["dFS"]
                ev["o3_done"] = cnt["cA"]

                def ln4_front(jj, C=C):
                    j = C * 4 + jj
                    slot = jj

                    def _l(sy, j=j, slot=slot, C=C):
                        wait(sy, "dFS", ev[f"scr_ff{C}"])
                        wait(sy, "cV", ev.get(f"ysb{j - 4}", 0))
                        ev[f"ff_tm{j}"] = inc(sy.dma_start(
                            ff_tm[:, slot * D:(slot + 1) * D],
                            scr_ff[:, j * 128:(j + 1) * 128], transpose=True), f"dFT{slot}")
                    bb.sync(_l)
                    ln_stats(ff_tm[:, slot * D:(slot + 1) * D], 8 + jj,
                             [(f"dFT{slot}", ev[f"ff_tm{j}"])])
                    ev[f"l4rstd{j}"] = ev[f"ln_rstd{8 + jj}"]

                def xpl_load(jj, C=C):
                    j = C * 4 + jj

                    def _l(sy, j=j, jj=jj):
                        wait(sy, "cV", ev.get(f"ysb{j - 2}", 0))
                        wait(sy, f"dXP{j % 4}", ev[f"xpw{j}"])
                        ev[f"xpl{j}"] = inc(sy.dma_start(
                            xpl[:, (jj % 2) * D:(jj % 2 + 1) * D],
                            scr_xp[j * 128:(j + 1) * 128, :]), f"dXL{jj % 2}")
                    bb.sync(_l)

                def ln4_finish(jj, C=C):
                    j = C * 4 + jj
                    slot = jj
                    c0 = 4 * (8 + jj)

                    def _f(v, j=j, slot=slot, c0=c0, jj=jj, vprev=cnt["cV"]):
                        wait(v, "cA", ev[f"l4rstd{j}"])
                        wait(v, f"dXL{jj % 2}", ev[f"xpl{j}"])
                        wait(v, f"dY{slot % 2}", ev.get(f"yout{j - 2}", 0))
                        wait(v, "cV", vprev)
                        inc(v.tensor_scalar(ln_tmp[:, :], ff_tm[:, slot * D:(slot + 1) * D],
                                            ln_st[:, c0:c0 + 1], ln_st[:, c0 + 2:c0 + 3],
                                            op0=ALU.add, op1=ALU.mult), "cV")
                        wait(v, "cV", cnt["cV"])
                        ins = v.tensor_add(y_sb[:, (slot % 2) * D:(slot % 2 + 1) * D],
                                           ln_tmp[:, :], xpl[:, (jj % 2) * D:(jj % 2 + 1) * D])
                        ev[f"ysb{j}"] = inc(ins, "cV")
                    bb.vector(_f)

                    def _o(sy, j=j, slot=slot):
                        wait(sy, "cV", ev[f"ysb{j}"])
                        ev[f"yout{j}"] = inc(sy.dma_start(
                            y[j * 128:(j + 1) * 128, :], y_sb[:, (slot % 2) * D:(slot % 2 + 1) * D]), f"dY{slot % 2}")
                    bb.gpsimd(_o)

                ln4_front(0)
                xpl_load(0)
                ln4_front(1)
                xpl_load(1)
                ln4_finish(0)
                ln4_front(2)
                xpl_load(2)
                ln4_finish(1)
                ln4_front(3)
                xpl_load(3)
                ln4_finish(2)
                ln4_finish(3)
                bb.flush()

            def _fin(sy):
                for s in cnt:
                    if s.startswith("d"):
                        wait(sy, s, cnt[s])
            bb.sync(_fin)
            bb.flush()
        p2es.close()

    return nc


_DUMP_ID = [0]


def _dump_fm(b, nc, blk, ev, slab_t, n_tiles, width, dram, dep=None):
    _DUMP_ID[0] += 1
    nm = f"dump{_DUMP_ID[0]}"
    with nc.sbuf_tensor(nm, [128, width], F32) as df:
        for i in range(n_tiles):
            def _c(v, i=i):
                if dep is not None:
                    b.wait(v, dep[0], dep[1])
                b.wait(v, "dDG", ev.get("dump_o", 0))
                ev["dump_c"] = b.inc(v.tensor_copy(df[:, 0:width], slab_t(i)), "cV")
            blk.vector(_c)

            def _o(sy, i=i):
                b.wait(sy, "cV", ev["dump_c"])
                ev["dump_o"] = b.inc(sy.dma_start(dram[i * 128:(i + 1) * 128, 0:width],
                                                  df[:, 0:width]), "dDG")
            blk.sync(_o)

        def _g(v):
            b.wait(v, "dDG", ev["dump_o"])
            v.engine_nop()
        blk.vector(_g)


_dump_tm = _dump_fm



def _build_mask(parity):
    j = np.arange(NJ)
    g = 2 * j + parity
    gidx = (g[:, None] * 128 + np.arange(128)[None, :]).reshape(-1)
    k = np.arange(TKP)
    vis = (3.0 * k[:, None] < gidx[None, :] + 0.001) & (k[:, None] < TK)
    m = np.where(vis, np.float32(0.0), np.float32(-1e9)).astype(np.float32)
    return m


_NC_CACHE = {}


def _get_nc(debug=False):
    key = bool(debug)
    if key not in _NC_CACHE:
        _NC_CACHE[key] = build_core_program(debug=key)
    return _NC_CACHE[key]


def kernel(L0_attn_x, L0_attn_Wq, L0_attn_Wk, L0_attn_Wv, L0_attn_Wo, L0_attn_bo,
           L0_attn_Wconv, L0_ffn_w1_w, L0_ffn_w1_s, L0_ffn_w2_w, L0_ffn_w2_s,
           L0_ffn_h1_w, L0_ffn_h1_s, L0_ffn_h2_w, L0_ffn_h2_s, L0_ffn_w3_w,
           L0_ffn_w3_s):
    x = np.ascontiguousarray(np.asarray(L0_attn_x, dtype=np.float32))
    masks = [_build_mask(0), _build_mask(1)]
    own_rows = []
    for par in range(2):
        g = 2 * np.arange(NJ) + par
        own_rows.append((g[:, None] * 128 + np.arange(128)[None, :]).reshape(-1))

    common = {
        "wq": np.ascontiguousarray(L0_attn_Wq, dtype=np.float32),
        "wk": np.ascontiguousarray(L0_attn_Wk, dtype=np.float32),
        "wv": np.ascontiguousarray(L0_attn_Wv, dtype=np.float32),
        "wo": np.ascontiguousarray(L0_attn_Wo, dtype=np.float32),
        "bo": np.ascontiguousarray(L0_attn_bo, dtype=np.float32),
        "wconv": np.ascontiguousarray(L0_attn_Wconv, dtype=np.float32),
        "w1": np.ascontiguousarray(L0_ffn_w1_w, dtype=np.float32),
        "w2": np.ascontiguousarray(L0_ffn_w2_w, dtype=np.float32),
        "h1": np.ascontiguousarray(L0_ffn_h1_w, dtype=np.float32),
        "h2": np.ascontiguousarray(L0_ffn_h2_w, dtype=np.float32),
        "w3": np.ascontiguousarray(L0_ffn_w3_w, dtype=np.float32),
        "s1": np.ascontiguousarray(L0_ffn_w1_s, dtype=np.float32),
        "s2": np.ascontiguousarray(L0_ffn_w2_s, dtype=np.float32),
        "sh1": np.ascontiguousarray(L0_ffn_h1_s, dtype=np.float32),
        "sh2": np.ascontiguousarray(L0_ffn_h2_s, dtype=np.float32),
        "s3": np.ascontiguousarray(L0_ffn_w3_s, dtype=np.float32),
    }
    in_maps = []
    for c in range(8):
        bb, par = c // 2, c % 2
        m = dict(common)
        m["x_full"] = np.ascontiguousarray(x[bb])
        m["x_own"] = np.ascontiguousarray(x[bb][own_rows[par]])
        m["maskT"] = masks[par]
        in_maps.append(m)

    nc = _get_nc(debug=False)
    res = run_bass_kernel_spmd(nc, in_maps, core_ids=list(range(8)))

    out = np.empty((B, T, D), dtype=np.float32)
    for c in range(8):
        bb, par = c // 2, c % 2
        out[bb][own_rows[par]] = res.results[c]["y"]
    return out


# version 22
# speedup vs baseline: 1.4255x; 1.0048x over previous
import numpy as np
import contextlib
import concourse.bass as bass
import concourse.mybir as mybir
from concourse.bass_utils import run_bass_kernel_spmd

F32 = mybir.dt.float32
BF16 = mybir.dt.bfloat16
AF = mybir.ActivationFunctionType
ALU = mybir.AluOpType
AX = mybir.AxisListType

B, T, D, H, HD = 4, 4096, 1024, 16, 64
TOK, NJ = 2048, 16
TK, TKP, NKT = 1366, 1408, 11
DH = 4096
TC, NCH = 512, 4
EXPB = 40.0
DT, DHT = D // 128, DH // 128
HW = 65


def _kvis(j):
    return min(TKP, ((2 * j + 1) * 128 + 127) // 3 + 1)


def _klo(j):
    return (2 * j) * 128 // 3 + 1


def _ktiles(j):
    return (_kvis(j) + 127) // 128


def _masked_kts(j):
    return list(range(_klo(j) // 128, _ktiles(j)))


MAX_MASK = max(len(_masked_kts(j)) for j in range(NJ))


class Bld:
    def __init__(self, nc, block, sems):
        self.nc = nc
        self.block = block
        self.sem = sems
        self.cnt = {k: 0 for k in sems}
        self._record = None
        self._replay = None

    def inc(self, instr, name, k=None):
        if k is None:
            k = 16 if name.startswith("d") else 1
        instr.then_inc(self.sem[name], k)
        self.cnt[name] += k
        return self.cnt[name]

    def wait(self, eng, name, val):
        if self._record is not None:
            self._record.append((name, val))
            return
        if self._replay is not None:
            name, val = self._replay.pop(0)
        if val and val > 0:
            eng.wait_ge(self.sem[name], val)

class _DummyInstr:
    def then_inc(self, sem, k):
        return self

    def annotate(self, *a, **k):
        return self


class _ShadowEngine:
    def __getattr__(self, name):
        return lambda *a, **k: _DummyInstr()


class PhaseBatcher:

    ENGINES = ["sync", "gpsimd", "tensor", "vector", "scalar"]

    def __init__(self, blk, bld):
        self.blk = blk
        self.bld = bld
        self.thunks = []
        self.recs = []
        self._cnt0 = None

    def _add(self, eng, fn):
        b = self.bld
        if not self.thunks:
            self._cnt0 = dict(b.cnt)
        stub = _ShadowEngine()
        b._record = []
        fn(stub)
        self.recs.append(b._record)
        b._record = None
        self.thunks.append((eng, fn))

    def sync(self, fn):
        self._add("sync", fn)

    def gpsimd(self, fn):
        self._add("gpsimd", fn)

    def tensor(self, fn):
        self._add("tensor", fn)

    def vector(self, fn):
        self._add("vector", fn)

    def scalar(self, fn):
        self._add("scalar", fn)

    def flush(self):
        if not self.thunks:
            return
        b = self.bld
        cnt_end = dict(b.cnt)
        recs = self.recs
        b.cnt.clear()
        b.cnt.update(self._cnt0)
        for eng in self.ENGINES:
            mine = [(i, fn) for i, (e, fn) in enumerate(self.thunks) if e == eng]
            if not mine:
                continue

            def _run(engine, mine=mine):
                for i, fn in mine:
                    b._replay = list(recs[i])
                    fn(engine)
                b._replay = None
            getattr(self.blk, eng)(_run)
        assert b.cnt == cnt_end, (b.cnt, cnt_end)
        self.thunks = []
        self.recs = []
        self._cnt0 = None



def build_core_program(debug=False):
    nc = bass.Bass("TRN2", target_bir_lowering=False)

    x_full = nc.dram_tensor("x_full", [T, D], F32, kind="ExternalInput").ap()
    x_own = nc.dram_tensor("x_own", [TOK, D], F32, kind="ExternalInput").ap()
    maskT = nc.dram_tensor("maskT", [TKP, TOK], F32, kind="ExternalInput").ap()
    wq = nc.dram_tensor("wq", [D, D], F32, kind="ExternalInput").ap()
    wk = nc.dram_tensor("wk", [D, D], F32, kind="ExternalInput").ap()
    wv = nc.dram_tensor("wv", [D, D], F32, kind="ExternalInput").ap()
    wo = nc.dram_tensor("wo", [D, D], F32, kind="ExternalInput").ap()
    bo = nc.dram_tensor("bo", [D], F32, kind="ExternalInput").ap()
    wconv = nc.dram_tensor("wconv", [D, D, 3], F32, kind="ExternalInput").ap()
    w1 = nc.dram_tensor("w1", [DH, D], F32, kind="ExternalInput").ap()
    w2 = nc.dram_tensor("w2", [DH, D], F32, kind="ExternalInput").ap()
    h1 = nc.dram_tensor("h1", [DH, DH], F32, kind="ExternalInput").ap()
    h2 = nc.dram_tensor("h2", [DH, DH], F32, kind="ExternalInput").ap()
    w3 = nc.dram_tensor("w3", [D, DH], F32, kind="ExternalInput").ap()
    s1 = nc.dram_tensor("s1", [DH], F32, kind="ExternalInput").ap()
    s2 = nc.dram_tensor("s2", [DH], F32, kind="ExternalInput").ap()
    sh1 = nc.dram_tensor("sh1", [DH], F32, kind="ExternalInput").ap()
    sh2 = nc.dram_tensor("sh2", [DH], F32, kind="ExternalInput").ap()
    s3 = nc.dram_tensor("s3", [D], F32, kind="ExternalInput").ap()
    y = nc.dram_tensor("y", [TOK, D], F32, kind="ExternalOutput").ap()

    n_wq = nc.dram_tensor("n_wq", [D, D], BF16, kind="Internal").ap()
    n_wk = nc.dram_tensor("n_wk", [D, D], BF16, kind="Internal").ap()
    n_wv = nc.dram_tensor("n_wv", [D, D], BF16, kind="Internal").ap()
    n_wo = nc.dram_tensor("n_wo", [D, D], BF16, kind="Internal").ap()
    n_cv = [nc.dram_tensor(f"n_cv{w}", [D, D], BF16, kind="Internal").ap() for w in range(3)]
    n_w1 = nc.dram_tensor("n_w1", [DH, D], BF16, kind="Internal").ap()
    n_w2 = nc.dram_tensor("n_w2", [DH, D], BF16, kind="Internal").ap()
    n_h1 = nc.dram_tensor("n_h1", [DH, DH], BF16, kind="Internal").ap()
    n_h2 = nc.dram_tensor("n_h2", [DH, DH], BF16, kind="Internal").ap()
    n_w3 = nc.dram_tensor("n_w3", [D, DH], BF16, kind="Internal").ap()

    t_w1 = nc.dram_tensor("t_w1", [D, DH], BF16, kind="Internal").ap()
    t_w2 = nc.dram_tensor("t_w2", [D, DH], BF16, kind="Internal").ap()
    t_h1 = nc.dram_tensor("t_h1", [DH, DH], BF16, kind="Internal").ap()
    t_h2 = nc.dram_tensor("t_h2", [DH, DH], BF16, kind="Internal").ap()
    t_w3 = nc.dram_tensor("t_w3", [DH, D], BF16, kind="Internal").ap()

    scr_xf = nc.dram_tensor("scr_xf", [T, D], BF16, kind="Internal").ap()
    scr_xo = nc.dram_tensor("scr_xo", [TOK, D], BF16, kind="Internal").ap()
    scr_o = nc.dram_tensor("scr_o", [TOK, D], BF16, kind="Internal").ap()
    scr_at = nc.dram_tensor("scr_at", [D, TOK], BF16, kind="Internal").ap()
    scr_x2 = nc.dram_tensor("scr_x2", [TOK, D], BF16, kind="Internal").ap()
    scr_ff = nc.dram_tensor("scr_ff", [D, TOK], BF16, kind="Internal").ap()
    scr_xp = nc.dram_tensor("scr_xp", [TOK, D], F32, kind="Internal").ap()

    dbg = {}
    if debug:
        for nm, shp in [("xlnf", [T, D]), ("ktmp", [D, TKP]), ("kfm", [D, TKP]),
                        ("vtm", [TKP, 16 * HW]), ("qfm", [D, TOK]), ("otm", [TOK, D]),
                        ("xp", [TOK, D]), ("hsb", [DH, TC]), ("csb", [DH, TC]),
                        ("attn", [TOK, D])]:
            dbg[nm] = nc.dram_tensor("dbg_" + nm, shp, F32, kind="ExternalOutput").ap()

    stack = contextlib.ExitStack()
    with stack:
        sems = {}
        for s in ["do", "dCV", "dL0", "dL1", "dXW", "dWC", "dWK", "dXO",
                  "dMK0", "dMK1", "dSO", "dOF0", "dOF1", "dAT0", "dAT1", "dTM0", "dTM1", "dTM2", "dTM3", "dL2", "dL3", "dLS2", "dLS3",
                  "dX2S0", "dX2S1", "dX2S2", "dX2S3", "dX2", "dFT2", "dFT3", "dLS0", "dLS1", "dY0", "dY1", "dPT0", "dPT1", "dPW0", "dPW1", "dW0", "dW1", "dFS", "dFT0", "dFT1", "dY", "dDG",
                  "dgA", "dgF", "dgH1", "dgH2", "dXP0", "dXP1", "dXP2", "dXP3",
                  "dXL0", "dXL1", "dWO", "cT", "cV", "cA"]:
            sems[s] = stack.enter_context(nc.semaphore(s))
        blk = stack.enter_context(nc.Block())
        b = Bld(nc, blk, sems)
        bb = PhaseBatcher(blk, b)
        inc, wait, cnt = b.inc, b.wait, b.cnt
        ev = {}

        consts = stack.enter_context(nc.sbuf_tensor("consts", [128, 4], F32))
        NSC = 4 * DHT + 64
        scales = stack.enter_context(nc.sbuf_tensor("scales", [128, NSC], F32))
        S1C, S2C, SH1C, SH2C, S3C, BOC = 0, DHT, 2 * DHT, 3 * DHT, 4 * DHT, 4 * DHT + 32

        bb.vector(lambda v: (
            v.memset(consts[:, 0:1], 1e-5),
            v.memset(consts[:, 1:2], -EXPB),
            ev.__setitem__("consts", inc(v.memset(consts[:, 2:3], 0.0), "cV")),
        ))

        ln_st = stack.enter_context(nc.sbuf_tensor("ln_st", [128, 48], F32))
        ln_waste = stack.enter_context(nc.sbuf_tensor("ln_waste", [128, D], BF16))

        def _prep(g):
            def d2d(dst, src, rows, sem, chunk=512):
                for r0 in range(0, rows, chunk):
                    inc(g.dma_start(dst[r0:r0 + chunk, :], src[r0:r0 + chunk, :]), sem)
            d2d(n_wq, wq, D, "dgA")
            d2d(n_wk, wk, D, "dgA")
            d2d(n_wv, wv, D, "dgA")
            d2d(n_wo, wo, D, "dgA")
            d2d(n_w1, w1, DH, "dgF")
            d2d(n_w2, w2, DH, "dgF")
        bb.gpsimd(_prep)
        ev["dgA"], ev["dgF"], ev["dgH1"], ev["dgH2"] = cnt["dgA"], cnt["dgF"], cnt["dgH1"], cnt["dgH2"]

        with nc.sbuf_tensor("cv_nat", [128, 3 * D], F32) as cv_nat, \
             nc.sbuf_tensor("cv_bf", [128, 3 * D], BF16) as cv_bf:
            for r in range(DT):
                def _l(sy, r=r):
                    wait(sy, "dCV", ev.get("cv_out", 0))
                    ev["cv_in"] = inc(sy.dma_start(
                        cv_nat[:, :],
                        wconv[r * 128:(r + 1) * 128, :, :].rearrange("a b c -> a (b c)")), "dCV")
                bb.sync(_l)

                def _c(v, r=r):
                    wait(v, "dCV", ev["cv_in"])
                    for w in range(3):
                        v.tensor_copy(cv_bf[:, w * D:(w + 1) * D], cv_nat[:, w:3 * D:3])
                    ev["cv_bf"] = inc(v.engine_nop(), "cV")
                bb.vector(_c)

                def _s(sy, r=r):
                    wait(sy, "cV", ev["cv_bf"])
                    for w in range(3):
                        inc(sy.dma_start(n_cv[w][r * 128:(r + 1) * 128, :],
                                         cv_bf[:, w * D:(w + 1) * D]), "dCV")
                    ev["cv_out"] = cnt["dCV"]
                bb.sync(_s)
            ev["conv_w"] = cnt["dCV"]

        with nc.sbuf_tensor("svec_nat", [32, 128], F32) as svec_nat:
            def load_svec(src, n_tiles, dst_col):
                for t0 in range(0, n_tiles, 32):
                    tn = min(32, n_tiles - t0)

                    def _z(v, tn=tn):
                        wait(v, "do", ev.get("svec_in", 0))
                        wait(v, "cV", ev.get("svec_used", 0))
                        if tn < 32:
                            ev["svec_z"] = inc(v.memset(svec_nat[:, :], 0.0), "cV")
                        else:
                            ev["svec_z"] = inc(v.engine_nop(), "cV")
                    bb.vector(_z)

                    def _l(sy, t0=t0, tn=tn):
                        wait(sy, "cV", ev["svec_z"])
                        ev["svec_in"] = inc(sy.dma_start(
                            svec_nat[0:tn, :],
                            src[t0 * 128:(t0 + tn) * 128].rearrange("(a b) -> a b", b=128)), "do")
                    bb.sync(_l)

                    def _t(v, t0=t0, tn=tn, dst_col=dst_col):
                        wait(v, "do", ev["svec_in"])
                        for bi in range((tn + 31) // 32):
                            for pj in range(4):
                                v.transpose(
                                    scales[pj * 32:(pj + 1) * 32,
                                           dst_col + t0 + bi * 32:dst_col + t0 + bi * 32 + 32],
                                    svec_nat[bi * 32:bi * 32 + 32, pj * 32:(pj + 1) * 32])
                        ev["svec_used"] = inc(v.engine_nop(), "cV")
                    bb.vector(_t)
            load_svec(s1, DHT, S1C)
            load_svec(s2, DHT, S2C)
            load_svec(sh1, DHT, SH1C)
            load_svec(sh2, DHT, SH2C)
            load_svec(s3, DT, S3C)
            load_svec(bo, DT, BOC)
            ev["scales"] = cnt["cV"]

        def ln_stats(src_ap, slot, dep_list):
            c0 = 4 * slot

            def _v(v):
                for s, val in dep_list:
                    wait(v, s, val)
                v.reduce_sum(ln_st[:, c0:c0 + 1], src_ap, AX.X)
                inc(v.engine_nop(), "cV")
                wait(v, "cV", cnt["cV"])
                inc(v.tensor_scalar_mul(ln_st[:, c0:c0 + 1], ln_st[:, c0:c0 + 1], -1.0 / D), "cV")
                wait(v, "cV", cnt["cV"])
                ev[f"ln_sum{slot}"] = inc(
                    v.scalar_tensor_tensor(ln_waste[:, :], src_ap,
                                           ln_st[:, c0:c0 + 1], src_ap,
                                           op0=ALU.add, op1=ALU.mult,
                                           accum_out=ln_st[:, c0 + 1:c0 + 2]), "cV")
            bb.vector(_v)

            def _a(sc):
                wait(sc, "cV", ev[f"ln_sum{slot}"])
                wait(sc, "cV", ev["consts"])
                inc(sc.activation(ln_st[:, c0 + 2:c0 + 3], ln_st[:, c0 + 1:c0 + 2], AF.Ln,
                                  bias=consts[:, 0:1], scale=1.0 / D), "cA")
                wait(sc, "cA", cnt["cA"])
                ev[f"ln_rstd{slot}"] = inc(
                    sc.activation(ln_st[:, c0 + 2:c0 + 3], ln_st[:, c0 + 2:c0 + 3],
                                  AF.Exp, bias=0.0, scale=-0.5), "cA")
            bb.scalar(_a)
            return ("cA", ev[f"ln_rstd{slot}"])


        with nc.sbuf_tensor("lx01", [128, 2 * D], F32, side="right") as lx01, \
             nc.sbuf_tensor("lx23", [128, 2 * D], F32, side="right") as lx23, \
             nc.sbuf_tensor("lb01", [128, 2 * D], BF16, side="right") as lb01, \
             nc.sbuf_tensor("lb23", [128, 2 * D], BF16, side="right") as lb23:
            lxs = [lx01[:, 0:D], lx01[:, D:2 * D], lx23[:, 0:D], lx23[:, D:2 * D]]
            lbs = [lb01[:, 0:D], lb01[:, D:2 * D], lb23[:, 0:D], lb23[:, D:2 * D]]

            def ln1_front(src_slice, slot):
                lx = lxs[slot]

                def _l(sy):
                    wait(sy, "cA", ev.get(f"ln_used{slot}", 0))
                    wait(sy, "cV", ev.get(f"lnbf{slot}", 0))
                    ev[f"lnx{slot}"] = inc(sy.dma_start(lx, src_slice), f"dL{slot}")
                bb.sync(_l)
                ln_stats(lx, slot, [(f"dL{slot}", ev[f"lnx{slot}"])])

            def ln1_finish(dst_slice, slot):
                lx, lb = lxs[slot], lbs[slot]
                c0 = 4 * slot

                def _f(v):
                    wait(v, "cA", ev[f"ln_rstd{slot}"])
                    wait(v, f"dLS{slot}", ev.get(f"lnout{slot}", 0))
                    ev[f"lnbf{slot}"] = inc(
                        v.tensor_scalar(lb, lx,
                                        ln_st[:, c0:c0 + 1], ln_st[:, c0 + 2:c0 + 3],
                                        op0=ALU.add, op1=ALU.mult), "cV")
                    ev[f"ln_used{slot}"] = ev[f"ln_rstd{slot}"]
                bb.vector(_f)

                def _o(sy):
                    wait(sy, "cV", ev[f"lnbf{slot}"])
                    ev[f"lnout{slot}"] = inc(sy.dma_start(dst_slice, lb), f"dLS{slot}")
                bb.sync(_o)

            specs = [(x_full[t * 128:(t + 1) * 128, :], scr_xf[t * 128:(t + 1) * 128, :])
                     for t in range(T // 128)]
            specs += [(x_own[t * 128:(t + 1) * 128, :], scr_xo[t * 128:(t + 1) * 128, :])
                      for t in range(TOK // 128)]
            rstd_ev = {}
            for t, (srcs, dsts) in enumerate(specs):
                ln1_front(srcs, t % 4)
                rstd_ev[t] = ev[f"ln_rstd{t % 4}"]
                if t >= 2:
                    ev[f"ln_rstd{(t - 2) % 4}"] = rstd_ev[t - 2]
                    ln1_finish(specs[t - 2][1], (t - 2) % 4)
            for t in (len(specs) - 2, len(specs) - 1):
                ev[f"ln_rstd{t % 4}"] = rstd_ev[t]
                ln1_finish(specs[t][1], t % 4)
            for s_ in range(4):
                ev[f"ln1_{s_}"] = cnt[f"dLS{s_}"]
            ev["ln1_v"] = cnt["cV"]
            bb.flush()

        otm = None
        otm_t = lambda j: otm[:, j * D:(j + 1) * D]

        otm_es = contextlib.ExitStack()
        otm = otm_es.enter_context(nc.sbuf_tensor("otm", [128, NJ * D], BF16, side="left"))
        attn_es = contextlib.ExitStack()
        ktmp_es = contextlib.ExitStack()
        if True:
            ktmp = ktmp_es.enter_context(nc.sbuf_tensor("ktmp", [128, DT * TKP], BF16, side="right"))
            ktmp_t = lambda kd: ktmp[:, kd * TKP:(kd + 1) * TKP]

            with nc.sbuf_tensor("xwin", [128, DT * 1536], BF16, side="right") as xwin, \
                 nc.sbuf_tensor("wt_cv", [128, DT * D], BF16, side="right") as wt_cv, \
                 contextlib.ExitStack() as cps:
                ps_cv = [cps.enter_context(nc.psum_tensor(f"ps_cv{i}", [128, 512], F32))
                         for i in range(8)]
                xwin_t = lambda kd: xwin[:, kd * 1536:(kd + 1) * 1536]
                wt_cv_t = lambda kd: wt_cv[:, kd * D:(kd + 1) * D]

                def _ktmp_init(v):
                    wait(v, "cV", ev["ln1_v"])
                    for s_ in range(4):
                        wait(v, f"dLS{s_}", ev[f"ln1_{s_}"])
                    for kd in range(DT):
                        v.memset(ktmp_t(kd)[:, TK:TKP], 0.0)
                    ev["ktmp_init"] = inc(v.engine_nop(), "cV")
                bb.vector(_ktmp_init)

                conv_chunks = [(0, 512), (512, 512), (1024, 341)]
                for ci, (p0, pn) in enumerate(conv_chunks):
                    rows0 = 3 * p0
                    rows1 = min(T, rows0 + 1536)

                    def _xl(sy, ci=ci, rows0=rows0, rows1=rows1):
                        for s_ in range(4):
                            wait(sy, f"dLS{s_}", ev[f"ln1_{s_}"])
                        wait(sy, "cT", ev.get("conv_mm_c", 0))
                        wait(sy, "cV", ev.get("ktmp_c0", 0))
                        for kd in range(DT):
                            inc(sy.dma_start(xwin_t(kd)[:, 0:rows1 - rows0],
                                             scr_xf[rows0:rows1, kd * 128:(kd + 1) * 128],
                                             transpose=True), "dXW")
                        ev["xwin"] = cnt["dXW"]
                    bb.sync(_xl)
                    if ci == 0:
                        def _c0(v):
                            wait(v, "dXW", ev["xwin"])
                            for kd in range(DT):
                                v.tensor_copy(ktmp_t(kd)[:, 0:1], xwin_t(kd)[:, 0:1])
                            ev["ktmp_c0"] = inc(v.engine_nop(), "cV")
                        bb.vector(_c0)

                    for w in range(3):
                        def _wl(sy, w=w, ci=ci):
                            wait(sy, "dCV", ev["conv_w"])
                            wait(sy, "cT", ev.get("conv_mm_w", 0))
                            for kd in range(DT):
                                inc(sy.dma_start(wt_cv_t(kd), n_cv[w][:, kd * 128:(kd + 1) * 128],
                                                 transpose=True), "dWC")
                            ev["cv_wl"] = cnt["dWC"]
                        bb.sync(_wl)

                        def _mm(t, w=w, ci=ci, p0=p0, pn=pn):
                            wait(t, "dWC", ev["cv_wl"])
                            wait(t, "dXW", ev["xwin"])
                            if w == 0:
                                wait(t, "cV", ev.get("conv_ev_c", 0))
                            for ot in range(DT):
                                for kd in range(DT):
                                    ins = t.matmul(ps_cv[ot][:, 0:pn],
                                                   wt_cv_t(kd)[:, ot * 128:(ot + 1) * 128],
                                                   xwin_t(kd)[:, w:3 * (pn - 1) + w + 1:3],
                                                   start=(w == 0 and kd == 0),
                                                   stop=(w == 2 and kd == DT - 1))
                            ev["conv_mm_w"] = inc(ins, "cT")
                        bb.tensor(_mm)
                    ev["conv_mm_c"] = cnt["cT"]

                    def _ev(v, ci=ci, p0=p0, pn=pn):
                        wait(v, "cT", ev["conv_mm_c"])
                        wait(v, "cV", ev["ktmp_init"])
                        for ot in range(DT):
                            ins = v.tensor_copy(ktmp_t(ot)[:, 1 + p0:1 + p0 + pn],
                                                ps_cv[ot][:, 0:pn])
                        ev["conv_ev_c"] = inc(ins, "cV")
                    bb.vector(_ev)
                ev["ktmp"] = cnt["cV"]
                ev["conv_mm_all"] = cnt["cT"]

            if debug:
                bb.flush()
                _dump_fm(b, nc, blk, ev, ktmp_t, DT, TKP, dbg["ktmp"], dep=("cV", ev["ktmp"]))

            kfm = attn_es.enter_context(nc.sbuf_tensor("kfm", [128, DT * TKP], BF16, side="left"))
            vtm = attn_es.enter_context(nc.sbuf_tensor("vtm", [128, NKT * 16 * HW], BF16, side="left"))
            kfm_t = lambda kd: kfm[:, kd * TKP:(kd + 1) * TKP]
            vtm_t = lambda kt: vtm[:, kt * 16 * HW:(kt + 1) * 16 * HW]

            with nc.sbuf_tensor("wt_kv", [128, DT * D], BF16, side="right") as wt_kv, \
                 nc.psum_tensor("ps_q0", [128, 512], F32) as ps_q0, \
                 nc.psum_tensor("ps_q1", [128, 512], F32) as ps_q1:
                ps_proj = [ps_q0, ps_q1]
                wt_kv_t = lambda kd: wt_kv[:, kd * D:(kd + 1) * D]

                def _wk(sy):
                    wait(sy, "dgA", ev["dgA"])
                    wait(sy, "cT", ev["conv_mm_all"])
                    for kd in range(DT):
                        inc(sy.dma_start(wt_kv_t(kd), n_wk[:, kd * 128:(kd + 1) * 128],
                                         transpose=True), "dWK")
                    ev["wkT"] = cnt["dWK"]
                bb.sync(_wk)

                pp = 0
                kproj_chunks = [(0, 512), (512, 512), (1024, 384)]
                for ot in range(DT):
                    for (p0, pn) in kproj_chunks:
                        ps = ps_proj[pp % 2]

                        def _mm(t, ot=ot, p0=p0, pn=pn, ps=ps, pp=pp):
                            wait(t, "dWK", ev["wkT"])
                            wait(t, "cV", ev["ktmp"])
                            wait(t, "cV", ev.get(f"kp_ev{pp % 2}", 0))
                            for kd in range(DT):
                                ins = t.matmul(ps[:, 0:pn],
                                               wt_kv_t(kd)[:, ot * 128:(ot + 1) * 128],
                                               ktmp_t(kd)[:, p0:p0 + pn],
                                               start=(kd == 0), stop=(kd == DT - 1))
                            ev["kp_mm"] = inc(ins, "cT")
                        bb.tensor(_mm)

                        def _ev(v, ot=ot, p0=p0, pn=pn, ps=ps, pp=pp):
                            wait(v, "cT", ev["kp_mm"])
                            ev[f"kp_ev{pp % 2}"] = inc(
                                v.tensor_copy(kfm_t(ot)[:, p0:p0 + pn], ps[:, 0:pn]), "cV")
                        bb.vector(_ev)
                        pp += 1
                ev["kfm"] = cnt["cV"]
                ev["kp_mm_all"] = cnt["cT"]

                def _wv(sy):
                    wait(sy, "cT", ev["kp_mm_all"])
                    for kd in range(DT):
                        inc(sy.dma_start(wt_kv_t(kd), n_wv[:, kd * 128:(kd + 1) * 128],
                                         transpose=True), "dWK")
                    ev["wvT"] = cnt["dWK"]
                bb.sync(_wv)

                def _vones(v):
                    for kt in range(NKT):
                        v.memset(vtm_t(kt)[:, 64:16 * HW:HW], 1.0)
                    ev["vones"] = inc(v.engine_nop(), "cV")
                bb.vector(_vones)

                for kt in range(NKT):
                    for half in range(2):
                        ps = ps_proj[pp % 2]

                        def _mm(t, kt=kt, half=half, ps=ps, pp=pp):
                            wait(t, "dWK", ev["wvT"])
                            wait(t, "cV", ev.get(f"vp_ev{pp % 2}", ev["kfm"]))
                            for kd in range(DT):
                                ins = t.matmul(ps[:, 0:512],
                                               ktmp_t(kd)[:, kt * 128:(kt + 1) * 128],
                                               wt_kv_t(kd)[:, half * 512:(half + 1) * 512],
                                               start=(kd == 0), stop=(kd == DT - 1))
                            ev["vp_mm"] = inc(ins, "cT")
                        bb.tensor(_mm)

                        def _ev(v, kt=kt, half=half, ps=ps, pp=pp):
                            wait(v, "cT", ev["vp_mm"])
                            wait(v, "cV", ev["vones"])
                            dst = vtm_t(kt)[:, half * 8 * HW:(half + 1) * 8 * HW]
                            ev[f"vp_ev{pp % 2}"] = inc(v.tensor_copy(
                                dst.rearrange("p (h c) -> p h c", c=HW)[:, :, 0:64],
                                ps[:, 0:512].rearrange("p (h c) -> p h c", c=64)), "cV")
                        bb.vector(_ev)
                        pp += 1
                ev["vtm"] = cnt["cV"]
                ev["vp_mm_all"] = cnt["cT"]

            ktmp_es.close()
            qfm = attn_es.enter_context(nc.sbuf_tensor("qfm", [128, DT * TOK], BF16, side="left"))
            qfm_t = lambda kd: qfm[:, kd * TOK:(kd + 1) * TOK]

            with nc.sbuf_tensor("xowin", [128, DT * 512], BF16, side="right") as xowin, \
                 nc.sbuf_tensor("wt_q", [128, DT * D], BF16, side="right") as wt_q, \
                 nc.psum_tensor("ps_qa", [128, 512], F32) as ps_qa, \
                 nc.psum_tensor("ps_qb", [128, 512], F32) as ps_qb:
                ps_proj = [ps_qa, ps_qb]
                xowin_t = lambda kd: xowin[:, kd * 512:(kd + 1) * 512]
                wt_q_t = lambda kd: wt_q[:, kd * D:(kd + 1) * D]

                def _wq(sy):
                    wait(sy, "dgA", ev["dgA"])
                    wait(sy, "cT", ev["vp_mm_all"])
                    for kd in range(DT):
                        inc(sy.dma_start(wt_q_t(kd), n_wq[:, kd * 128:(kd + 1) * 128],
                                         transpose=True), "dWK")
                    ev["wqT"] = cnt["dWK"]
                bb.sync(_wq)

                pp = 0
                for nch in range(TOK // 512):
                    def _xl(sy, nch=nch):
                        for s_ in range(4):
                            wait(sy, f"dLS{s_}", ev[f"ln1_{s_}"])
                        wait(sy, "cT", ev.get("qp_mm_n", 0))
                        for kd in range(DT):
                            inc(sy.dma_start(xowin_t(kd),
                                             scr_xo[nch * 512:(nch + 1) * 512,
                                                    kd * 128:(kd + 1) * 128],
                                             transpose=True), "dXO")
                        ev["xowin"] = cnt["dXO"]
                    bb.sync(_xl)

                    for ot in range(DT):
                        ps = ps_proj[pp % 2]

                        def _mm(t, ot=ot, nch=nch, ps=ps, pp=pp):
                            wait(t, "dWK", ev["wqT"])
                            wait(t, "dXO", ev["xowin"])
                            if pp < 2:
                                wait(t, "cV", ev["vtm"])
                            wait(t, "cA", ev.get(f"qp_ev{pp % 2}", 0))
                            for kd in range(DT):
                                ins = t.matmul(ps[:, 0:512],
                                               wt_q_t(kd)[:, ot * 128:(ot + 1) * 128],
                                               xowin_t(kd),
                                               start=(kd == 0), stop=(kd == DT - 1))
                            ev["qp_mm"] = inc(ins, "cT")
                        bb.tensor(_mm)

                        def _ev(sc, ot=ot, nch=nch, ps=ps, pp=pp):
                            wait(sc, "cT", ev["qp_mm"])
                            ev[f"qp_ev{pp % 2}"] = inc(
                                sc.activation(qfm_t(ot)[:, nch * 512:(nch + 1) * 512],
                                              ps[:, 0:512], AF.Copy, bias=0.0, scale=0.125), "cA")
                        bb.scalar(_ev)
                        pp += 1
                    ev["qp_mm_n"] = cnt["cT"]
                ev["qfm"] = cnt["cA"]

            bb.flush()
            if debug:
                _dump_fm(b, nc, blk, ev, kfm_t, DT, TKP, dbg["kfm"], dep=("cV", ev["kfm"]))
                _dump_fm(b, nc, blk, ev, vtm_t, NKT, 16 * HW, dbg["vtm"], dep=("cV", ev["vtm"]))
                _dump_fm(b, nc, blk, ev, qfm_t, DT, TOK, dbg["qfm"], dep=("cA", ev["qfm"]))

        p2es = contextlib.ExitStack()
        p2b0 = p2es.enter_context(nc.sbuf_tensor("p2b0", [128, DH], BF16, side="right"))
        p2b = [p2b0, p2b0]
        cast_q = {"h1": [(n_h1, h1, r0, "dgH1") for r0 in range(0, DH, 256)],
                  "h2": [(n_h2, h2, r0, "dgH2") for r0 in range(0, DH, 256)],
                  "w3": [(n_w3, w3, r0, "dgH2") for r0 in range(0, D, 256)]}
        cast_order = ["h1", "h2", "w3"]

        def emit_casts(n):
            def _g(g, items):
                for dst_, src_, r0_, sem_ in items:
                    inc(g.dma_start(dst_[r0_:r0_ + 256, :], src_[r0_:r0_ + 256, :]), sem_)
            items = []
            while n > 0 and cast_order:
                qn = cast_order[0]
                if not cast_q[qn]:
                    cast_order.pop(0)
                    continue
                items.append(cast_q[qn].pop(0))
                n -= 1
            if items:
                bb.gpsimd(lambda g, items=tuple(items): _g(g, items))
            ev["dgF"], ev["dgH1"], ev["dgH2"] = cnt["dgF"], cnt["dgH1"], cnt["dgH2"]

        def drain_casts_for(w):
            names = {"t_h1": "h1", "t_h2": "h2", "t_w3": "w3"}
            qn = names.get(w)
            while qn and cast_q[qn]:
                emit_casts(8)
        p2_steps = []
        for nat_, tdst_, I_, O_, gate_, mark_ in [
                (n_w1, t_w1, D, DH, "dgF", None),
                (n_w2, t_w2, D, DH, "dgF", "p2_ab"),
                (n_h1, t_h1, DH, DH, "dgH1", "p2_h1"),
                (n_h2, t_h2, DH, DH, "dgH2", "p2_h2"),
                (n_w3, t_w3, DH, D, "dgH2", "p2_w3")]:
            nkd = I_ // 128
            for kd_ in range(nkd):
                p2_steps.append((nat_, tdst_, O_, gate_, kd_,
                                 mark_ if kd_ == nkd - 1 else None))
        p2_i = [0]

        def emit_p2(nsteps):
            while nsteps > 0 and p2_i[0] < len(p2_steps):
                nat_, tdst_, O_, gate_, kd_, mark_ = p2_steps[p2_i[0]]
                if kd_ == 0:
                    drain_casts_for(tdst_.tensor.name)
                par = p2_i[0] % 2

                def _t(sy, nat_=nat_, O_=O_, gate_=gate_, kd_=kd_, par=par, i=p2_i[0]):
                    if i < 2:
                        wait(sy, "cA", ev["qfm"])
                    wait(sy, gate_, ev[gate_])
                    wait(sy, "dPW0", ev.get("p2w0", 0))
                    wait(sy, "dPW1", ev.get("p2w1", 0))
                    ev[f"p2t{par}"] = inc(sy.dma_start(
                        p2b[par][:, 0:O_], nat_[0:O_, kd_ * 128:(kd_ + 1) * 128],
                        transpose=True), f"dPT{par}")
                bb.sync(_t)

                def _w(g, tdst_=tdst_, O_=O_, kd_=kd_, par=par):
                    wait(g, f"dPT{par}", ev[f"p2t{par}"])
                    ev[f"p2w{par}"] = inc(g.dma_start(
                        tdst_[kd_ * 128:(kd_ + 1) * 128, 0:O_], p2b[par][:, 0:O_]), f"dPW{par}")
                bb.gpsimd(_w)
                if mark_:
                    ev[mark_] = (cnt["dPW0"], cnt["dPW1"])
                p2_i[0] += 1
                nsteps -= 1

        def _mk_pair(jp):
            return list(range(_klo(2 * jp) // 128, _ktiles(2 * jp + 1)))

        MKP = max(len(_mk_pair(jp)) for jp in range(NJ // 2))
        NMK = sum(len(_mk_pair(jp)) for jp in range(NJ // 2))
        a2 = contextlib.ExitStack()
        with a2:
            pt_sb = a2.enter_context(nc.sbuf_tensor("pt_sb", [128, 2 * 6 * 256], BF16, side="right"))
            mk_sb = a2.enter_context(nc.sbuf_tensor("mk_sb", [128, NMK * 256], BF16, side="right"))
            ps_scA = a2.enter_context(nc.psum_tensor("ps_scA", [128, 3 * 512], F32))
            ps_scB = a2.enter_context(nc.psum_tensor("ps_scB", [128, 3 * 512], F32))
            ps_oA = a2.enter_context(nc.psum_tensor("ps_oA", [128, 512], F32))
            ps_oB = a2.enter_context(nc.psum_tensor("ps_oB", [128, 512], F32))
            ps_sc = [ps_scA, ps_scB]
            ps_o = [ps_oA, ps_oB]
            mk_base = {}
            _mb = 0
            for jp_ in range(NJ // 2):
                mk_base[jp_] = _mb
                _mb += len(_mk_pair(jp_))
            mk_slot = lambda jp, i: mk_sb[:, (mk_base[jp] + i) * 256:
                                          (mk_base[jp] + i + 1) * 256]

            def _mload_all(g):
                wait(g, "cA", ev["qfm"])
                for jp_ in range(NJ // 2):
                    for i_, kt_ in enumerate(_mk_pair(jp_)):
                        inc(g.dma_start(mk_slot(jp_, i_),
                                        maskT[kt_ * 128:(kt_ + 1) * 128,
                                              2 * jp_ * 128:(2 * jp_ + 2) * 128]), "dMK0")
            bb.gpsimd(_mload_all)
            ev["mload_all"] = cnt["dMK0"]

            gnum = 0
            itc = 0
            for jp in range(NJ // 2):
                mk = _mk_pair(jp)
                j0 = 2 * jp

                emit_casts(7)
                emit_p2(16)

                nkt = _ktiles(2 * jp + 1)
                for h in range(H):
                    hq, hp = h // 2, (h % 2) * 64
                    groups = [list(range(nkt))[i:i + 6] for i in range(0, nkt, 6)]
                    po = ps_o[itc % 2]
                    for gi, g in enumerate(groups):
                        psc = ps_sc[gnum % 2]
                        slot = gnum % 2

                        def _sc_mm(t, jp=jp, h=h, g=tuple(g), psc=psc, hq=hq, hp=hp,
                                   gnum=gnum, j0=j0):
                            if gnum < 2:
                                wait(t, "cA", ev["qfm"])
                                wait(t, "cV", ev["kfm"])
                                wait(t, "cV", ev["vtm"])
                            wait(t, "cA", ev.get(f"exp_g{gnum - 2}", 0))
                            q_sl = qfm_t(hq)[hp:hp + 64, j0 * 128:(j0 + 2) * 128]
                            for si, kt in enumerate(g):
                                last = (si == len(g) - 1)
                                ins = t.matmul(psc[:, si * 256:si * 256 + 256],
                                               kfm_t(hq)[hp:hp + 64, kt * 128:(kt + 1) * 128],
                                               q_sl,
                                               start=(si % 2 == 0),
                                               stop=(si % 2 == 1) or last)
                            ev[f"scmm_g{gnum}"] = inc(ins, "cT")
                        bb.tensor(_sc_mm)

                        has_mask = any(kt in mk for kt in g)
                        if has_mask:
                            def _mask(v, jp=jp, g=tuple(g), psc=psc, mk=tuple(mk), gnum=gnum):
                                wait(v, "cT", ev[f"scmm_g{gnum}"])
                                wait(v, "dMK0", ev["mload_all"])
                                for si, kt in enumerate(g):
                                    if kt in mk:
                                        mi = list(mk).index(kt)
                                        ins = v.tensor_tensor(psc[:, si * 256:si * 256 + 256],
                                                              psc[:, si * 256:si * 256 + 256],
                                                              mk_slot(jp, mi), ALU.add)
                                ev[f"mask_g{gnum}"] = inc(ins, "cV")
                            bb.vector(_mask)

                        def _exp(sc, g=tuple(g), psc=psc, slot=slot, gnum=gnum, jp=jp,
                                 has_mask=has_mask):
                            if has_mask:
                                wait(sc, "cV", ev[f"mask_g{gnum}"])
                            else:
                                wait(sc, "cT", ev[f"scmm_g{gnum}"])
                            wait(sc, "cT", ev.get(f"pv_g{gnum - 2}", 0))
                            ng = len(g)
                            ins = sc.activation(
                                pt_sb[:, slot * 1536:slot * 1536 + ng * 256],
                                psc[:, 0:ng * 256],
                                AF.Exp, bias=consts[:, 1:2], scale=1.0)
                            ev[f"exp_g{gnum}"] = inc(ins, "cA")
                            ev[f"mask_used{jp}"] = ev[f"exp_g{gnum}"]
                        bb.scalar(_exp)

                        def _pv(t, g=tuple(g), po=po, slot=slot, gnum=gnum, h=h,
                                gi=gi, ngroups=len(groups), itc=itc):
                            wait(t, "cA", ev[f"exp_g{gnum}"])
                            if gi == 0 and itc >= 2:
                                wait(t, "cV", ev.get(f"oevac{itc - 2}", 0))
                            for si, kt in enumerate(g):
                                first = (gi == 0 and si == 0)
                                lastg = (gi == ngroups - 1 and si == len(g) - 1)
                                t.matmul(po[:, 0:65],
                                         pt_sb[:, slot * 1536 + si * 256:
                                               slot * 1536 + si * 256 + 128],
                                         vtm_t(kt)[:, h * HW:(h + 1) * HW],
                                         start=first, stop=False)
                                ins = t.matmul(po[:, 65:130],
                                               pt_sb[:, slot * 1536 + si * 256 + 128:
                                                     slot * 1536 + si * 256 + 256],
                                               vtm_t(kt)[:, h * HW:(h + 1) * HW],
                                               start=False, stop=lastg)
                            ev[f"pv_g{gnum}"] = inc(ins, "cT")
                        bb.tensor(_pv)
                        gnum += 1

                    def _oev(v, jp=jp, h=h, po=po, itc=itc, j0=j0):
                        wait(v, "cT", ev[f"pv_g{gnum - 1}"])
                        inc(v.reciprocal(po[:, 64:130:65], po[:, 64:130:65]), "cV")
                        wait(v, "cV", cnt["cV"])
                        v.tensor_scalar(otm_t(j0)[:, h * 64:(h + 1) * 64],
                                        po[:, 0:64], po[:, 64:65], None, op0=ALU.mult)
                        ins = v.tensor_scalar(otm_t(j0 + 1)[:, h * 64:(h + 1) * 64],
                                              po[:, 65:129], po[:, 129:130], None, op0=ALU.mult)
                        ev[f"oevac{itc}"] = inc(ins, "cV")
                    bb.vector(_oev)
                    itc += 1

                def _so(sy, jp=jp, val=cnt["cV"]):
                    wait(sy, "cV", val)
                    for j in (2 * jp, 2 * jp + 1):
                        inc(sy.dma_start(scr_o[j * 128:(j + 1) * 128, :], otm_t(j)), "dSO")
                bb.gpsimd(_so)
            emit_casts(10 ** 6)
            emit_p2(len(p2_steps))
            ev["otm"] = cnt["cV"]
            ev["scr_o"] = cnt["dSO"]
            ev["attn_T"] = cnt["cT"]
            ev["attn_A"] = cnt["cA"]
            bb.flush()
        attn_es.close()

        otm_es.close()

        hslab = stack.enter_context(nc.sbuf_tensor("hslab", [128, DHT * TC], BF16, side="left"))
        cslab = stack.enter_context(nc.sbuf_tensor("cslab", [128, DHT * TC], BF16, side="left"))
        h_t = lambda m: hslab[:, m * TC:(m + 1) * TC]
        c_t = lambda m: cslab[:, m * TC:(m + 1) * TC]

        fb = contextlib.ExitStack()
        with fb:
            ln_tmp = fb.enter_context(nc.sbuf_tensor("ln_tmp", [128, D], F32, side="right"))
            x2fm = fb.enter_context(nc.sbuf_tensor("x2fm", [128, DT * TC], BF16, side="right"))
            wbuf = fb.enter_context(nc.sbuf_tensor("wbuf", [128, 2 * 32 * 256], BF16, side="right"))
            silu_sb = fb.enter_context(nc.sbuf_tensor("silu_sb", [128, 2 * 512], F32, side="right"))
            ff_sb = fb.enter_context(nc.sbuf_tensor("ff_sb", [128, 512], BF16, side="right"))
            ff_tm = fb.enter_context(nc.sbuf_tensor("ff_tm", [128, 4 * D], BF16, side="right"))
            y_sb = fb.enter_context(nc.sbuf_tensor("y_sb", [128, 2 * D], F32, side="right"))
            wt_o = fb.enter_context(nc.sbuf_tensor("wt_o", [128, DT * D], BF16, side="right"))
            ofm = fb.enter_context(nc.sbuf_tensor("ofm", [128, DT * 512], BF16, side="right"))
            at_sb = fb.enter_context(nc.sbuf_tensor("at_sb", [128, 2 * 512], BF16, side="right"))
            at_tm = fb.enter_context(nc.sbuf_tensor("at_tm", [128, 4 * D], BF16, side="right"))
            lb2 = fb.enter_context(nc.sbuf_tensor("lb2", [128, 2 * D], BF16, side="right"))
            xpg = fb.enter_context(nc.sbuf_tensor("xpg", [128, 4 * D], F32, side="right"))
            xpl = fb.enter_context(nc.sbuf_tensor("xpl", [128, 2 * D], F32, side="right"))
            ps_f = [fb.enter_context(nc.psum_tensor(f"ps_f{i}", [128, 512], F32))
                    for i in range(8)]
            ps_w = [ps_f[4], ps_f[5]]
            x2fm_t = lambda kd: x2fm[:, kd * TC:(kd + 1) * TC]
            wt_o_t = lambda kd: wt_o[:, kd * D:(kd + 1) * D]
            ofm_t = lambda kd: ofm[:, kd * 512:(kd + 1) * 512]
            xpg_t = lambda j: xpg[:, (j % 4) * D:(j % 4 + 1) * D]

            def wbuf_t(par, i):
                return wbuf[:, (par * 32 + i) * 256:(par * 32 + i + 1) * 256]

            def woT_load():
                def _w(sy):
                    wait(sy, "dgA", ev["dgA"])
                    wait(sy, "cT", ev["attn_T"])
                    wait(sy, "cV", ev["otm"])
                    for kd in range(DT):
                        inc(sy.dma_start(wt_o_t(kd), n_wo[:, kd * 128:(kd + 1) * 128],
                                         transpose=True), "dWO")
                    ev["woT"] = cnt["dWO"]
                bb.sync(_w)

            def a3_ofl(g):
                def _ofl(sy, g=g):
                    if g == 0:
                        wait(sy, "cV", ev["consts"])
                        wait(sy, "cT", ev["attn_T"])
                    wait(sy, "cT", ev.get(f"wo_mm_g{g - 1}", 0))
                    wait(sy, "dSO", ev["scr_o"])
                    for kd in range(DT):
                        inc(sy.dma_start(ofm_t(kd),
                                         scr_o[g * 512:(g + 1) * 512, kd * 128:(kd + 1) * 128],
                                         transpose=True), "dOF0")
                    ev[f"ofl{g}"] = cnt["dOF0"]
                bb.sync(_ofl)

            def a3_wo_m(g, m):
                def _mm(t, g=g, m=m):
                    if g == 0 and m == 0:
                        wait(t, "cV", ev["otm"])
                        wait(t, "cA", ev["attn_A"])
                    wait(t, "dOF0", ev[f"ofl{g}"])
                    wait(t, "dWO", ev["woT"])
                    wait(t, "cA", ev.get(f"at_ev{m % 2}", 0))
                    for kd in range(DT):
                        ins = t.matmul(ps_w[m % 2][:, 0:512],
                                       wt_o_t(kd)[:, m * 128:(m + 1) * 128],
                                       ofm_t(kd),
                                       start=(kd == 0), stop=(kd == DT - 1))
                    ev["at_mm"] = inc(ins, "cT")
                    if m == DT - 1:
                        ev[f"wo_mm_g{g}"] = ev["at_mm"]
                bb.tensor(_mm)

                def _ev(sc, g=g, m=m):
                    wait(sc, "cT", ev["at_mm"])
                    wait(sc, f"dAT{m % 2}", ev.get(f"at_w{m % 2}", 0))
                    wait(sc, "cV", ev["scales"])
                    ev["at_act"] = inc(
                        sc.activation(at_sb[:, (m % 2) * 512:(m % 2) * 512 + 512],
                                      ps_w[m % 2][:, 0:512], AF.Identity,
                                      bias=scales[:, BOC + m:BOC + m + 1], scale=1.0), "cA")
                    ev[f"at_ev{m % 2}"] = ev["at_act"]
                bb.scalar(_ev)

                def _st(sy, g=g, m=m):
                    wait(sy, "cA", ev["at_act"])
                    ev[f"at_w{m % 2}"] = inc(sy.dma_start(
                        scr_at[m * 128:(m + 1) * 128, g * 512:(g + 1) * 512],
                        at_sb[:, (m % 2) * 512:(m % 2) * 512 + 512]), f"dAT{m % 2}")
                    if m == DT - 1:
                        ev[f"at_grp{g}"] = (cnt["dAT0"], cnt["dAT1"])
                bb.gpsimd(_st)

            def a3_front(j):
                slot = j % 4
                g = j // 4

                def _l(sy, j=j, slot=slot, g=g):
                    wait(sy, "dAT0", ev[f"at_grp{g}"][0])
                    wait(sy, "dAT1", ev[f"at_grp{g}"][1])
                    wait(sy, "cV", ev.get(f"lb2{j - 4}", 0))
                    wait(sy, f"dXP{j % 4}", ev.get(f"xpw{j - 4}", 0))
                    inc(sy.dma_start(at_tm[:, slot * D:(slot + 1) * D],
                                     scr_at[:, j * 128:(j + 1) * 128], transpose=True),
                        f"dTM{slot}")
                    ev[f"resx{j}"] = inc(sy.dma_start(xpg_t(j), x_own[j * 128:(j + 1) * 128, :]),
                                         f"dTM{slot}")
                bb.sync(_l)
                ln_stats(at_tm[:, slot * D:(slot + 1) * D], slot,
                         [(f"dTM{slot}", ev[f"resx{j}"])])
                ev[f"a3rstd{j}"] = ev[f"ln_rstd{slot}"]

            def a3_finish(j):
                slot = j % 4
                c0 = 4 * slot

                def _f(v, j=j, slot=slot, c0=c0, vprev=cnt["cV"]):
                    wait(v, "cA", ev[f"a3rstd{j}"])
                    wait(v, f"dTM{slot}", ev[f"resx{j}"])
                    wait(v, "cV", vprev)
                    inc(v.tensor_scalar(ln_tmp[:, :], at_tm[:, slot * D:(slot + 1) * D],
                                        ln_st[:, c0:c0 + 1], ln_st[:, c0 + 2:c0 + 3],
                                        op0=ALU.add, op1=ALU.mult), "cV")
                    wait(v, "cV", cnt["cV"])
                    ins = v.tensor_tensor(xpg_t(j), xpg_t(j), ln_tmp[:, :], ALU.add)
                    ev[f"xp{j}"] = inc(ins, "cV")
                bb.vector(_f)

                def _sp(sy, j=j):
                    wait(sy, "cV", ev[f"xp{j}"])
                    ev[f"xpw{j}"] = inc(sy.dma_start(scr_xp[j * 128:(j + 1) * 128, :],
                                                     xpg_t(j)), f"dXP{j % 4}")
                bb.gpsimd(_sp)

            def ln2_front(j):
                slot = 4 + j % 4
                ln_stats(xpg_t(j), slot, [("cV", ev[f"xp{j}"])])
                ev[f"l2rstd{j}"] = ev[f"ln_rstd{slot}"]

            def ln2_finish(j):
                slot = 4 + j % 4
                b2 = j % 2
                c0 = 4 * slot

                def _f(v, j=j, slot=slot, c0=c0, b2=b2):
                    wait(v, "cA", ev[f"l2rstd{j}"])
                    wait(v, f"dX2S{b2}", ev.get(f"x2w{j - 2}", 0))
                    ins = v.tensor_scalar(lb2[:, b2 * D:(b2 + 1) * D], xpg_t(j),
                                          ln_st[:, c0:c0 + 1], ln_st[:, c0 + 2:c0 + 3],
                                          op0=ALU.add, op1=ALU.mult)
                    ev[f"lb2{j}"] = inc(ins, "cV")
                bb.vector(_f)

                def _w(sy, j=j, b2=b2):
                    wait(sy, "cV", ev[f"lb2{j}"])
                    ev[f"x2w{j}"] = inc(sy.dma_start(
                        scr_x2[j * 128:(j + 1) * 128, :], lb2[:, b2 * D:(b2 + 1) * D]), f"dX2S{b2}")
                bb.gpsimd(_w)

            a3_pend = []

            def pump(n):
                while n > 0 and a3_pend:
                    f_ = a3_pend.pop(0)
                    if f_ is not None:
                        f_()
                    n -= 1

            def a3_group_bundles(g):
                js = [4 * g + i_ for i_ in range(4)]
                out = []
                if g == 0:
                    out.append(woT_load)
                out.append(lambda g=g: a3_ofl(g))
                out.extend([None] * 6)
                for m in range(DT):
                    out.append(lambda g=g, m=m: a3_wo_m(g, m))
                out.extend([None] * 2)
                out.append(lambda j=js[0]: a3_front(j))
                out.append(lambda j=js[1]: a3_front(j))
                out.append(lambda j=js[0]: (a3_finish(j), ln2_front(j)))
                out.append(lambda j=js[2]: a3_front(j))
                out.append(lambda j=js[1]: (a3_finish(j), ln2_front(j)))
                out.append(lambda j=js[3]: a3_front(j))
                out.append(lambda j=js[2]: (a3_finish(j), ln2_front(j)))
                out.append(lambda j=js[0]: ln2_finish(j))
                out.append(lambda j=js[3]: (a3_finish(j), ln2_front(j)))
                out.append(lambda j=js[1]: ln2_finish(j))
                out.append(lambda j=js[2]: ln2_finish(j))
                out.append(lambda j=js[3]: ln2_finish(j))
                return out

            for f_ in a3_group_bundles(0):
                if f_ is not None:
                    f_()

            for C in range(NCH):
                pump(10 ** 6)
                if C + 1 < NCH:
                    a3_pend.extend(a3_group_bundles(C + 1))

                def _x2l(sy, C=C):
                    wait(sy, "dX2S0", ev[f"x2w{4 * C + 2}"])
                    wait(sy, "dX2S1", ev[f"x2w{4 * C + 3}"])
                    if C > 0:
                        wait(sy, "cT", ev.get(f"ab_mm_all{C - 1}", 0))
                    for kd in range(DT):
                        inc(sy.dma_start(x2fm_t(kd),
                                         scr_x2[C * 512:(C + 1) * 512, kd * 128:(kd + 1) * 128],
                                         transpose=True), "dX2")
                    ev["x2fm"] = cnt["dX2"]
                bb.sync(_x2l)

                prev_T_ab = cnt["cT"]
                for mb in range(DH // 256):
                    par = mb % 2

                    def _wl(sy, mb=mb, par=par, C=C, prev_T_ab=prev_T_ab):
                        if C == 0 and mb == 0:
                            wait(sy, "dPW0", ev["p2_ab"][0])
                            wait(sy, "dPW1", ev["p2_ab"][1])
                        wait(sy, "cT", max(ev.get(f"ab_mmb{mb - 2}", 0) if mb >= 2 else 0, prev_T_ab))
                        d1 = wbuf[:, (par * 32) * 256:(par * 32 + DT) * 256]
                        d2 = wbuf[:, (par * 32 + 8) * 256:(par * 32 + 8 + DT) * 256]
                        inc(sy.dma_start(
                            d1.rearrange("p (a c) -> p a c", c=256),
                            t_w1.rearrange("(a p) c -> p a c", p=128)[:, :, mb * 256:(mb + 1) * 256]),
                            f"dW{par}")
                        inc(sy.dma_start(
                            d2.rearrange("p (a c) -> p a c", c=256),
                            t_w2.rearrange("(a p) c -> p a c", p=128)[:, :, mb * 256:(mb + 1) * 256]),
                            f"dW{par}")
                        ev[f"wl_ab{mb}"] = cnt[f"dW{par}"]
                    bb.sync(_wl)

                    for mt in range(2):
                        m = 2 * mb + mt

                        def _mm(t, mb=mb, par=par, mt=mt, m=m, C=C):
                            wait(t, f"dW{par}", ev[f"wl_ab{mb}"])
                            wait(t, "dX2", ev["x2fm"])
                            if C > 0 and m < 2:
                                wait(t, "cA", ev["o3_done"])
                            wait(t, "cV", ev.get(f"h_ev{m - 2}", 0) if m >= 2 else 0)
                            pa, pb_ = ps_f[2 * mt], ps_f[2 * mt + 1]
                            for kd in range(DT):
                                t.matmul(pa[:, 0:TC],
                                         wbuf_t(par, kd)[:, mt * 128:(mt + 1) * 128],
                                         x2fm_t(kd),
                                         start=(kd == 0), stop=(kd == DT - 1))
                            for kd in range(DT):
                                ins = t.matmul(pb_[:, 0:TC],
                                               wbuf_t(par, 8 + kd)[:, mt * 128:(mt + 1) * 128],
                                               x2fm_t(kd),
                                               start=(kd == 0), stop=(kd == DT - 1))
                            ev[f"ab_mm{m}"] = inc(ins, "cT")
                            if mt == 1:
                                ev[f"ab_mmb{mb}"] = ev[f"ab_mm{m}"]
                        bb.tensor(_mm)

                        def _silu(sc, mt=mt, m=m):
                            wait(sc, "cT", ev[f"ab_mm{m}"])
                            wait(sc, "cV", ev.get(f"h_ev{m - 2}", 0) if m >= 2 else 0)
                            ev[f"silu{mt}"] = inc(sc.activation(
                                silu_sb[:, mt * 512:mt * 512 + TC], ps_f[2 * mt][:, 0:TC],
                                AF.Silu, bias=0.0,
                                scale=scales[:, S1C + m:S1C + m + 1]), "cA")
                        bb.scalar(_silu)

                        def _h(v, mt=mt, m=m):
                            wait(v, "cA", ev[f"silu{mt}"])
                            ins = v.scalar_tensor_tensor(
                                h_t(m), ps_f[2 * mt + 1][:, 0:TC],
                                scales[:, S2C + m:S2C + m + 1],
                                silu_sb[:, mt * 512:mt * 512 + TC],
                                op0=ALU.mult, op1=ALU.mult)
                            ev[f"h_ev{m}"] = inc(ins, "cV")
                        bb.vector(_h)
                    pump(1)
                ev[f"ab_mm_all{C}"] = cnt["cT"]
                ev["h_all"] = cnt["cV"]

                def _hid_pass(nsrc, src_t, dst_t, scol, dst_name, dep, p2gate, C=C):
                    prev_T = cnt["cT"]
                    for mb in range(DH // 256):
                        par = mb % 2
                        pb = ps_f[2 * par:2 * par + 2]

                        def _wl(sy, mb=mb, par=par):
                            if C == 0 and mb == 0:
                                wait(sy, "dPW0", ev[p2gate][0])
                                wait(sy, "dPW1", ev[p2gate][1])
                            wait(sy, "cT", max(ev.get(f"{dst_name}_mm{mb - 2}", 0) if mb >= 2 else 0, prev_T))
                            dall = wbuf[:, (par * 32) * 256:(par * 32 + 16) * 256]
                            dall2 = wbuf[:, (par * 32 + 16) * 256:(par * 32 + 32) * 256]
                            tv = nsrc.rearrange("(a p) c -> p a c", p=128)
                            inc(sy.dma_start(
                                dall.rearrange("p (a c) -> p a c", c=256),
                                tv[:, 0:16, mb * 256:(mb + 1) * 256]), f"dW{par}")
                            inc(sy.dma_start(
                                dall2.rearrange("p (a c) -> p a c", c=256),
                                tv[:, 16:32, mb * 256:(mb + 1) * 256]), f"dW{par}")
                            ev[f"{dst_name}_wl{mb}"] = cnt[f"dW{par}"]
                        bb.sync(_wl)

                        def _mm(t, mb=mb, par=par, pb=pb):
                            wait(t, f"dW{par}", ev[f"{dst_name}_wl{mb}"])
                            if mb == 0:
                                wait(t, dep[0], dep[1])
                            wait(t, "cA", ev.get(f"{dst_name}_ev{mb - 2}", 0) if mb >= 2 else 0)
                            for mt in range(2):
                                for k in range(DHT):
                                    ins = t.matmul(pb[mt][:, 0:TC],
                                                   wbuf_t(par, k)[:, mt * 128:(mt + 1) * 128],
                                                   src_t(k),
                                                   start=(k == 0), stop=(k == DHT - 1))
                            ev[f"{dst_name}_mm{mb}"] = inc(ins, "cT")
                        bb.tensor(_mm)

                        def _ge(sc, mb=mb, pb=pb):
                            wait(sc, "cT", ev[f"{dst_name}_mm{mb}"])
                            for mt in range(2):
                                m = mb * 2 + mt
                                ins = sc.activation(dst_t(m), pb[mt][:, 0:TC], AF.Gelu,
                                                    bias=0.0,
                                                    scale=scales[:, scol + m:scol + m + 1])
                            ev[f"{dst_name}_ev{mb}"] = inc(ins, "cA")
                        bb.scalar(_ge)
                        pump(1)
                    ev[f"{dst_name}_all"] = cnt["cA"]
                    ev[f"{dst_name}_mm_done"] = cnt["cT"]

                _hid_pass(t_h1, h_t, c_t, SH1C, "c", ("cV", ev["h_all"]), "p2_h1")
                _hid_pass(t_h2, c_t, h_t, SH2C, "d", ("cA", ev["c_all"]), "p2_h2")

                if debug and C == 0:
                    bb.flush()
                    _dump_fm(b, nc, blk, ev, c_t, DHT, TC, dbg["csb"],
                             dep=("cA", ev["c_all"]))
                    _dump_fm(b, nc, blk, ev, h_t, DHT, TC, dbg["hsb"],
                             dep=("cA", ev["d_all"]))

                prev_T_o3 = cnt["cT"]
                for mb in range(D // 256):
                    par = mb % 2
                    pb = ps_f[2 * par:2 * par + 2]

                    def _wl(sy, mb=mb, par=par, C=C, prev_T_o3=prev_T_o3):
                        if C == 0 and mb == 0:
                            wait(sy, "dPW0", ev["p2_w3"][0])
                            wait(sy, "dPW1", ev["p2_w3"][1])
                        wait(sy, "cT", max(ev.get(f"o3_mm{mb - 2}", 0) if mb >= 2 else 0, prev_T_o3))
                        dall = wbuf[:, (par * 32) * 256:(par * 32 + 16) * 256]
                        dall2 = wbuf[:, (par * 32 + 16) * 256:(par * 32 + 32) * 256]
                        tv = t_w3.rearrange("(a p) c -> p a c", p=128)
                        inc(sy.dma_start(
                            dall.rearrange("p (a c) -> p a c", c=256),
                            tv[:, 0:16, mb * 256:(mb + 1) * 256]), f"dW{par}")
                        inc(sy.dma_start(
                            dall2.rearrange("p (a c) -> p a c", c=256),
                            tv[:, 16:32, mb * 256:(mb + 1) * 256]), f"dW{par}")
                        ev[f"o3_wl{mb}"] = cnt[f"dW{par}"]
                    bb.sync(_wl)

                    def _mm(t, mb=mb, par=par, pb=pb):
                        wait(t, f"dW{par}", ev[f"o3_wl{mb}"])
                        if mb == 0:
                            wait(t, "cA", ev["d_all"])
                        wait(t, "cA", ev.get(f"o3_ev{mb - 2}", 0) if mb >= 2 else 0)
                        for mt in range(2):
                            for k in range(DHT):
                                ins = t.matmul(pb[mt][:, 0:TC],
                                               wbuf_t(par, k)[:, mt * 128:(mt + 1) * 128],
                                               h_t(k),
                                               start=(k == 0), stop=(k == DHT - 1))
                        ev[f"o3_mm{mb}"] = inc(ins, "cT")
                    bb.tensor(_mm)

                    for mt in range(2):
                        m = mb * 2 + mt

                        def _ev(sc, mb=mb, pb=pb, mt=mt, m=m):
                            wait(sc, "cT", ev[f"o3_mm{mb}"])
                            wait(sc, "dFS", ev.get("ff_w", 0))
                            ev["ff_act"] = inc(sc.activation(
                                ff_sb[:, :], pb[mt][:, 0:TC], AF.Copy, bias=0.0,
                                scale=scales[:, S3C + m:S3C + m + 1]), "cA")
                            if mt == 1:
                                ev[f"o3_ev{mb}"] = ev["ff_act"]
                        bb.scalar(_ev)

                        def _st(sy, m=m, C=C):
                            wait(sy, "cA", ev["ff_act"])
                            ev["ff_w"] = inc(sy.dma_start(
                                scr_ff[m * 128:(m + 1) * 128, C * 512:(C + 1) * 512],
                                ff_sb[:, :]), "dFS")
                        bb.gpsimd(_st)
                    pump(1)
                ev[f"scr_ff{C}"] = cnt["dFS"]
                ev["o3_done"] = cnt["cA"]

                def ln4_front(jj, C=C):
                    j = C * 4 + jj
                    slot = jj

                    def _l(sy, j=j, slot=slot, C=C):
                        wait(sy, "dFS", ev[f"scr_ff{C}"])
                        wait(sy, "cV", ev.get(f"ysb{j - 4}", 0))
                        ev[f"ff_tm{j}"] = inc(sy.dma_start(
                            ff_tm[:, slot * D:(slot + 1) * D],
                            scr_ff[:, j * 128:(j + 1) * 128], transpose=True), f"dFT{slot}")
                    bb.sync(_l)
                    ln_stats(ff_tm[:, slot * D:(slot + 1) * D], 8 + jj,
                             [(f"dFT{slot}", ev[f"ff_tm{j}"])])
                    ev[f"l4rstd{j}"] = ev[f"ln_rstd{8 + jj}"]

                def xpl_load(jj, C=C):
                    j = C * 4 + jj

                    def _l(sy, j=j, jj=jj):
                        wait(sy, "cV", ev.get(f"ysb{j - 2}", 0))
                        wait(sy, f"dXP{j % 4}", ev[f"xpw{j}"])
                        ev[f"xpl{j}"] = inc(sy.dma_start(
                            xpl[:, (jj % 2) * D:(jj % 2 + 1) * D],
                            scr_xp[j * 128:(j + 1) * 128, :]), f"dXL{jj % 2}")
                    bb.sync(_l)

                def ln4_finish(jj, C=C):
                    j = C * 4 + jj
                    slot = jj
                    c0 = 4 * (8 + jj)

                    def _f(v, j=j, slot=slot, c0=c0, jj=jj, vprev=cnt["cV"]):
                        wait(v, "cA", ev[f"l4rstd{j}"])
                        wait(v, f"dXL{jj % 2}", ev[f"xpl{j}"])
                        wait(v, f"dY{slot % 2}", ev.get(f"yout{j - 2}", 0))
                        wait(v, "cV", vprev)
                        inc(v.tensor_scalar(ln_tmp[:, :], ff_tm[:, slot * D:(slot + 1) * D],
                                            ln_st[:, c0:c0 + 1], ln_st[:, c0 + 2:c0 + 3],
                                            op0=ALU.add, op1=ALU.mult), "cV")
                        wait(v, "cV", cnt["cV"])
                        ins = v.tensor_add(y_sb[:, (slot % 2) * D:(slot % 2 + 1) * D],
                                           ln_tmp[:, :], xpl[:, (jj % 2) * D:(jj % 2 + 1) * D])
                        ev[f"ysb{j}"] = inc(ins, "cV")
                    bb.vector(_f)

                    def _o(sy, j=j, slot=slot):
                        wait(sy, "cV", ev[f"ysb{j}"])
                        ev[f"yout{j}"] = inc(sy.dma_start(
                            y[j * 128:(j + 1) * 128, :], y_sb[:, (slot % 2) * D:(slot % 2 + 1) * D]), f"dY{slot % 2}")
                    bb.gpsimd(_o)

                ln4_front(0)
                xpl_load(0)
                ln4_front(1)
                xpl_load(1)
                ln4_finish(0)
                ln4_front(2)
                xpl_load(2)
                ln4_finish(1)
                ln4_front(3)
                xpl_load(3)
                ln4_finish(2)
                ln4_finish(3)
                bb.flush()

            def _fin(sy):
                for s in cnt:
                    if s.startswith("d"):
                        wait(sy, s, cnt[s])
            bb.sync(_fin)
            bb.flush()
        p2es.close()

    return nc


_DUMP_ID = [0]


def _dump_fm(b, nc, blk, ev, slab_t, n_tiles, width, dram, dep=None):
    _DUMP_ID[0] += 1
    nm = f"dump{_DUMP_ID[0]}"
    with nc.sbuf_tensor(nm, [128, width], F32) as df:
        for i in range(n_tiles):
            def _c(v, i=i):
                if dep is not None:
                    b.wait(v, dep[0], dep[1])
                b.wait(v, "dDG", ev.get("dump_o", 0))
                ev["dump_c"] = b.inc(v.tensor_copy(df[:, 0:width], slab_t(i)), "cV")
            blk.vector(_c)

            def _o(sy, i=i):
                b.wait(sy, "cV", ev["dump_c"])
                ev["dump_o"] = b.inc(sy.dma_start(dram[i * 128:(i + 1) * 128, 0:width],
                                                  df[:, 0:width]), "dDG")
            blk.sync(_o)

        def _g(v):
            b.wait(v, "dDG", ev["dump_o"])
            v.engine_nop()
        blk.vector(_g)


_dump_tm = _dump_fm



def _build_mask(parity):
    j = np.arange(NJ)
    g = 2 * j + parity
    gidx = (g[:, None] * 128 + np.arange(128)[None, :]).reshape(-1)
    k = np.arange(TKP)
    vis = (3.0 * k[:, None] < gidx[None, :] + 0.001) & (k[:, None] < TK)
    m = np.where(vis, np.float32(0.0), np.float32(-1e9)).astype(np.float32)
    return m


_NC_CACHE = {}


def _get_nc(debug=False):
    key = bool(debug)
    if key not in _NC_CACHE:
        _NC_CACHE[key] = build_core_program(debug=key)
    return _NC_CACHE[key]


def kernel(L0_attn_x, L0_attn_Wq, L0_attn_Wk, L0_attn_Wv, L0_attn_Wo, L0_attn_bo,
           L0_attn_Wconv, L0_ffn_w1_w, L0_ffn_w1_s, L0_ffn_w2_w, L0_ffn_w2_s,
           L0_ffn_h1_w, L0_ffn_h1_s, L0_ffn_h2_w, L0_ffn_h2_s, L0_ffn_w3_w,
           L0_ffn_w3_s):
    x = np.ascontiguousarray(np.asarray(L0_attn_x, dtype=np.float32))
    masks = [_build_mask(0), _build_mask(1)]
    own_rows = []
    for par in range(2):
        g = 2 * np.arange(NJ) + par
        own_rows.append((g[:, None] * 128 + np.arange(128)[None, :]).reshape(-1))

    common = {
        "wq": np.ascontiguousarray(L0_attn_Wq, dtype=np.float32),
        "wk": np.ascontiguousarray(L0_attn_Wk, dtype=np.float32),
        "wv": np.ascontiguousarray(L0_attn_Wv, dtype=np.float32),
        "wo": np.ascontiguousarray(L0_attn_Wo, dtype=np.float32),
        "bo": np.ascontiguousarray(L0_attn_bo, dtype=np.float32),
        "wconv": np.ascontiguousarray(L0_attn_Wconv, dtype=np.float32),
        "w1": np.ascontiguousarray(L0_ffn_w1_w, dtype=np.float32),
        "w2": np.ascontiguousarray(L0_ffn_w2_w, dtype=np.float32),
        "h1": np.ascontiguousarray(L0_ffn_h1_w, dtype=np.float32),
        "h2": np.ascontiguousarray(L0_ffn_h2_w, dtype=np.float32),
        "w3": np.ascontiguousarray(L0_ffn_w3_w, dtype=np.float32),
        "s1": np.ascontiguousarray(L0_ffn_w1_s, dtype=np.float32),
        "s2": np.ascontiguousarray(L0_ffn_w2_s, dtype=np.float32),
        "sh1": np.ascontiguousarray(L0_ffn_h1_s, dtype=np.float32),
        "sh2": np.ascontiguousarray(L0_ffn_h2_s, dtype=np.float32),
        "s3": np.ascontiguousarray(L0_ffn_w3_s, dtype=np.float32),
    }
    in_maps = []
    for c in range(8):
        bb, par = c // 2, c % 2
        m = dict(common)
        m["x_full"] = np.ascontiguousarray(x[bb])
        m["x_own"] = np.ascontiguousarray(x[bb][own_rows[par]])
        m["maskT"] = masks[par]
        in_maps.append(m)

    nc = _get_nc(debug=False)
    res = run_bass_kernel_spmd(nc, in_maps, core_ids=list(range(8)))

    out = np.empty((B, T, D), dtype=np.float32)
    for c in range(8):
        bb, par = c // 2, c % 2
        out[bb][own_rows[par]] = res.results[c]["y"]
    return out
